# version 04
# baseline (speedup 1.0000x reference)
import contextlib
import numpy as np
import concourse.bass as bass
import concourse.mybir as mybir
from concourse.bass_utils import run_bass_kernel_spmd

F32 = mybir.dt.float32
BF16 = mybir.dt.bfloat16
AF = mybir.ActivationFunctionType
ALU = mybir.AluOpType
AX = mybir.AxisListType

D = 1024
NCHUNK = 8
H = 16
HD = 64
KK = 2
EXP = 8
II = 4096
IG = 512
IGN = II // IG
ILN = IG // 128


B_STAGE = None
B_SUB = None


class Cfg:
    def __init__(self, n_cores=8, t_core=2048):
        self.n_cores = n_cores
        self.t_core = t_core
        self.nt = t_core // 128
        self.sct = min(8, self.nt)
        self.nsc = self.nt // self.sct
        self.scale = float(n_cores * t_core) ** -0.5


class P:
    ENGS = ("sync", "tensor", "vector", "scalar", "gpsimd")

    def __init__(self, nc):
        self.nc = nc
        self.q = {e: [] for e in self.ENGS}
        self.semh = {}
        self.cnt = {}

    def sem(self, name):
        if name not in self.semh:
            self.semh[name] = self.nc.alloc_semaphore(name="sem_" + name)
            self.cnt[name] = 0
        return self.semh[name]

    def add(self, eng, fn, wait=(), inc=None, incby=1):
        seen = {}
        for t in wait:
            if t is None:
                continue
            nm, v = t
            if v > 0 and seen.get(nm, -1) < v:
                seen[nm] = v
        wl = [(self.sem(nm), v) for nm, v in seen.items()]
        tok = None
        semh = None
        if inc is not None:
            semh = self.sem(inc)
            self.cnt[inc] += incby
            tok = (inc, self.cnt[inc])

        def run(engine, wl=wl, fn=fn, semh=semh, incby=incby):
            for (sh, v) in wl:
                engine.wait_ge(sh, v)
            inst = fn(engine)
            if semh is not None and inst is not None:
                inst.then_inc(semh, incby)

        self.q[eng].append(run)
        return tok

    def waitonly(self, eng, wait):
        self.add(eng, lambda e: None, wait=wait)

    def now(self, name):
        return self.cnt.get(name, 0)

    def barrier_toks(self):
        return [(n, self.cnt[n]) for n in self.cnt]

    def finalize(self, block):
        for e in self.ENGS:
            fns = self.q[e]
            if fns:
                getattr(block, e)(lambda eng, fns=fns: [f(eng) for f in fns])


def _ap(x):
    return x[:] if not isinstance(x, bass.AP) else x


def pairs3(lhs_hi, lhs_lo, rhs_hi, rhs_lo, idx):
    out = []
    for j in idx:
        out += [(lhs_hi(j), rhs_hi(j)), (lhs_hi(j), rhs_lo(j)), (lhs_lo(j), rhs_hi(j))]
    return out


def mm_group(p, out_ap, pairs, wait_first, inc=True):
    n = len(pairs)
    tok = None
    for i, (la, ra) in enumerate(pairs):
        first, last = i == 0, i == n - 1
        tok = p.add("tensor",
                    lambda t, la=la, ra=ra, first=first, last=last:
                    t.matmul(out_ap, la, ra, start=first, stop=last),
                    wait=wait_first if first else (),
                    inc="MM" if (last and inc) else None)
    return tok



def emit_mod_phase(p, bufs, n_groups, want):
    csilu, mrow, wmst = bufs["csilu"], bufs["modrow"], bufs["wmstage"]
    pb = bufs["pb"]
    wmod = bufs["wmod_hbm"]
    t_sil = p.add("scalar", lambda s: s.activation(csilu[:], bufs["ccol"][:], AF.Silu),
                  wait=[bufs["tok_consts"]], inc="AC")
    t_zero = p.add("vector", lambda v: v.memset(mrow[0:1, 0:n_groups * 512], 0.0), inc="VE")
    acc_tok = t_zero
    stage_free = None
    for j in range(NCHUNK):
        t_dma = p.add("sync", lambda s, j=j: s.dma_start(
            out=wmst[:, 0:n_groups * 512],
            in_=wmod[j * 128:(j + 1) * 128, 0:n_groups * 512]),
            wait=[stage_free], inc="WMS", incby=16)
        for g in range(n_groups):
            bank = pb[6 + g % 2]
            t_mm = p.add("tensor", lambda t, j=j, g=g, bank=bank: t.matmul(
                bank[0:1, 0:512], csilu[:, j:j + 1],
                wmst[:, g * 512:(g + 1) * 512], start=True, stop=True),
                wait=[t_dma, t_sil, acc_tok, bufs["pbfree"][6 + g % 2]], inc="MM")
            bufs["pbfree"][6 + g % 2] = None
            acc_tok = p.add("vector", lambda v, g=g, bank=bank: v.tensor_add(
                mrow[0:1, g * 512:(g + 1) * 512], mrow[0:1, g * 512:(g + 1) * 512],
                bank[0:1, 0:512]), wait=[t_mm, acc_tok], inc="VE")
            bufs["pbfree"][6 + g % 2] = acc_tok
        stage_free = acc_tok
    last = acc_tok
    for slot, kind, rep in want:
        rep = _ap(rep)
        src = mrow[0:1, slot * 1024:(slot + 1) * 1024]
        dst = bufs["vrows"][0:1, slot * 1024:(slot + 1) * 1024]
        if kind == "plain":
            t_row = p.add("scalar", lambda s, src=src, dst=dst: s.copy(dst, src),
                          wait=[acc_tok], inc="AC")
        elif kind == "plus1":
            t_row = p.add("scalar", lambda s, src=src, dst=dst: s.add(dst, src, 1.0),
                          wait=[acc_tok], inc="AC")
        else:
            ab, sg = bufs["grow1"], bufs["grow2"]
            t1 = p.add("scalar", lambda s, src=src, ab=ab: s.activation(
                ab[0:1, :], src, AF.Abs), wait=[acc_tok, bufs["grow_free"]], inc="AC")
            t2 = p.add("vector", lambda v, ab=ab: v.tensor_scalar_max(
                ab[0:1, :], ab[0:1, :], 0.25), wait=[t1], inc="VE")
            t3 = p.add("vector", lambda v, src=src, sg=sg: v.tensor_single_scalar(
                sg[0:1, :], src, 0.0, ALU.is_ge),
                wait=[acc_tok, bufs["grow_free"]], inc="VE")
            t4 = p.add("vector", lambda v, sg=sg: v.tensor_scalar(
                sg[0:1, :], sg[0:1, :], 2.0, -1.0, ALU.mult, ALU.add),
                wait=[t3], inc="VE")
            t_row = p.add("vector", lambda v, ab=ab, sg=sg, dst=dst: v.tensor_tensor(
                dst, ab[0:1, :], sg[0:1, :], ALU.mult), wait=[t2, t4], inc="VE")
            bufs["grow_free"] = t_row
        for n in range(2):
            bi = 6 + n
            t_mm = p.add("tensor", lambda t, dst=dst, n=n, bi=bi: t.matmul(
                bufs["pb"][bi][:, 0:512], bufs["ones1"][:],
                dst[0:1, n * 512:(n + 1) * 512], start=True, stop=True),
                wait=[t_row, bufs["tok_ones"], bufs["pbfree"][bi]], inc="MM")
            last = p.add("vector", lambda v, rep=rep, n=n, bi=bi: v.tensor_copy(
                rep[:, n * 512:(n + 1) * 512], bufs["pb"][bi][:, 0:512]),
                wait=[t_mm], inc="VE")
            bufs["pbfree"][bi] = last
    return last


def emit_ln(p, bufs, par, in_ap, out_ap, mul_rep, add_rep, tok_in, extra=()):
    st, xc, sq = bufs["stats"][par], bufs["scr"][par], bufs["scr2"][par]
    mul_rep = None if mul_rep is None else _ap(mul_rep)
    add_rep = None if add_rep is None else _ap(add_rep)
    w = [tok_in] + list(extra)
    t0 = p.add("vector", lambda v: v.reduce_sum(st[:, 0:1], in_ap, axis=AX.X),
               wait=w + [bufs["stats_free"][par]], inc="VE")
    t1 = p.add("vector", lambda v: v.tensor_scalar_mul(st[:, 1:2], st[:, 0:1], 1.0 / D),
               wait=[t0], inc="VE")
    t2 = p.add("vector", lambda v: v.tensor_scalar(xc[:], in_ap, st[:, 1:2], None,
                                                   ALU.subtract),
               wait=[t1, bufs["scr_free"][par]], inc="VE")
    t3 = p.add("scalar", lambda s: s.activation(sq[:], xc[:], AF.Square),
               wait=[t2, bufs["scr2_free"][par]], inc="AC")
    t4 = p.add("vector", lambda v: v.reduce_sum(st[:, 2:3], sq[:], axis=AX.X),
               wait=[t3], inc="VE")
    t5 = p.add("vector", lambda v: v.tensor_scalar(st[:, 3:4], st[:, 2:3], 1.0 / D,
                                                   1e-5, ALU.mult, ALU.add),
               wait=[t4], inc="VE")
    t6 = p.add("scalar", lambda s: s.activation(st[:, 4:5], st[:, 3:4], AF.Sqrt),
               wait=[t5], inc="AC")
    t7 = p.add("vector", lambda v: v.reciprocal(st[:, 5:6], st[:, 4:5]),
               wait=[t6], inc="VE")
    if mul_rep is not None:
        t8 = p.add("vector", lambda v: v.scalar_tensor_tensor(
            out_ap, xc[:], st[:, 5:6], mul_rep, ALU.mult, ALU.mult),
            wait=[t7], inc="VE")
    else:
        t8 = p.add("vector", lambda v: v.tensor_scalar(out_ap, xc[:], st[:, 5:6],
                                                       None, ALU.mult),
                   wait=[t7], inc="VE")
    if add_rep is not None:
        t8 = p.add("vector", lambda v: v.tensor_add(out_ap, out_ap, add_rep),
                   wait=[t8], inc="VE")
    bufs["stats_free"][par] = t8
    bufs["scr_free"][par] = t8
    bufs["scr2_free"][par] = t4
    return t8


def emit_transpose_split(p, bufs, src_fn, hi, lo, tok_in, dst_free=None):
    pb = bufs["pb"]
    if dst_free is None:
        dst_free = []
    elif not isinstance(dst_free, list):
        dst_free = [dst_free]
    hi_tok = lo_tok = None
    for j in range(NCHUNK):
        bi = j % 2
        slot = pb[bi][:, 0:128]
        key = ("tr", bi)
        t_tr = p.add("tensor", lambda t, slot=slot, j=j: t.transpose(
            slot, src_fn(j), bufs["ident"][:]),
            wait=[tok_in, bufs["slot_free"].get(key), bufs["tok_ident"]], inc="MM")
        t_hi = p.add("scalar", lambda s, slot=slot, j=j: s.copy(
            hi[:, j * 128:(j + 1) * 128], slot),
            wait=[t_tr] + dst_free, inc="AC")
        t_lo = p.add("vector", lambda v, slot=slot, j=j: v.tensor_tensor(
            lo[:, j * 128:(j + 1) * 128], slot, hi[:, j * 128:(j + 1) * 128],
            ALU.subtract),
            wait=[t_hi] + dst_free, inc="VE")
        bufs["slot_free"][key] = t_lo
        hi_tok, lo_tok = t_hi, t_lo
    return hi_tok, lo_tok


def emit_split(p, src_ap, hi_ap, lo_ap, tok_in, extra=()):
    t_hi = p.add("vector", lambda v: v.tensor_copy(hi_ap, src_ap),
                 wait=[tok_in] + list(extra), inc="VE")
    t_lo = p.add("vector", lambda v: v.tensor_tensor(lo_ap, src_ap, hi_ap,
                                                     ALU.subtract),
                 wait=[t_hi], inc="VE")
    return t_lo


def emit_wload(p, bufs, hbm_ap, whi, wlo, fold_ident, prev_free, ncols=D,
               dst_free=None):
    ws = bufs["wstage"]
    tok = None
    for j in range(NCHUNK):
        t_dma = p.add("sync", lambda s, j=j: s.dma_start(
            out=ws[:, 0:ncols], in_=hbm_ap[j * 128:(j + 1) * 128, 0:ncols]),
            wait=[prev_free, bufs["wstage_free"]], inc="WST", incby=16)
        t_f = t_dma
        if fold_ident:
            t_f = p.add("vector", lambda v, j=j: v.tensor_add(
                ws[:, j * 128:(j + 1) * 128], ws[:, j * 128:(j + 1) * 128],
                bufs["ident"][:]), wait=[t_dma, bufs["tok_ident"]], inc="VE")
        t_hi = p.add("vector", lambda v, j=j: v.tensor_copy(
            whi[:, j * ncols:(j + 1) * ncols], ws[:, 0:ncols]),
            wait=[t_f, dst_free], inc="VE")
        tok = p.add("vector", lambda v, j=j: v.tensor_tensor(
            wlo[:, j * ncols:(j + 1) * ncols], ws[:, 0:ncols],
            whi[:, j * ncols:(j + 1) * ncols], ALU.subtract),
            wait=[t_hi, dst_free], inc="VE")
        bufs["wstage_free"] = tok
    return tok


def emit_kernel_norm(p, bufs, par, q_ap, out_ap, inv_scale, tok_in):
    t1, sq, hs = bufs["scr"][par], bufs["scr2"][par], bufs["hstats"][par]
    ta = p.add("scalar", lambda s: s.activation(t1[:], q_ap, AF.Silu),
               wait=[tok_in, bufs["scr_free"][par]], inc="AC")
    tb = p.add("vector", lambda v: v.tensor_scalar_add(t1[:], t1[:], 0.5),
               wait=[ta], inc="VE")
    tc = p.add("scalar", lambda s: s.activation(sq[:], t1[:], AF.Square),
               wait=[tb, bufs["scr2_free"][par]], inc="AC")
    td = p.add("vector", lambda v: v.reduce_sum(
        hs[:, 0:H], sq[:].rearrange("p (h d) -> p h d", h=H), axis=AX.X),
        wait=[tc, bufs["hstats_free"][par]], inc="VE")
    te = p.add("scalar", lambda s: s.activation(hs[:, H:2 * H], hs[:, 0:H], AF.Sqrt),
               wait=[td], inc="AC")
    tf = p.add("vector", lambda v: v.tensor_scalar_add(hs[:, H:2 * H],
                                                       hs[:, H:2 * H], 1e-6),
               wait=[te], inc="VE")
    tg = p.add("vector", lambda v: v.reciprocal(hs[:, 2 * H:3 * H], hs[:, H:2 * H]),
               wait=[tf], inc="VE")
    if inv_scale != 1.0:
        tg = p.add("vector", lambda v: v.tensor_scalar_mul(
            hs[:, 2 * H:3 * H], hs[:, 2 * H:3 * H], inv_scale), wait=[tg], inc="VE")
    bc = hs[:, 2 * H:3 * H].unsqueeze(2).broadcast_to((128, H, HD))
    th = p.add("vector", lambda v: v.tensor_tensor(
        out_ap.rearrange("p (h d) -> p h d", h=H),
        t1[:].rearrange("p (h d) -> p h d", h=H), bc, ALU.mult),
        wait=[tg], inc="VE")
    bufs["scr_free"][par] = th
    bufs["scr2_free"][par] = td
    bufs["hstats_free"][par] = th
    return th


def setup_common(nc, p, cfg, bufs, es):
    al = lambda n, sh, dt=F32: es.enter_context(nc.sbuf_tensor(n, sh, dt))
    bufs["pb"] = [es.enter_context(nc.psum_tensor(f"pb{i}", [128, 512], F32))
                  for i in range(8)]
    bufs["pbfree"] = [None] * 8
    bufs["slot_free"] = {}
    bufs["grow_free"] = None
    bufs["ident"] = al("ident_sb", [128, 128])
    bufs["ones1"] = al("ones1_sb", [1, 128])
    bufs["ccol"] = al("ccol_sb", [128, 8])
    bufs["csilu"] = al("csilu_sb", [128, 8])
    bufs["stats"] = [al(f"stats{i}", [128, 8]) for i in range(2)]
    bufs["hstats"] = [al(f"hstats{i}", [128, 3 * H]) for i in range(2)]
    bufs["scr"] = [al(f"scr{i}", [128, D]) for i in range(2)]
    bufs["scr2"] = [al(f"scr2_{i}", [128, D]) for i in range(2)]
    bufs["xbuf"] = [al(f"xbuf{i}", [128, D]) for i in range(2)]
    bufs["h1tm"] = [al(f"h1tm{i}", [128, D]) for i in range(2)]
    for nm in ("stats_free", "scr_free", "scr2_free", "hstats_free",
               "xbuf_free", "h1tm_free"):
        bufs[nm] = [None, None]
    ident_h = nc.dram_tensor("ident", [128, 128], F32, kind="ExternalInput").ap()
    ones1_h = nc.dram_tensor("ones1", [1, 128], F32, kind="ExternalInput").ap()
    ccol_h = nc.dram_tensor("c_col", [128, 8], F32, kind="ExternalInput").ap()
    bufs["x_hbm"] = nc.dram_tensor("xs", [cfg.t_core, D], F32,
                                   kind="ExternalInput").ap()
    bufs["wmod_hbm"] = nc.dram_tensor("wmod", [D, 6 * D], F32,
                                      kind="ExternalInput").ap()
    t_i = p.add("sync", lambda s: s.dma_start(out=bufs["ident"][:], in_=ident_h),
                inc="LDI", incby=16)
    t_o = p.add("sync", lambda s: s.dma_start(out=bufs["ones1"][:], in_=ones1_h),
                inc="LDO", incby=16)
    t_c = p.add("sync", lambda s: s.dma_start(
        out=bufs["ccol"][:], in_=ccol_h), inc="LDC", incby=16)
    bufs["tok_ident"] = t_i
    bufs["tok_ones"] = t_o
    bufs["tok_consts"] = t_c


def emit_x_load(p, bufs, t):
    par = t % 2
    xb = bufs["xbuf"][par]
    tok = p.add("sync", lambda s: s.dma_start(
        out=xb[:], in_=bufs["x_hbm"][t * 128:(t + 1) * 128, :]),
        wait=[bufs["xbuf_free"][par]], inc=f"XB{par}", incby=16)
    return xb, tok



def _emit_A(nc, p, bufs, cfg, es):
    if True:
        al = lambda n, sh, dt=F32: es.enter_context(nc.sbuf_tensor(n, sh, dt))
        wk_h = [nc.dram_tensor(f"wk{k}t", [D, D], F32, kind="ExternalInput").ap()
                for k in range(KK)]
        wv_h = [nc.dram_tensor(f"wv{k}t", [D, D], F32, kind="ExternalInput").ap()
                for k in range(KK)]
        repsA_h = nc.dram_tensor("repsA", [6, 128, D], F32,
                                 kind="ExternalInput").ap()

        reps = al("repsA_sb", [128, 6 * D])
        ca1_rep = reps[:, 4 * D:5 * D]
        sa_rep = reps[:, 5 * D:6 * D]
        h1hi = al("h1hi", [128, D], BF16)
        h1lo = al("h1lo", [128, D], BF16)
        kshi = al("kshi", [128, D], BF16)
        kslo = al("kslo", [128, D], BF16)
        vshi = al("vshi", [128, D], BF16)
        vslo = al("vslo", [128, D], BF16)
        kvtm = al("kvtm", [128, D])
        stev = al("stev", [64, 4 * 512])
        bufs["modrow"] = al("modrowA", [1, 2 * D])
        bufs["vrows"] = al("vrowsA", [1, 2 * D])
        bufs["grow1"] = al("grow1A", [1, D])
        bufs["grow2"] = al("grow2A", [1, D])
        bufs["wmstage"] = al("wmstageA", [128, 4 * 512])

        t_reps = p.add("sync", lambda s: s.dma_start(
            out=reps[:].rearrange("p (n d) -> p n d", n=6),
            in_=repsA_h.transpose([1, 0, 2])), inc="LDR", incby=16)
        t_bvs = t_reps
        for k in range(KK):
            t_bvs = p.add("vector", lambda v, k=k: v.tensor_scalar_mul(
                reps[:, (2 + k) * D:(3 + k) * D], reps[:, (2 + k) * D:(3 + k) * D],
                cfg.scale), wait=[t_bvs], inc="VE")

        t_mod = emit_mod_phase(p, bufs, 4,
                               [(1, "plus1", ca1_rep), (0, "plain", sa_rep)])

        wkhi = al("wkhi", [128, NCHUNK * D], BF16)
        wklo = al("wklo", [128, NCHUNK * D], BF16)
        wvhi = al("wvhi", [128, NCHUNK * D], BF16)
        wvlo = al("wvlo", [128, NCHUNK * D], BF16)
        bufs["wstage"] = al("wstageA", [128, D])
        bufs["wstage_free"] = None

        kv_free = None
        h1_free = None
        ks_free = None
        vs_free = None
        st_open = [False] * 8
        last_states = [None] * 4
        wk_free = None
        for k in range(KK):
          t_w = emit_wload(p, bufs, wk_h[k], wkhi, wklo, True, wk_free,
                           dst_free=wk_free)
          t_w = emit_wload(p, bufs, wv_h[k], wvhi, wvlo, True, t_w,
                           dst_free=wk_free)
          for t in range(cfg.nt):
            par = t % 2
            xb, t_x = emit_x_load(p, bufs, t)
            h1 = bufs["h1tm"][par]
            t_h1 = emit_ln(p, bufs, par, xb[:], h1[:], ca1_rep, sa_rep, t_x,
                           extra=[t_mod, bufs["h1tm_free"][par]])
            bufs["xbuf_free"][par] = t_h1
            t_hi, t_lo = emit_transpose_split(
                p, bufs, lambda j, h1=h1: h1[:, j * 128:(j + 1) * 128],
                h1hi, h1lo, t_h1, dst_free=h1_free)
            bufs["h1tm_free"][par] = t_lo
            lhs_hi = lambda j: h1hi[:, j * 128:(j + 1) * 128]
            lhs_lo = lambda j: h1lo[:, j * 128:(j + 1) * 128]
            if True:
                toks_k = []
                for ng in range(2):
                    prs = pairs3(
                        lhs_hi, lhs_lo,
                        lambda j, ng=ng: wkhi[
                            :, j * D + ng * 512: j * D + ng * 512 + 512],
                        lambda j, ng=ng: wklo[
                            :, j * D + ng * 512: j * D + ng * 512 + 512],
                        range(NCHUNK))
                    toks_k.append(mm_group(
                        p, bufs["pb"][2 + ng][:], prs,
                        [t_hi, t_lo, t_w, bufs["pbfree"][2 + ng]]))
                t_ev = None
                for ng in range(2):
                    t_ev = p.add("vector", lambda v, ng=ng, k=k: v.tensor_add(
                        kvtm[:, ng * 512:(ng + 1) * 512], bufs["pb"][2 + ng][:],
                        reps[:, k * D + ng * 512:k * D + ng * 512 + 512]),
                        wait=[toks_k[ng], kv_free, t_reps], inc="VE")
                    bufs["pbfree"][2 + ng] = t_ev
                t_kn = emit_kernel_norm(p, bufs, par, kvtm[:], kvtm[:],
                                        cfg.scale, t_ev)
                t_ks = emit_split(p, kvtm[:], kshi[:], kslo[:], t_kn,
                                  extra=[ks_free])
                toks_v = []
                for ng in range(2):
                    prs = pairs3(
                        lhs_hi, lhs_lo,
                        lambda j, ng=ng: wvhi[
                            :, j * D + ng * 512: j * D + ng * 512 + 512],
                        lambda j, ng=ng: wvlo[
                            :, j * D + ng * 512: j * D + ng * 512 + 512],
                        range(NCHUNK))
                    toks_v.append(mm_group(
                        p, bufs["pb"][2 + ng][:], prs,
                        [t_hi, t_lo, t_ks, bufs["pbfree"][2 + ng]]))
                t_ev2 = None
                for ng in range(2):
                    t_ev2 = p.add("vector", lambda v, ng=ng, k=k:
                                  v.scalar_tensor_tensor(
                        kvtm[:, ng * 512:(ng + 1) * 512], bufs["pb"][2 + ng][:],
                        cfg.scale,
                        reps[:, (2 + k) * D + ng * 512:(2 + k) * D + ng * 512 + 512],
                        ALU.mult, ALU.add),
                        wait=[toks_v[ng], t_ks, t_bvs], inc="VE")
                    bufs["pbfree"][2 + ng] = t_ev2
                t_vs = emit_split(p, kvtm[:], vshi[:], vslo[:], t_ev2,
                                  extra=[vs_free])
                kv_free = t_vs
                tok_mm = None
                n_i = 0
                for h in range(H):
                    bi = 4 + 2 * k + h // 8
                    c0 = (h % 8) * 64
                    bank = bufs["pb"][bi]
                    for (ka, va) in ((kshi, vshi), (kshi, vslo), (kslo, vshi)):
                        first = not st_open[bi]
                        st_open[bi] = True
                        last_tile = (h == H - 1) and (ka is kslo)
                        bank_last = (h % 8 == 7) and (ka is kslo)
                        last_glob = (t == cfg.nt - 1) and bank_last
                        w = ([t_ks, t_vs, bufs["pbfree"][bi]]
                             if n_i == 0 else [])
                        tok_mm = p.add("tensor", lambda tt, ka=ka, va=va, h=h,
                                       c0=c0, first=first, last=last_glob,
                                       bank=bank: tt.matmul(
                            bank[0:64, c0:c0 + 64],
                            ka[:, h * 64:(h + 1) * 64],
                            va[:, h * 64:(h + 1) * 64],
                            start=first, stop=last),
                            wait=w, inc="MM" if bank_last else None)
                        if bank_last:
                            last_states[bi - 4] = tok_mm
                        n_i += 1
                ks_free = tok_mm
                vs_free = tok_mm
            h1_free = kv_free
          wk_free = last_states[2 * k + 1]

        t_out = None
        t_e = None
        for kb in range(4):
            t_e = p.add("vector", lambda v, kb=kb: v.tensor_copy(
                stev[0:64, kb * 512:(kb + 1) * 512], bufs["pb"][4 + kb][0:64, :]),
                wait=[last_states[kb], t_e], inc="VE")
        return stev, t_e


def build_A(cfg):
    nc = bass.Bass("TRN2", target_bir_lowering=False)
    p = P(nc)
    bufs = {}
    es = contextlib.ExitStack()
    with es:
        setup_common(nc, p, cfg, bufs, es)
        spart_h = nc.dram_tensor("spart", [64, 2048], F32,
                                 kind="ExternalOutput").ap()
        stev, t_e = _emit_A(nc, p, bufs, cfg, es)
        t_out = p.add("gpsimd", lambda g: g.dma_start(
            out=spart_h, in_=stev[0:64, :]),
            wait=[t_e], inc="GSP", incby=16)
        p.waitonly("gpsimd", [t_out])
        with nc.Block() as block:
            p.finalize(block)
    return nc



def _emit_B(nc, p, bufs, cfg, es, sp_src=None):
    if True:
        al = lambda n, sh, dt=F32: es.enter_context(nc.sbuf_tensor(n, sh, dt))

        y_h = nc.dram_tensor("y", [cfg.t_core, D], F32, kind="ExternalOutput").ap()
        wq_h = nc.dram_tensor("wqt", [D, D], F32, kind="ExternalInput").ap()
        wo_h = nc.dram_tensor("wot", [D, D], F32, kind="ExternalInput").ap()
        repsB_h = nc.dram_tensor("repsB", [5, 128, D], F32,
                                 kind="ExternalInput").ap()
        if sp_src is None:
            spart_h = nc.dram_tensor("spart_all", [cfg.n_cores, 64, 2048], F32,
                                     kind="ExternalInput").ap()
        gatew_h = nc.dram_tensor("gatew", [D, EXP], F32, kind="ExternalInput").ap()
        gbias_h = nc.dram_tensor("gbias_rep", [128, EXP], F32,
                                 kind="ExternalInput").ap()
        upw_h = nc.dram_tensor("upw", [EXP, D, II], F32, kind="ExternalInput").ap()
        downw_h = nc.dram_tensor("downw", [EXP, II, D], F32,
                                 kind="ExternalInput").ap()
        swup_h = nc.dram_tensor("swup", [D, II], F32, kind="ExternalInput").ap()
        swdown_h = nc.dram_tensor("swdown", [II, D], F32, kind="ExternalInput").ap()
        upb_h = nc.dram_tensor("upb_fm", [EXP, 128, II // 128], F32,
                               kind="ExternalInput").ap()
        sbup_h = nc.dram_tensor("sbup_fm", [128, II // 128], F32,
                                kind="ExternalInput").ap()
        downb_h = nc.dram_tensor("downb", [EXP, D], F32, kind="ExternalInput").ap()
        x2_sc = nc.dram_tensor("x2_sc", [cfg.t_core, D], F32).ap()
        h3hi_sc = nc.dram_tensor("h3hi_sc", [cfg.nt, 128, D], BF16).ap()
        h3lo_sc = nc.dram_tensor("h3lo_sc", [cfg.nt, 128, D], BF16).ap()

        gb_rep = al("gb_rep", [128, D])
        sbd_rep = al("sbd_rep", [128, D])
        gbias = al("gbias_sb", [128, EXP])
        w_all = al("w_all", [128, cfg.nt * EXP])
        se_all = al("se_all", [128, cfg.nt * EXP])
        downb_sb = al("downb_sb", [8, D])
        mr1 = al("mr1", [128, 8])
        mr2 = al("mr2", [128, 8])
        gsc = al("gsc", [128, 48])
        x3tm = al("x3tm", [128, D])

        t_gb = p.add("sync", lambda s: s.dma_start(out=gbias[:], in_=gbias_h),
                     inc="LDG", incby=16)
        t_db = p.add("sync", lambda s: s.dma_start(out=downb_sb[:], in_=downb_h),
                     inc="LDD", incby=16)
        t_sbd0 = p.add("sync", lambda s: s.dma_start(
            out=sbd_rep[:], in_=repsB_h[4]), inc="LDS", incby=16)
        t_sbd = p.add("vector", lambda v: v.tensor_scalar_mul(sbd_rep[:], sbd_rep[:],
                                                              0.1),
                      wait=[t_sbd0], inc="VE")
        t_mr = p.add("vector", lambda v: v.memset(mr1[:], -1e30), inc="VE")
        t_mr = p.add("vector", lambda v: v.memset(mr2[:], -1e30), wait=[t_mr],
                     inc="VE")

        x2w_toks = [None] * cfg.nt
        h3_last = None
        y_tok = None

        s1 = contextlib.ExitStack()
        with s1:
            a1 = lambda n, sh, dt=F32: s1.enter_context(nc.sbuf_tensor(n, sh, dt))
            ca1_rep = a1("ca1_rep", [128, D])
            sa_rep = a1("sa_rep", [128, D])
            cb1_rep = a1("cb1_rep", [128, D])
            sbm_rep = a1("sbm_rep", [128, D])
            ga_rep = a1("ga_rep", [128, D])
            with contextlib.ExitStack() as s1a:
                bufs["modrow"] = s1a.enter_context(
                    nc.sbuf_tensor("modrowB", [1, 6 * D], F32))
                bufs["vrows"] = s1a.enter_context(
                    nc.sbuf_tensor("vrowsB", [1, 6 * D], F32))
                bufs["wmstage"] = s1a.enter_context(
                    nc.sbuf_tensor("wmstageB", [128, 12 * 512], F32))
                bufs["grow1"] = s1a.enter_context(
                    nc.sbuf_tensor("grow1B", [1, D], F32))
                bufs["grow2"] = s1a.enter_context(
                    nc.sbuf_tensor("grow2B", [1, D], F32))
                t_mod = emit_mod_phase(p, bufs, 12, [
                    (1, "plus1", ca1_rep), (0, "plain", sa_rep),
                    (4, "plus1", cb1_rep), (3, "plain", sbm_rep),
                    (2, "gate", ga_rep), (5, "gate", gb_rep)])
            reps = a1("repsB_sb", [128, 4 * D])
            bq_rep, bo_rep = reps[:, 0:D], reps[:, D:2 * D]
            lng_rep, lnb_rep = reps[:, 2 * D:3 * D], reps[:, 3 * D:4 * D]
            t_reps = p.add("sync", lambda s: s.dma_start(
                out=reps[:].rearrange("p (n d) -> p n d", n=4),
                in_=repsB_h[0:4].transpose([1, 0, 2])),
                wait=[t_mod], inc="LDR", incby=16)

            st_f32 = a1("st_f32", [128, 1024])
            st_hi = a1("st_hi", [128, 1024], BF16)
            st_lo = a1("st_lo", [128, 1024], BF16)
            with contextlib.ExitStack() as s1b:
                sp_acc = s1b.enter_context(nc.sbuf_tensor("sp_acc", [64, 2048], F32))
                if sp_src is None:
                    sp_buf = s1b.enter_context(
                        nc.sbuf_tensor("sp_buf", [64, 2048], F32))
                    t_acc = None
                    for c in range(cfg.n_cores):
                        t_l = p.add("sync", lambda s, c=c: s.dma_start(
                            out=sp_buf[:], in_=spart_h[c]),
                            wait=[t_acc, t_mod], inc="SPB", incby=16)
                        if c == 0:
                            t_acc = p.add("vector", lambda v: v.tensor_copy(
                                sp_acc[:], sp_buf[:]), wait=[t_l], inc="VE")
                        else:
                            t_acc = p.add("vector", lambda v: v.tensor_add(
                                sp_acc[:], sp_acc[:], sp_buf[:]),
                                wait=[t_l], inc="VE")
                else:
                    sp_ap, sp_tok = sp_src
                    t_acc = p.add("sync", lambda s: s.dma_start(
                        out=sp_acc[:], in_=sp_ap),
                        wait=[sp_tok, t_mod], inc="SPB", incby=16)
                t_st0 = p.add("vector", lambda v: v.tensor_tensor(
                    st_f32[0:64, :], sp_acc[:, 0:1024], sp_acc[:, 1024:2048],
                    ALU.mult), wait=[t_acc], inc="VE")
            t_st1 = p.add("sync", lambda s: s.dma_start(
                out=st_f32[64:128, :], in_=st_f32[0:64, :]),
                wait=[t_st0], inc="STR", incby=16)
            t_st = emit_split(p, st_f32[:], st_hi[:], st_lo[:], t_st1)

            wqhi = a1("wqhi", [128, NCHUNK * D], BF16)
            wqlo = a1("wqlo", [128, NCHUNK * D], BF16)
            wohi = a1("wohi", [128, NCHUNK * D], BF16)
            wolo = a1("wolo", [128, NCHUNK * D], BF16)
            gwhi = a1("gwhi", [128, NCHUNK * EXP], BF16)
            gwlo = a1("gwlo", [128, NCHUNK * EXP], BF16)
            bufs["wstage"] = a1("wstageB", [128, D])
            bufs["wstage_free"] = None
            t_w = emit_wload(p, bufs, wq_h, wqhi, wqlo, True, t_mod,
                             dst_free=t_st)
            t_w = emit_wload(p, bufs, wo_h, wohi, wolo, False, t_w)
            t_w = emit_wload(p, bufs, gatew_h, gwhi, gwlo, False, t_w, ncols=EXP)

            h1hi = a1("h1hiB", [128, D], BF16)
            h1lo = a1("h1loB", [128, D], BF16)
            qtm = a1("qtm", [128, D])
            qfhi = a1("qfhi", [128, D], BF16)
            qflo = a1("qflo", [128, D], BF16)
            attm = a1("attm", [128, D])
            lfhi = a1("lfhi", [128, D], BF16)
            lflo = a1("lflo", [128, D], BF16)
            x2tm = a1("x2tm", [128, D])
            h3tm = a1("h3tm", [128, D])
            h3hi = a1("h3hi", [128, D], BF16)
            h3lo = a1("h3lo", [128, D], BF16)

            h1_free = None
            qtm_free = None
            qf_free = None
            attm_free = None
            lf_free = None
            x2_free = None
            h3tm_free = None
            h3p_free = None
            gate_free = None

            nt_eff = 0 if (B_STAGE is not None and B_STAGE < 2) else cfg.nt
            for t in range(nt_eff):
                par = t % 2
                xb, t_x = emit_x_load(p, bufs, t)
                h1 = bufs["h1tm"][par]
                t_h1 = emit_ln(p, bufs, par, xb[:], h1[:], ca1_rep, sa_rep, t_x,
                               extra=[t_mod, bufs["h1tm_free"][par]])
                t_hi, t_lo = emit_transpose_split(
                    p, bufs, lambda j, h1=h1: h1[:, j * 128:(j + 1) * 128],
                    h1hi, h1lo, t_h1, dst_free=h1_free)
                bufs["h1tm_free"][par] = t_lo
                if B_SUB is not None and B_SUB < 2:
                    h1_free = t_lo
                    continue
                toks_q = []
                for ng in range(2):
                    prs = pairs3(
                        lambda j: h1hi[:, j * 128:(j + 1) * 128],
                        lambda j: h1lo[:, j * 128:(j + 1) * 128],
                        lambda j, ng=ng: wqhi[:, j * D + ng * 512:
                                              j * D + ng * 512 + 512],
                        lambda j, ng=ng: wqlo[:, j * D + ng * 512:
                                              j * D + ng * 512 + 512],
                        range(NCHUNK))
                    toks_q.append(mm_group(
                        p, bufs["pb"][2 + ng][:], prs,
                        [t_hi, t_lo, t_w, bufs["pbfree"][2 + ng]]))
                t_ev = None
                for ng in range(2):
                    t_ev = p.add("vector", lambda v, ng=ng: v.tensor_add(
                        qtm[:, ng * 512:(ng + 1) * 512], bufs["pb"][2 + ng][:],
                        bq_rep[:, ng * 512:ng * 512 + 512]),
                        wait=[toks_q[ng], qtm_free, t_reps], inc="VE")
                    bufs["pbfree"][2 + ng] = t_ev
                if B_SUB is not None and B_SUB < 3:
                    h1_free = toks_q[1]
                    qtm_free = t_ev
                    continue
                t_kn = emit_kernel_norm(p, bufs, par, qtm[:], qtm[:], 1.0, t_ev)
                t_qhi, t_qlo = emit_transpose_split(
                    p, bufs, lambda j: qtm[:, j * 128:(j + 1) * 128],
                    qfhi, qflo, t_kn, dst_free=qf_free)
                qtm_free = t_qlo
                if B_SUB is not None and B_SUB < 4:
                    h1_free = toks_q[1]
                    qf_free = t_qlo
                    continue
                tok_at = [None, None]
                for hb in range(2):
                    bank = bufs["pb"][4 + hb]
                    r0 = hb * 64
                    n_i = 0
                    for sslot in range(8):
                        h = 2 * sslot + hb
                        oc = sslot * 64
                        for (qa, sa_) in ((qfhi, st_hi), (qfhi, st_lo),
                                          (qflo, st_hi)):
                            first = n_i == 0
                            last = n_i == 23
                            w = ([t_qhi, t_qlo, t_st, bufs["pbfree"][4 + hb]]
                                 if first else [])
                            tok = p.add("tensor", lambda tt, qa=qa, sa_=sa_, h=h,
                                        r0=r0, oc=oc, first=first,
                                        last=last, bank=bank: tt.matmul(
                                bank[:, oc:oc + 64],
                                qa[r0:r0 + 64, (h // 2) * 128:(h // 2) * 128 + 128],
                                sa_[r0:r0 + 64, h * 64:(h + 1) * 64],
                                start=first, stop=last),
                                wait=w, inc="MM" if last else None)
                            n_i += 1
                    tok_at[hb] = tok
                qf_free = tok_at[1]
                t_evat = None
                for hb in range(2):
                    dst = attm[:].rearrange("p (s x) -> p s x", x=128)[
                        :, :, hb * 64:(hb + 1) * 64]
                    src = bufs["pb"][4 + hb][:].rearrange(
                        "p (s x) -> p s x", x=64)
                    t_evat = p.add("vector", lambda v, dst=dst, src=src:
                                   v.tensor_copy(dst, src),
                        wait=[tok_at[hb], attm_free], inc="VE")
                    bufs["pbfree"][4 + hb] = t_evat
                if B_SUB is not None and B_SUB < 5:
                    h1_free = toks_q[1]
                    continue
                t_ln = emit_ln(p, bufs, par, attm[:], attm[:], lng_rep, lnb_rep,
                               t_evat)
                t_lhi, t_llo = emit_transpose_split(
                    p, bufs, lambda j: attm[:, j * 128:(j + 1) * 128],
                    lfhi, lflo, t_ln, dst_free=lf_free)
                attm_free = t_llo
                toks_o = []
                for ng in range(2):
                    prs = pairs3(
                        lambda j: lfhi[:, j * 128:(j + 1) * 128],
                        lambda j: lflo[:, j * 128:(j + 1) * 128],
                        lambda j, ng=ng: wohi[:, j * D + ng * 512:
                                              j * D + ng * 512 + 512],
                        lambda j, ng=ng: wolo[:, j * D + ng * 512:
                                              j * D + ng * 512 + 512],
                        range(NCHUNK))
                    toks_o.append(mm_group(
                        p, bufs["pb"][2 + ng][:], prs,
                        [t_lhi, t_llo, bufs["pbfree"][2 + ng]]))
                lf_free = toks_o[1]
                if B_SUB is not None and B_SUB < 6:
                    h1_free = toks_q[1]
                    continue
                t_x2 = None
                for ng in range(2):
                    sl = slice(ng * 512, (ng + 1) * 512)
                    t_a = p.add("vector", lambda v, ng=ng, sl=sl: v.tensor_add(
                        x2tm[:, sl], bufs["pb"][2 + ng][:], bo_rep[:, sl]),
                        wait=[toks_o[ng], x2_free, t_reps], inc="VE")
                    bufs["pbfree"][2 + ng] = t_a
                    t_b = p.add("vector", lambda v, sl=sl: v.tensor_tensor(
                        x2tm[:, sl], x2tm[:, sl], ga_rep[:, sl], ALU.mult),
                        wait=[t_a], inc="VE")
                    t_x2 = p.add("vector", lambda v, sl=sl, xb=xb: v.tensor_add(
                        x2tm[:, sl], x2tm[:, sl], xb[:, sl]),
                        wait=[t_b, t_x], inc="VE")
                bufs["xbuf_free"][par] = t_x2
                t_x2w = p.add("gpsimd", lambda g, t=t: g.dma_start(
                    out=x2_sc[t * 128:(t + 1) * 128, :], in_=x2tm[:]),
                    wait=[t_x2], inc="GX2", incby=16)
                x2w_toks[t] = t_x2w
                x2_free = t_x2w
                if B_SUB is not None and B_SUB < 7:
                    h1_free = toks_q[1]
                    continue
                t_h3 = emit_ln(p, bufs, par, x2tm[:], h3tm[:], cb1_rep, sbm_rep,
                               t_x2, extra=[h3tm_free])
                t3hi, t3lo = emit_transpose_split(
                    p, bufs, lambda j: h3tm[:, j * 128:(j + 1) * 128],
                    h3hi, h3lo, t_h3, dst_free=h3p_free)
                h3tm_free = t3lo
                t_w1 = p.add("gpsimd", lambda g, t=t: g.dma_start(
                    out=h3hi_sc[t], in_=h3hi[:]), wait=[t3hi, t3lo],
                    inc="GH3", incby=16)
                t_w2 = p.add("gpsimd", lambda g, t=t: g.dma_start(
                    out=h3lo_sc[t], in_=h3lo[:]), wait=[t3lo, t_w1],
                    inc="GH3", incby=16)
                h3_last = t_w2
                if B_STAGE is not None and B_STAGE < 3:
                    h1_free = toks_q[1]
                    h3p_free = [t_w1, t_w2]
                    continue
                prs = pairs3(
                    lambda j: h3hi[:, j * 128:(j + 1) * 128],
                    lambda j: h3lo[:, j * 128:(j + 1) * 128],
                    lambda j: gwhi[:, j * EXP:(j + 1) * EXP],
                    lambda j: gwlo[:, j * EXP:(j + 1) * EXP],
                    range(NCHUNK))
                tok_g = mm_group(p, bufs["pb"][6][:, 0:EXP], prs,
                                 [t3hi, t3lo, t_w, bufs["pbfree"][6]])
                sS, s2, s3 = gsc[:, 0:8], gsc[:, 8:16], gsc[:, 16:24]
                msk, t8, den = gsc[:, 24:32], gsc[:, 32:40], gsc[:, 40:41]
                t_g1 = p.add("vector", lambda v: v.tensor_add(
                    sS, bufs["pb"][6][:, 0:EXP], gbias[:]),
                    wait=[tok_g, t_gb, gate_free], inc="VE")
                bufs["pbfree"][6] = t_g1
                t_g2 = p.add("scalar", lambda s: s.activation(sS, sS, AF.Sigmoid),
                             wait=[t_g1], inc="AC")
                t_g3 = p.add("vector", lambda v: v.max(t8, sS), wait=[t_g2],
                             inc="VE")
                t_g4 = p.add("vector", lambda v: v.tensor_copy(mr1[:, 0:1],
                                                               t8[:, 0:1]),
                             wait=[t_g3, t_mr], inc="VE")
                t_g5 = p.add("vector", lambda v: v.tensor_copy(mr2[:, 0:1],
                                                               t8[:, 1:2]),
                             wait=[t_g3], inc="VE")
                t_g6 = p.add("vector", lambda v: v.match_replace(
                    s2, mr1[:], sS, -1e30), wait=[t_g4, t_g2], inc="VE")
                t_g7 = p.add("vector", lambda v: v.match_replace(
                    s3, mr2[:], s2, -1e30), wait=[t_g5, t_g6], inc="VE")
                t_g8 = p.add("vector", lambda v: v.tensor_tensor(
                    msk, sS, s3, ALU.not_equal), wait=[t_g7], inc="VE")
                t_g9 = p.add("vector", lambda v: v.tensor_add(
                    den, t8[:, 0:1], t8[:, 1:2]), wait=[t_g3], inc="VE")
                t_g10 = p.add("vector", lambda v: v.tensor_scalar_add(
                    den, den, 1e-6), wait=[t_g9], inc="VE")
                t_g11 = p.add("vector", lambda v: v.reciprocal(den, den),
                              wait=[t_g10], inc="VE")
                wsl = w_all[:, t * EXP:(t + 1) * EXP]
                t_g12 = p.add("vector", lambda v, wsl=wsl: v.tensor_tensor(
                    wsl, sS, msk, ALU.mult), wait=[t_g8], inc="VE")
                t_g13 = p.add("vector", lambda v, wsl=wsl: v.tensor_scalar(
                    wsl, wsl, den, None, ALU.mult), wait=[t_g12, t_g11], inc="VE")
                p.add("vector", lambda v, t=t, wsl=wsl: v.tensor_scalar_mul(
                    se_all[:, t * EXP:(t + 1) * EXP], wsl, 0.1),
                    wait=[t_g13], inc="VE")
                gate_free = t_g13
                h1_free = toks_q[1]
                h3p_free = [tok_g, t_w1, t_w2]

        bar = p.barrier_toks()
        for e in P.ENGS:
            p.waitonly(e, bar)
        bufs["pbfree"] = [None] * 8
        bufs["slot_free"] = {}

        s2s = contextlib.ExitStack()
        with s2s:
            a2 = lambda n, sh, dt=F32: s2s.enter_context(nc.sbuf_tensor(n, sh, dt))
            h3shi = a2("h3shi", [128, cfg.sct * D], BF16)
            h3slo = a2("h3slo", [128, cfg.sct * D], BF16)
            acc = a2("ffn_acc", [128, cfg.sct * D])
            hacthi = a2("hacthi", [128, ILN * cfg.sct * 128], BF16)
            hactlo = a2("hactlo", [128, ILN * cfg.sct * 128], BF16)
            uphi = a2("uphi", [128, NCHUNK * IG], BF16)
            uplo = a2("uplo", [128, NCHUNK * IG], BF16)
            dnhi = a2("dnhi", [128, ILN * D], BF16)
            dnlo = a2("dnlo", [128, ILN * D], BF16)
            fstage = a2("fstage", [128, NCHUNK * IG])
            tfull = a2("tfull", [128, 512])
            sbup_sb = a2("sbup_sb", [128, II // 128])
            upb_sb = a2("upb_sb", [128, II // 128])
            wt_t = a2("wt_t", [8, 128])

            t_sbup = p.add("sync", lambda s: s.dma_start(
                out=sbup_sb[:], in_=sbup_h), inc="LDU", incby=16)

            fstage_free = None
            wt_free = [None]
            up_free = None
            dn_free = None
            hact_free = None
            tfull_free = None
            acc_free = None
            h3s_free = None
            upb_free = None
            x3_free = None
            n_tg = max(1, cfg.sct // 4)
            tpg = min(4, cfg.sct)

            nsc_eff = 0 if (B_STAGE is not None and B_STAGE < 4) else cfg.nsc
            for sc in range(nsc_eff):
                t0 = sc * cfg.sct
                t_l1 = p.add("sync", lambda s, t0=t0: s.dma_start(
                    out=h3shi[:].rearrange("p (a c) -> p a c", a=cfg.sct),
                    in_=h3hi_sc[t0:t0 + cfg.sct].transpose([1, 0, 2])),
                    wait=[h3_last, h3s_free], inc="H3S", incby=16)
                t_h3s = p.add("sync", lambda s, t0=t0: s.dma_start(
                    out=h3slo[:].rearrange("p (a c) -> p a c", a=cfg.sct),
                    in_=h3lo_sc[t0:t0 + cfg.sct].transpose([1, 0, 2])),
                    wait=[h3_last, h3s_free, t_l1], inc="H3S", incby=16)
                t_acc0 = p.add("vector", lambda v: v.memset(acc[:], 0.0),
                               wait=[acc_free], inc="VE")
                acc_tok = [[t_acc0, t_acc0] for _ in range(cfg.sct)]

                def h3s_mov(buf, j, tg):
                    a = buf[:].rearrange("p (a j c) -> p a j c", a=cfg.sct,
                                         j=NCHUNK)
                    return a[:, tg * tpg:(tg + 1) * tpg, j, :]

                for br in range(1 + EXP):
                    is_sh = br == 0
                    e = br - 1
                    if is_sh:
                        t_upb = t_sbup
                        bias_sb = sbup_sb
                    else:
                        t_upb = p.add("sync", lambda s, e=e: s.dma_start(
                            out=upb_sb[:], in_=upb_h[e]),
                            wait=[upb_free], inc="UPB", incby=16)
                        bias_sb = upb_sb
                    for g in range(IGN):
                        src_u = (swup_h[:, g * IG:(g + 1) * IG] if is_sh
                                 else upw_h[e][:, g * IG:(g + 1) * IG])
                        t_d = p.add("sync", lambda s, src_u=src_u: s.dma_start(
                            out=fstage[:].rearrange("p (j c) -> p j c", j=NCHUNK),
                            in_=src_u.rearrange("(j p) c -> j p c",
                                                j=NCHUNK).transpose([1, 0, 2])),
                            wait=[fstage_free], inc="FST", incby=16)
                        t_u = emit_split(p, fstage[:], uphi[:], uplo[:], t_d,
                                         extra=[up_free])
                        fstage_free = t_u
                        hact_last = None
                        up_mm_last = None
                        for tg in range(n_tg):
                            for il in range(ILN):
                                bi = (tg * ILN + il) % 2
                                bank = bufs["pb"][bi]
                                prs = pairs3(
                                    lambda j, il=il: uphi[
                                        :, j * IG + il * 128:
                                        j * IG + il * 128 + 128],
                                    lambda j, il=il: uplo[
                                        :, j * IG + il * 128:
                                        j * IG + il * 128 + 128],
                                    lambda j, tg=tg: h3s_mov(h3shi, j, tg),
                                    lambda j, tg=tg: h3s_mov(h3slo, j, tg),
                                    range(NCHUNK))
                                tok_up = mm_group(
                                    p, bank[:, 0:tpg * 128], prs,
                                    [t_u, t_h3s, bufs["pbfree"][bi]])
                                up_mm_last = tok_up
                                t_gl = p.add("scalar", lambda s, bank=bank, il=il,
                                             g=g, bias_sb=bias_sb, tpg=tpg:
                                             s.activation(
                                    tfull[:, 0:tpg * 128], bank[:, 0:tpg * 128],
                                    AF.Gelu,
                                    bias=bias_sb[:, g * ILN + il:
                                                 g * ILN + il + 1]),
                                    wait=[tok_up, tfull_free, t_upb], inc="AC")
                                bufs["pbfree"][bi] = t_gl
                                last_gelu = t_gl
                                hh = hacthi[:].rearrange(
                                    "p (a il c) -> p a il c", a=cfg.sct, il=ILN)
                                hl = hactlo[:].rearrange(
                                    "p (a il c) -> p a il c", a=cfg.sct, il=ILN)
                                hdst_hi = hh[:, tg * tpg:(tg + 1) * tpg, il, :]
                                hdst_lo = hl[:, tg * tpg:(tg + 1) * tpg, il, :]
                                t_hh = p.add("vector", lambda v, hdst_hi=hdst_hi,
                                             tpg=tpg: v.tensor_copy(
                                    hdst_hi, tfull[:, 0:tpg * 128].rearrange(
                                        "p (a c) -> p a c", a=tpg)),
                                    wait=[t_gl, hact_free], inc="VE")
                                t_hl = p.add("vector", lambda v, hdst_hi=hdst_hi,
                                             hdst_lo=hdst_lo, tpg=tpg:
                                             v.tensor_tensor(
                                    hdst_lo, tfull[:, 0:tpg * 128].rearrange(
                                        "p (a c) -> p a c", a=tpg),
                                    hdst_hi, ALU.subtract),
                                    wait=[t_hh], inc="VE")
                                tfull_free = t_hl
                                hact_last = t_hl
                        up_free = up_mm_last
                        src_d = (swdown_h[g * IG:(g + 1) * IG, :] if is_sh
                                 else downw_h[e][g * IG:(g + 1) * IG, :])
                        t_dd = p.add("sync", lambda s, src_d=src_d: s.dma_start(
                            out=fstage[:, 0:ILN * D].rearrange(
                                "p (j c) -> p j c", j=ILN),
                            in_=src_d.rearrange("(j p) c -> j p c",
                                                j=ILN).transpose([1, 0, 2])),
                            wait=[fstage_free], inc="FST", incby=16)
                        t_dn = emit_split(p, fstage[:, 0:ILN * D], dnhi[:],
                                          dnlo[:], t_dd, extra=[dn_free])
                        fstage_free = t_dn
                        dn_mm_last = None
                        for tl in range(cfg.sct):
                            for ng in range(2):
                                bi = 2 + (tl % 2) * 2 + ng
                                bank = bufs["pb"][bi]
                                prs = pairs3(
                                    lambda j, tl=tl: hacthi[
                                        :, (tl * ILN + j) * 128:
                                        (tl * ILN + j) * 128 + 128],
                                    lambda j, tl=tl: hactlo[
                                        :, (tl * ILN + j) * 128:
                                        (tl * ILN + j) * 128 + 128],
                                    lambda j, ng=ng: dnhi[
                                        :, j * D + ng * 512:
                                        j * D + ng * 512 + 512],
                                    lambda j, ng=ng: dnlo[
                                        :, j * D + ng * 512:
                                        j * D + ng * 512 + 512],
                                    range(ILN))
                                tok_dn = mm_group(
                                    p, bank[:], prs,
                                    [t_dn, hact_last, bufs["pbfree"][bi]])
                                dn_mm_last = tok_dn
                                asl = acc[:, tl * D + ng * 512:
                                          tl * D + ng * 512 + 512]
                                if is_sh:
                                    t_ae = p.add("vector", lambda v, asl=asl,
                                                 bank=bank:
                                                 v.scalar_tensor_tensor(
                                        asl, bank[:], 0.1, asl, ALU.mult,
                                        ALU.add),
                                        wait=[tok_dn, acc_tok[tl][ng]], inc="VE")
                                else:
                                    sesl = se_all[:, (t0 + tl) * EXP + e:
                                                  (t0 + tl) * EXP + e + 1]
                                    t_ae = p.add("vector", lambda v, asl=asl,
                                                 bank=bank, sesl=sesl:
                                                 v.scalar_tensor_tensor(
                                        asl, bank[:], sesl, asl, ALU.mult,
                                        ALU.add),
                                        wait=[tok_dn, acc_tok[tl][ng]], inc="VE")
                                acc_tok[tl][ng] = t_ae
                                bufs["pbfree"][bi] = t_ae
                        dn_free = dn_mm_last
                        hact_free = dn_mm_last
                    upb_free = last_gelu
                h3s_free = up_free

                for tl in range(cfg.sct):
                    t = t0 + tl
                    par = tl % 2
                    st = bufs["stats"][par]
                    t_tr = p.add("tensor", lambda tt, t=t: tt.transpose(
                        bufs["pb"][7][0:8, 0:128],
                        w_all[:, t * EXP:(t + 1) * EXP], bufs["ident"][:]),
                        wait=[gate_free, bufs["pbfree"][7]], inc="MM")
                    t_wt = p.add("vector", lambda v: v.tensor_copy(
                        wt_t[:], bufs["pb"][7][0:8, 0:128]),
                        wait=[t_tr, wt_free[0]], inc="VE")
                    bufs["pbfree"][7] = t_wt
                    t_wdb = None
                    for ng in range(2):
                        tok = p.add("tensor", lambda tt, ng=ng: tt.matmul(
                            bufs["pb"][6][:, 0:512],
                            wt_t[:],
                            downb_sb[:, ng * 512:(ng + 1) * 512],
                            start=True, stop=True),
                            wait=[t_wt, t_db, bufs["pbfree"][6]],
                            inc="MM")
                        asl = acc[:, tl * D + ng * 512:tl * D + ng * 512 + 512]
                        t_wdb = p.add("vector", lambda v, asl=asl:
                                      v.scalar_tensor_tensor(
                            asl, bufs["pb"][6][:, 0:512],
                            0.1, asl, ALU.mult, ALU.add),
                            wait=[tok, acc_tok[tl][ng]], inc="VE")
                        acc_tok[tl][ng] = t_wdb
                        bufs["pbfree"][6] = t_wdb
                    wt_free[0] = t_wdb
                    t_sb2 = None
                    for ng in range(2):
                        asl = acc[:, tl * D + ng * 512:tl * D + ng * 512 + 512]
                        t_sb2 = p.add("vector", lambda v, asl=asl, ng=ng:
                                      v.tensor_add(
                            asl, asl, sbd_rep[:, ng * 512:(ng + 1) * 512]),
                            wait=[t_sbd, acc_tok[tl][ng]], inc="VE")
                        acc_tok[tl][ng] = t_sb2
                    accsl = acc[:, tl * D:(tl + 1) * D]
                    t_mx = p.add("vector", lambda v, accsl=accsl, st=st:
                                 v.reduce_max(st[:, 0:1], accsl, axis=AX.X,
                                              apply_absolute_value=True),
                        wait=[acc_tok[tl][0], acc_tok[tl][1],
                              bufs["stats_free"][par]], inc="VE")
                    t_mx = p.add("vector", lambda v, st=st: v.tensor_scalar_add(
                        st[:, 0:1], st[:, 0:1], 1e-6), wait=[t_mx], inc="VE")
                    t_mx = p.add("vector", lambda v, st=st: v.reciprocal(
                        st[:, 1:2], st[:, 0:1]), wait=[t_mx], inc="VE")
                    bufs["stats_free"][par] = t_mx
                    xb = bufs["xbuf"][par]
                    t_xr = p.add("sync", lambda s, t=t, xb=xb: s.dma_start(
                        out=xb[:], in_=x2_sc[t * 128:(t + 1) * 128, :]),
                        wait=[x2w_toks[t], bufs["xbuf_free"][par]],
                        inc=f"XB{par}", incby=16)
                    t_f1 = p.add("vector", lambda v, accsl=accsl, st=st:
                                 v.tensor_scalar(x3tm[:], accsl, st[:, 1:2],
                                                 None, ALU.mult),
                        wait=[t_mx, x3_free], inc="VE")
                    t_f2 = p.add("vector", lambda v: v.tensor_tensor(
                        x3tm[:], x3tm[:], gb_rep[:], ALU.mult),
                        wait=[t_f1], inc="VE")
                    t_f3 = p.add("vector", lambda v, xb=xb: v.tensor_add(
                        x3tm[:], x3tm[:], xb[:]), wait=[t_f2, t_xr], inc="VE")
                    bufs["xbuf_free"][par] = t_f3
                    y_tok = p.add("gpsimd", lambda g, t=t: g.dma_start(
                        out=y_h[t * 128:(t + 1) * 128, :], in_=x3tm[:]),
                        wait=[t_f3], inc="GY", incby=16)
                    x3_free = y_tok
                acc_free = y_tok

            if B_STAGE is not None and B_STAGE < 4:
                bar2 = p.barrier_toks()
                t_z = p.add("vector", lambda v: v.memset(x3tm[:], 0.0),
                            wait=bar2, inc="VE")
                p.add("gpsimd", lambda g: g.dma_start(
                    out=y_h[0:128, :], in_=x3tm[:]), wait=[t_z],
                    inc="GY", incby=16)
            p.waitonly("gpsimd", [("GY", p.now("GY")), ("GX2", p.now("GX2")),
                                  ("GH3", p.now("GH3"))])


def build_B(cfg):
    nc = bass.Bass("TRN2", target_bir_lowering=False)
    p = P(nc)
    bufs = {}
    es = contextlib.ExitStack()
    with es:
        setup_common(nc, p, cfg, bufs, es)
        _emit_B(nc, p, bufs, cfg, es, sp_src=None)
        with nc.Block() as block:
            p.finalize(block)
    return nc


def build_fused(cfg):
    nc = bass.Bass("TRN2", target_bir_lowering=False)
    p = P(nc)
    bufs = {}
    es = contextlib.ExitStack()
    with es:
        setup_common(nc, p, cfg, bufs, es)
        sp_loc = nc.dram_tensor("sp_loc", [64, 2048], F32).ap()
        sp_sum = nc.dram_tensor("sp_sum", [64, 2048], F32).ap()
        with contextlib.ExitStack() as esa:
            stev, t_e = _emit_A(nc, p, bufs, cfg, esa)
            t_w = p.add("gpsimd", lambda g: g.dma_start(
                out=sp_loc, in_=stev[0:64, :]),
                wait=[t_e], inc="GSP", incby=16)
            rg = [list(range(cfg.n_cores))]
            t_cc = p.add("gpsimd", lambda g: g.collective_compute(
                "AllReduce", ALU.add, replica_groups=rg,
                ins=[sp_loc], outs=[sp_sum]),
                wait=[t_w], inc="CC")
            bar = p.barrier_toks()
            for e in P.ENGS:
                p.waitonly(e, bar)
        bufs["pbfree"] = [None] * 8
        bufs["slot_free"] = {}
        bufs["wstage_free"] = None
        _emit_B(nc, p, bufs, cfg, es, sp_src=(sp_sum, t_cc))
        with nc.Block() as block:
            p.finalize(block)
    return nc



def _rep(v):
    v = np.ascontiguousarray(np.asarray(v, np.float32).reshape(-1))
    return np.ascontiguousarray(np.broadcast_to(v[None, :], (128, v.shape[0])))


def kernel(**inputs):
    return _run(inputs, Cfg(8, 2048), run_bass_kernel_spmd)


def _run(inputs, cfg, run_fn):
    f = lambda k: np.ascontiguousarray(np.asarray(inputs[k], np.float32))
    x = f("L0_attn_x").reshape(cfg.n_cores * cfg.t_core, D)
    wq_t = np.ascontiguousarray(f("L0_attn_Wq").T)
    wk = f("L0_attn_Wk")
    wv = f("L0_attn_Wv")
    wo_t = np.ascontiguousarray(f("L0_attn_Wo").T)
    bk = f("L0_attn_bk")
    bv = f("L0_attn_bv")
    z128 = np.zeros((128, D), np.float32)
    repsA = np.ascontiguousarray(np.stack(
        [_rep(bk[0]), _rep(bk[1]), _rep(bv[0]), _rep(bv[1]), z128, z128]))
    repsB = np.ascontiguousarray(np.stack(
        [_rep(f("L0_attn_bq")), _rep(f("L0_attn_bo")), _rep(f("L0_attn_ln_g")),
         _rep(f("L0_attn_ln_b")), _rep(f("L0_ffn_sb_down"))]))
    c_col = np.ascontiguousarray(f("L0_c").reshape(NCHUNK, 128).T)
    wmod = f("L0_wmod")
    ident = np.eye(128, dtype=np.float32)
    ones1 = np.ones((1, 128), np.float32)
    gbias_rep = _rep(f("L0_ffn_bias"))
    upb_fm = np.ascontiguousarray(
        f("L0_ffn_up_b").reshape(EXP, II // 128, 128).transpose(0, 2, 1))
    sbup_fm = np.ascontiguousarray(f("L0_ffn_sb_up").reshape(II // 128, 128).T)

    ncF = build_fused(cfg)
    in_maps = []
    for ci in range(cfg.n_cores):
        in_maps.append({
            "xs": x[ci * cfg.t_core:(ci + 1) * cfg.t_core],
            "wk0t": np.ascontiguousarray(wk[0].T),
            "wk1t": np.ascontiguousarray(wk[1].T),
            "wv0t": np.ascontiguousarray(wv[0].T),
            "wv1t": np.ascontiguousarray(wv[1].T),
            "repsA": repsA,
            "wqt": wq_t, "wot": wo_t, "repsB": repsB,
            "gatew": f("L0_ffn_gate_w"),
            "gbias_rep": gbias_rep, "upw": f("L0_ffn_up_w"),
            "downw": f("L0_ffn_down_w"), "swup": f("L0_ffn_sw_up"),
            "swdown": f("L0_ffn_sw_down"), "upb_fm": upb_fm,
            "sbup_fm": sbup_fm, "downb": f("L0_ffn_down_b"),
            "c_col": c_col, "wmod": wmod, "ident": ident, "ones1": ones1,
        })
    res = run_fn(ncF, in_maps, core_ids=list(range(cfg.n_cores)))
    y = np.concatenate([np.asarray(r["y"]) for r in res.results], axis=0)
    return y.reshape(1, cfg.n_cores * cfg.t_core, D).astype(np.float32)


# version 07
# speedup vs baseline: 1.2122x; 1.2122x over previous
import contextlib
import numpy as np
import concourse.bass as bass
import concourse.mybir as mybir
from concourse.bass_core_rust import IndirectOffsetOnAxis
from concourse.bass_utils import run_bass_kernel_spmd

F32 = mybir.dt.float32
BF16 = mybir.dt.bfloat16
AF = mybir.ActivationFunctionType
ALU = mybir.AluOpType
AX = mybir.AxisListType

D = 1024
NCHUNK = 8
H = 16
HD = 64
KK = 2
EXP = 8
II = 4096
IG = 512
IGN = II // IG
ILN = IG // 128


B_STAGE = None
B_SUB = None


class Cfg:
    def __init__(self, n_cores=8, t_core=2048):
        self.n_cores = n_cores
        self.t_core = t_core
        self.nt = t_core // 128
        self.sct = min(8, self.nt)
        self.nsc = self.nt // self.sct
        self.scale = float(n_cores * t_core) ** -0.5
        self.cap_tiles = max(1, (self.sct * 128 * 2 * 2) // (EXP * 128))
        self.cap = self.cap_tiles * 128


class P:
    ENGS = ("sync", "tensor", "vector", "scalar", "gpsimd")

    def __init__(self, nc):
        self.nc = nc
        self.q = {e: [] for e in self.ENGS}
        self.semh = {}
        self.cnt = {}

    def sem(self, name):
        if name not in self.semh:
            self.semh[name] = self.nc.alloc_semaphore(name="sem_" + name)
            self.cnt[name] = 0
        return self.semh[name]

    def add(self, eng, fn, wait=(), inc=None, incby=1):
        seen = {}
        for t in wait:
            if t is None:
                continue
            nm, v = t
            if v > 0 and seen.get(nm, -1) < v:
                seen[nm] = v
        wl = [(self.sem(nm), v) for nm, v in seen.items()]
        tok = None
        semh = None
        if inc is not None:
            semh = self.sem(inc)
            self.cnt[inc] += incby
            tok = (inc, self.cnt[inc])

        def run(engine, wl=wl, fn=fn, semh=semh, incby=incby):
            for (sh, v) in wl:
                engine.wait_ge(sh, v)
            inst = fn(engine)
            if semh is not None and inst is not None:
                inst.then_inc(semh, incby)

        self.q[eng].append(run)
        return tok

    def waitonly(self, eng, wait):
        self.add(eng, lambda e: None, wait=wait)

    def now(self, name):
        return self.cnt.get(name, 0)

    def barrier_toks(self):
        return [(n, self.cnt[n]) for n in self.cnt]

    def finalize(self, block):
        for e in self.ENGS:
            fns = self.q[e]
            if fns:
                getattr(block, e)(lambda eng, fns=fns: [f(eng) for f in fns])


def _ap(x):
    return x[:] if not isinstance(x, bass.AP) else x


def pairs3(lhs_hi, lhs_lo, rhs_hi, rhs_lo, idx):
    out = []
    for j in idx:
        out += [(lhs_hi(j), rhs_hi(j)), (lhs_hi(j), rhs_lo(j)), (lhs_lo(j), rhs_hi(j))]
    return out


def mm_group(p, out_ap, pairs, wait_first, inc=True):
    n = len(pairs)
    tok = None
    for i, (la, ra) in enumerate(pairs):
        first, last = i == 0, i == n - 1
        tok = p.add("tensor",
                    lambda t, la=la, ra=ra, first=first, last=last:
                    t.matmul(out_ap, la, ra, start=first, stop=last),
                    wait=wait_first if first else (),
                    inc="MM" if (last and inc) else None)
    return tok



def emit_mod_phase(p, bufs, n_groups, want):
    csilu, mrow, wmst = bufs["csilu"], bufs["modrow"], bufs["wmstage"]
    pb = bufs["pb"]
    wmod = bufs["wmod_hbm"]
    t_sil = p.add("scalar", lambda s: s.activation(csilu[:], bufs["ccol"][:], AF.Silu),
                  wait=[bufs["tok_consts"]], inc="AC")
    t_zero = p.add("vector", lambda v: v.memset(mrow[0:1, 0:n_groups * 512], 0.0), inc="VE")
    acc_tok = t_zero
    stage_free = None
    for j in range(NCHUNK):
        t_dma = p.add("sync", lambda s, j=j: s.dma_start(
            out=wmst[:, 0:n_groups * 512],
            in_=wmod[j * 128:(j + 1) * 128, 0:n_groups * 512]),
            wait=[stage_free], inc="WMS", incby=16)
        for g in range(n_groups):
            bank = pb[6 + g % 2]
            t_mm = p.add("tensor", lambda t, j=j, g=g, bank=bank: t.matmul(
                bank[0:1, 0:512], csilu[:, j:j + 1],
                wmst[:, g * 512:(g + 1) * 512], start=True, stop=True),
                wait=[t_dma, t_sil, acc_tok, bufs["pbfree"][6 + g % 2]], inc="MM")
            bufs["pbfree"][6 + g % 2] = None
            acc_tok = p.add("vector", lambda v, g=g, bank=bank: v.tensor_add(
                mrow[0:1, g * 512:(g + 1) * 512], mrow[0:1, g * 512:(g + 1) * 512],
                bank[0:1, 0:512]), wait=[t_mm, acc_tok], inc="VE")
            bufs["pbfree"][6 + g % 2] = acc_tok
        stage_free = acc_tok
    last = acc_tok
    for slot, kind, rep in want:
        rep = _ap(rep)
        src = mrow[0:1, slot * 1024:(slot + 1) * 1024]
        dst = bufs["vrows"][0:1, slot * 1024:(slot + 1) * 1024]
        if kind == "plain":
            t_row = p.add("scalar", lambda s, src=src, dst=dst: s.copy(dst, src),
                          wait=[acc_tok], inc="AC")
        elif kind == "plus1":
            t_row = p.add("scalar", lambda s, src=src, dst=dst: s.add(dst, src, 1.0),
                          wait=[acc_tok], inc="AC")
        else:
            ab, sg = bufs["grow1"], bufs["grow2"]
            t1 = p.add("scalar", lambda s, src=src, ab=ab: s.activation(
                ab[0:1, :], src, AF.Abs), wait=[acc_tok, bufs["grow_free"]], inc="AC")
            t2 = p.add("vector", lambda v, ab=ab: v.tensor_scalar_max(
                ab[0:1, :], ab[0:1, :], 0.25), wait=[t1], inc="VE")
            t3 = p.add("vector", lambda v, src=src, sg=sg: v.tensor_single_scalar(
                sg[0:1, :], src, 0.0, ALU.is_ge),
                wait=[acc_tok, bufs["grow_free"]], inc="VE")
            t4 = p.add("vector", lambda v, sg=sg: v.tensor_scalar(
                sg[0:1, :], sg[0:1, :], 2.0, -1.0, ALU.mult, ALU.add),
                wait=[t3], inc="VE")
            t_row = p.add("vector", lambda v, ab=ab, sg=sg, dst=dst: v.tensor_tensor(
                dst, ab[0:1, :], sg[0:1, :], ALU.mult), wait=[t2, t4], inc="VE")
            bufs["grow_free"] = t_row
        for n in range(2):
            bi = 6 + n
            t_mm = p.add("tensor", lambda t, dst=dst, n=n, bi=bi: t.matmul(
                bufs["pb"][bi][:, 0:512], bufs["ones1"][:],
                dst[0:1, n * 512:(n + 1) * 512], start=True, stop=True),
                wait=[t_row, bufs["tok_ones"], bufs["pbfree"][bi]], inc="MM")
            last = p.add("vector", lambda v, rep=rep, n=n, bi=bi: v.tensor_copy(
                rep[:, n * 512:(n + 1) * 512], bufs["pb"][bi][:, 0:512]),
                wait=[t_mm], inc="VE")
            bufs["pbfree"][bi] = last
    return last


def emit_ln(p, bufs, par, in_ap, out_ap, mul_rep, add_rep, tok_in, extra=()):
    st, xc, sq = bufs["stats"][par], bufs["scr"][par], bufs["scr2"][par]
    mul_rep = None if mul_rep is None else _ap(mul_rep)
    add_rep = None if add_rep is None else _ap(add_rep)
    w = [tok_in] + list(extra)
    t0 = p.add("vector", lambda v: v.reduce_sum(st[:, 0:1], in_ap, axis=AX.X),
               wait=w + [bufs["stats_free"][par]], inc="VE")
    t1 = p.add("vector", lambda v: v.tensor_scalar_mul(st[:, 1:2], st[:, 0:1], 1.0 / D),
               wait=[t0], inc="VE")
    t2 = p.add("vector", lambda v: v.tensor_scalar(xc[:], in_ap, st[:, 1:2], None,
                                                   ALU.subtract),
               wait=[t1, bufs["scr_free"][par]], inc="VE")
    t3 = p.add("scalar", lambda s: s.activation(sq[:], xc[:], AF.Square),
               wait=[t2, bufs["scr2_free"][par]], inc="AC")
    t4 = p.add("vector", lambda v: v.reduce_sum(st[:, 2:3], sq[:], axis=AX.X),
               wait=[t3], inc="VE")
    t5 = p.add("vector", lambda v: v.tensor_scalar(st[:, 3:4], st[:, 2:3], 1.0 / D,
                                                   1e-5, ALU.mult, ALU.add),
               wait=[t4], inc="VE")
    t6 = p.add("scalar", lambda s: s.activation(st[:, 4:5], st[:, 3:4], AF.Sqrt),
               wait=[t5], inc="AC")
    t7 = p.add("vector", lambda v: v.reciprocal(st[:, 5:6], st[:, 4:5]),
               wait=[t6], inc="VE")
    if mul_rep is not None:
        t8 = p.add("vector", lambda v: v.scalar_tensor_tensor(
            out_ap, xc[:], st[:, 5:6], mul_rep, ALU.mult, ALU.mult),
            wait=[t7], inc="VE")
    else:
        t8 = p.add("vector", lambda v: v.tensor_scalar(out_ap, xc[:], st[:, 5:6],
                                                       None, ALU.mult),
                   wait=[t7], inc="VE")
    if add_rep is not None:
        t8 = p.add("vector", lambda v: v.tensor_add(out_ap, out_ap, add_rep),
                   wait=[t8], inc="VE")
    bufs["stats_free"][par] = t8
    bufs["scr_free"][par] = t8
    bufs["scr2_free"][par] = t4
    return t8


def emit_transpose_split(p, bufs, src_fn, hi, lo, tok_in, dst_free=None):
    pb = bufs["pb"]
    if dst_free is None:
        dst_free = []
    elif not isinstance(dst_free, list):
        dst_free = [dst_free]
    hi_tok = lo_tok = None
    for j in range(NCHUNK):
        bi = j % 2
        slot = pb[bi][:, 0:128]
        key = ("tr", bi)
        t_tr = p.add("tensor", lambda t, slot=slot, j=j: t.transpose(
            slot, src_fn(j), bufs["ident"][:]),
            wait=[tok_in, bufs["slot_free"].get(key), bufs["tok_ident"]], inc="MM")
        t_hi = p.add("scalar", lambda s, slot=slot, j=j: s.copy(
            hi[:, j * 128:(j + 1) * 128], slot),
            wait=[t_tr] + dst_free, inc="AC")
        t_lo = p.add("vector", lambda v, slot=slot, j=j: v.tensor_tensor(
            lo[:, j * 128:(j + 1) * 128], slot, hi[:, j * 128:(j + 1) * 128],
            ALU.subtract),
            wait=[t_hi] + dst_free, inc="VE")
        bufs["slot_free"][key] = t_lo
        hi_tok, lo_tok = t_hi, t_lo
    return hi_tok, lo_tok


def emit_split(p, src_ap, hi_ap, lo_ap, tok_in, extra=()):
    t_hi = p.add("vector", lambda v: v.tensor_copy(hi_ap, src_ap),
                 wait=[tok_in] + list(extra), inc="VE")
    t_lo = p.add("vector", lambda v: v.tensor_tensor(lo_ap, src_ap, hi_ap,
                                                     ALU.subtract),
                 wait=[t_hi], inc="VE")
    return t_lo


def emit_wload(p, bufs, hbm_ap, whi, wlo, fold_ident, prev_free, ncols=D,
               dst_free=None):
    ws = bufs["wstage"]
    tok = None
    for j in range(NCHUNK):
        t_dma = p.add("sync", lambda s, j=j: s.dma_start(
            out=ws[:, 0:ncols], in_=hbm_ap[j * 128:(j + 1) * 128, 0:ncols]),
            wait=[prev_free, bufs["wstage_free"]], inc="WST", incby=16)
        t_f = t_dma
        if fold_ident:
            t_f = p.add("vector", lambda v, j=j: v.tensor_add(
                ws[:, j * 128:(j + 1) * 128], ws[:, j * 128:(j + 1) * 128],
                bufs["ident"][:]), wait=[t_dma, bufs["tok_ident"]], inc="VE")
        t_hi = p.add("vector", lambda v, j=j: v.tensor_copy(
            whi[:, j * ncols:(j + 1) * ncols], ws[:, 0:ncols]),
            wait=[t_f, dst_free], inc="VE")
        tok = p.add("vector", lambda v, j=j: v.tensor_tensor(
            wlo[:, j * ncols:(j + 1) * ncols], ws[:, 0:ncols],
            whi[:, j * ncols:(j + 1) * ncols], ALU.subtract),
            wait=[t_hi, dst_free], inc="VE")
        bufs["wstage_free"] = tok
    return tok


def emit_block_load(p, fstage, src, nj, ncols, hi, lo, stage_free, dst_free):
    tok = stage_free
    last = None
    half = 2048 // ncols
    for h0 in range(0, nj, half):
        hj = min(half, nj - h0)
        t_d = p.add("sync", lambda s, h0=h0, hj=hj: s.dma_start(
            out=fstage[:, 0:hj * ncols].rearrange("p (j c) -> p j c", j=hj),
            in_=src[h0 * 128:(h0 + hj) * 128, :].rearrange(
                "(j p) c -> j p c", j=hj).transpose([1, 0, 2])),
            wait=[tok], inc="FST", incby=16)
        t_hi = p.add("vector", lambda v, h0=h0, hj=hj: v.tensor_copy(
            hi[:, h0 * ncols:(h0 + hj) * ncols], fstage[:, 0:hj * ncols]),
            wait=[t_d, dst_free], inc="VE")
        t_lo = p.add("vector", lambda v, h0=h0, hj=hj: v.tensor_tensor(
            lo[:, h0 * ncols:(h0 + hj) * ncols], fstage[:, 0:hj * ncols],
            hi[:, h0 * ncols:(h0 + hj) * ncols], ALU.subtract),
            wait=[t_hi, dst_free], inc="VE")
        tok = t_lo
        last = t_lo
    return last


def emit_kernel_norm(p, bufs, par, q_ap, out_ap, inv_scale, tok_in):
    t1, sq, hs = bufs["scr"][par], bufs["scr2"][par], bufs["hstats"][par]
    ta = p.add("scalar", lambda s: s.activation(t1[:], q_ap, AF.Silu),
               wait=[tok_in, bufs["scr_free"][par]], inc="AC")
    tb = p.add("vector", lambda v: v.tensor_scalar_add(t1[:], t1[:], 0.5),
               wait=[ta], inc="VE")
    tc = p.add("scalar", lambda s: s.activation(sq[:], t1[:], AF.Square),
               wait=[tb, bufs["scr2_free"][par]], inc="AC")
    td = p.add("vector", lambda v: v.reduce_sum(
        hs[:, 0:H], sq[:].rearrange("p (h d) -> p h d", h=H), axis=AX.X),
        wait=[tc, bufs["hstats_free"][par]], inc="VE")
    te = p.add("scalar", lambda s: s.activation(hs[:, H:2 * H], hs[:, 0:H], AF.Sqrt),
               wait=[td], inc="AC")
    tf = p.add("vector", lambda v: v.tensor_scalar_add(hs[:, H:2 * H],
                                                       hs[:, H:2 * H], 1e-6),
               wait=[te], inc="VE")
    tg = p.add("vector", lambda v: v.reciprocal(hs[:, 2 * H:3 * H], hs[:, H:2 * H]),
               wait=[tf], inc="VE")
    if inv_scale != 1.0:
        tg = p.add("vector", lambda v: v.tensor_scalar_mul(
            hs[:, 2 * H:3 * H], hs[:, 2 * H:3 * H], inv_scale), wait=[tg], inc="VE")
    bc = hs[:, 2 * H:3 * H].unsqueeze(2).broadcast_to((128, H, HD))
    th = p.add("vector", lambda v: v.tensor_tensor(
        out_ap.rearrange("p (h d) -> p h d", h=H),
        t1[:].rearrange("p (h d) -> p h d", h=H), bc, ALU.mult),
        wait=[tg], inc="VE")
    bufs["scr_free"][par] = th
    bufs["scr2_free"][par] = td
    bufs["hstats_free"][par] = th
    return th


def setup_common(nc, p, cfg, bufs, es):
    al = lambda n, sh, dt=F32: es.enter_context(nc.sbuf_tensor(n, sh, dt))
    bufs["pb"] = [es.enter_context(nc.psum_tensor(f"pb{i}", [128, 512], F32))
                  for i in range(8)]
    bufs["pbfree"] = [None] * 8
    bufs["slot_free"] = {}
    bufs["grow_free"] = None
    bufs["ident"] = al("ident_sb", [128, 128])
    bufs["ones1"] = al("ones1_sb", [1, 128])
    bufs["ccol"] = al("ccol_sb", [128, 8])
    bufs["csilu"] = al("csilu_sb", [128, 8])
    bufs["stats"] = [al(f"stats{i}", [128, 8]) for i in range(2)]
    bufs["hstats"] = [al(f"hstats{i}", [128, 3 * H]) for i in range(2)]
    bufs["scr"] = [al(f"scr{i}", [128, D]) for i in range(2)]
    bufs["scr2"] = [al(f"scr2_{i}", [128, D]) for i in range(2)]
    bufs["xbuf"] = [al(f"xbuf{i}", [128, D]) for i in range(2)]
    bufs["h1tm"] = [al(f"h1tm{i}", [128, D]) for i in range(2)]
    for nm in ("stats_free", "scr_free", "scr2_free", "hstats_free",
               "xbuf_free", "h1tm_free"):
        bufs[nm] = [None, None]
    ident_h = nc.dram_tensor("ident", [128, 128], F32, kind="ExternalInput").ap()
    ones1_h = nc.dram_tensor("ones1", [1, 128], F32, kind="ExternalInput").ap()
    ccol_h = nc.dram_tensor("c_col", [128, 8], F32, kind="ExternalInput").ap()
    bufs["x_hbm"] = nc.dram_tensor("xs", [cfg.t_core, D], F32,
                                   kind="ExternalInput").ap()
    bufs["wmod_hbm"] = nc.dram_tensor("wmod", [D, 6 * D], F32,
                                      kind="ExternalInput").ap()
    t_i = p.add("sync", lambda s: s.dma_start(out=bufs["ident"][:], in_=ident_h),
                inc="LDI", incby=16)
    t_o = p.add("sync", lambda s: s.dma_start(out=bufs["ones1"][:], in_=ones1_h),
                inc="LDO", incby=16)
    t_c = p.add("sync", lambda s: s.dma_start(
        out=bufs["ccol"][:], in_=ccol_h), inc="LDC", incby=16)
    bufs["tok_ident"] = t_i
    bufs["tok_ones"] = t_o
    bufs["tok_consts"] = t_c


def emit_x_load(p, bufs, t):
    par = t % 2
    xb = bufs["xbuf"][par]
    tok = p.add("sync", lambda s: s.dma_start(
        out=xb[:], in_=bufs["x_hbm"][t * 128:(t + 1) * 128, :]),
        wait=[bufs["xbuf_free"][par]], inc=f"XB{par}", incby=16)
    return xb, tok



def _emit_A(nc, p, bufs, cfg, es):
    if True:
        al = lambda n, sh, dt=F32: es.enter_context(nc.sbuf_tensor(n, sh, dt))
        wk_h = [nc.dram_tensor(f"wk{k}t", [D, D], F32, kind="ExternalInput").ap()
                for k in range(KK)]
        wv_h = [nc.dram_tensor(f"wv{k}t", [D, D], F32, kind="ExternalInput").ap()
                for k in range(KK)]
        repsA_h = nc.dram_tensor("repsA", [6, 128, D], F32,
                                 kind="ExternalInput").ap()

        reps = al("repsA_sb", [128, 6 * D])
        ca1_rep = reps[:, 4 * D:5 * D]
        sa_rep = reps[:, 5 * D:6 * D]
        h1hi = al("h1hi", [128, D], BF16)
        h1lo = al("h1lo", [128, D], BF16)
        kshi = al("kshi", [128, D], BF16)
        kslo = al("kslo", [128, D], BF16)
        vshi = al("vshi", [128, D], BF16)
        vslo = al("vslo", [128, D], BF16)
        kvtm = al("kvtm", [128, D])
        stev = al("stev", [64, 4 * 512])
        bufs["modrow"] = al("modrowA", [1, 2 * D])
        bufs["vrows"] = al("vrowsA", [1, 2 * D])
        bufs["grow1"] = al("grow1A", [1, D])
        bufs["grow2"] = al("grow2A", [1, D])
        bufs["wmstage"] = al("wmstageA", [128, 4 * 512])

        t_reps = p.add("sync", lambda s: s.dma_start(
            out=reps[:].rearrange("p (n d) -> p n d", n=6),
            in_=repsA_h.transpose([1, 0, 2])), inc="LDR", incby=16)
        t_bvs = t_reps
        for k in range(KK):
            t_bvs = p.add("vector", lambda v, k=k: v.tensor_scalar_mul(
                reps[:, (2 + k) * D:(3 + k) * D], reps[:, (2 + k) * D:(3 + k) * D],
                cfg.scale), wait=[t_bvs], inc="VE")

        t_mod = emit_mod_phase(p, bufs, 4,
                               [(1, "plus1", ca1_rep), (0, "plain", sa_rep)])

        wkhi = al("wkhi", [128, NCHUNK * D], BF16)
        wklo = al("wklo", [128, NCHUNK * D], BF16)
        wvhi = al("wvhi", [128, NCHUNK * D], BF16)
        wvlo = al("wvlo", [128, NCHUNK * D], BF16)
        bufs["wstage"] = al("wstageA", [128, D])
        bufs["wstage_free"] = None

        kv_free = None
        h1_free = None
        ks_free = None
        vs_free = None
        st_open = [False] * 8
        last_states = [None] * 4
        wk_free = None
        for k in range(KK):
          t_w = emit_wload(p, bufs, wk_h[k], wkhi, wklo, True, wk_free,
                           dst_free=wk_free)
          t_w = emit_wload(p, bufs, wv_h[k], wvhi, wvlo, True, t_w,
                           dst_free=wk_free)
          for t in range(cfg.nt):
            par = t % 2
            xb, t_x = emit_x_load(p, bufs, t)
            h1 = bufs["h1tm"][par]
            t_h1 = emit_ln(p, bufs, par, xb[:], h1[:], ca1_rep, sa_rep, t_x,
                           extra=[t_mod, bufs["h1tm_free"][par]])
            bufs["xbuf_free"][par] = t_h1
            t_hi, t_lo = emit_transpose_split(
                p, bufs, lambda j, h1=h1: h1[:, j * 128:(j + 1) * 128],
                h1hi, h1lo, t_h1, dst_free=h1_free)
            bufs["h1tm_free"][par] = t_lo
            lhs_hi = lambda j: h1hi[:, j * 128:(j + 1) * 128]
            lhs_lo = lambda j: h1lo[:, j * 128:(j + 1) * 128]
            if True:
                toks_k = []
                for ng in range(2):
                    prs = pairs3(
                        lhs_hi, lhs_lo,
                        lambda j, ng=ng: wkhi[
                            :, j * D + ng * 512: j * D + ng * 512 + 512],
                        lambda j, ng=ng: wklo[
                            :, j * D + ng * 512: j * D + ng * 512 + 512],
                        range(NCHUNK))
                    toks_k.append(mm_group(
                        p, bufs["pb"][2 + ng][:], prs,
                        [t_hi, t_lo, t_w, bufs["pbfree"][2 + ng]]))
                t_ev = None
                for ng in range(2):
                    t_ev = p.add("vector", lambda v, ng=ng, k=k: v.tensor_add(
                        kvtm[:, ng * 512:(ng + 1) * 512], bufs["pb"][2 + ng][:],
                        reps[:, k * D + ng * 512:k * D + ng * 512 + 512]),
                        wait=[toks_k[ng], kv_free, t_reps], inc="VE")
                    bufs["pbfree"][2 + ng] = t_ev
                t_kn = emit_kernel_norm(p, bufs, par, kvtm[:], kvtm[:],
                                        cfg.scale, t_ev)
                t_ks = emit_split(p, kvtm[:], kshi[:], kslo[:], t_kn,
                                  extra=[ks_free])
                toks_v = []
                for ng in range(2):
                    prs = pairs3(
                        lhs_hi, lhs_lo,
                        lambda j, ng=ng: wvhi[
                            :, j * D + ng * 512: j * D + ng * 512 + 512],
                        lambda j, ng=ng: wvlo[
                            :, j * D + ng * 512: j * D + ng * 512 + 512],
                        range(NCHUNK))
                    toks_v.append(mm_group(
                        p, bufs["pb"][2 + ng][:], prs,
                        [t_hi, t_lo, t_ks, bufs["pbfree"][2 + ng]]))
                t_ev2 = None
                for ng in range(2):
                    t_ev2 = p.add("vector", lambda v, ng=ng, k=k:
                                  v.scalar_tensor_tensor(
                        kvtm[:, ng * 512:(ng + 1) * 512], bufs["pb"][2 + ng][:],
                        cfg.scale,
                        reps[:, (2 + k) * D + ng * 512:(2 + k) * D + ng * 512 + 512],
                        ALU.mult, ALU.add),
                        wait=[toks_v[ng], t_ks, t_bvs], inc="VE")
                    bufs["pbfree"][2 + ng] = t_ev2
                t_vs = emit_split(p, kvtm[:], vshi[:], vslo[:], t_ev2,
                                  extra=[vs_free])
                kv_free = t_vs
                tok_mm = None
                n_i = 0
                for h in range(H):
                    bi = 4 + 2 * k + h // 8
                    c0 = (h % 8) * 64
                    bank = bufs["pb"][bi]
                    for (ka, va) in ((kshi, vshi), (kshi, vslo), (kslo, vshi)):
                        first = not st_open[bi]
                        st_open[bi] = True
                        last_tile = (h == H - 1) and (ka is kslo)
                        bank_last = (h % 8 == 7) and (ka is kslo)
                        last_glob = (t == cfg.nt - 1) and bank_last
                        w = ([t_ks, t_vs, bufs["pbfree"][bi]]
                             if n_i == 0 else [])
                        tok_mm = p.add("tensor", lambda tt, ka=ka, va=va, h=h,
                                       c0=c0, first=first, last=last_glob,
                                       bank=bank: tt.matmul(
                            bank[0:64, c0:c0 + 64],
                            ka[:, h * 64:(h + 1) * 64],
                            va[:, h * 64:(h + 1) * 64],
                            start=first, stop=last),
                            wait=w, inc="MM" if bank_last else None)
                        if bank_last:
                            last_states[bi - 4] = tok_mm
                        n_i += 1
                ks_free = tok_mm
                vs_free = tok_mm
            h1_free = kv_free
          wk_free = last_states[2 * k + 1]

        t_out = None
        t_e = None
        for kb in range(4):
            t_e = p.add("vector", lambda v, kb=kb: v.tensor_copy(
                stev[0:64, kb * 512:(kb + 1) * 512], bufs["pb"][4 + kb][0:64, :]),
                wait=[last_states[kb], t_e], inc="VE")
        return stev, t_e


def build_A(cfg):
    nc = bass.Bass("TRN2", target_bir_lowering=False)
    p = P(nc)
    bufs = {}
    es = contextlib.ExitStack()
    with es:
        setup_common(nc, p, cfg, bufs, es)
        spart_h = nc.dram_tensor("spart", [64, 2048], F32,
                                 kind="ExternalOutput").ap()
        stev, t_e = _emit_A(nc, p, bufs, cfg, es)
        t_out = p.add("gpsimd", lambda g: g.dma_start(
            out=spart_h, in_=stev[0:64, :]),
            wait=[t_e], inc="GSP", incby=16)
        p.waitonly("gpsimd", [t_out])
        with nc.Block() as block:
            p.finalize(block)
    return nc



def _emit_B(nc, p, bufs, cfg, es, sp_src=None):
    if True:
        al = lambda n, sh, dt=F32: es.enter_context(nc.sbuf_tensor(n, sh, dt))

        y_h = nc.dram_tensor("y", [cfg.t_core, D], F32, kind="ExternalOutput").ap()
        wq_h = nc.dram_tensor("wqt", [D, D], F32, kind="ExternalInput").ap()
        wo_h = nc.dram_tensor("wot", [D, D], F32, kind="ExternalInput").ap()
        repsB_h = nc.dram_tensor("repsB", [5, 128, D], F32,
                                 kind="ExternalInput").ap()
        if sp_src is None:
            spart_h = nc.dram_tensor("spart_all", [cfg.n_cores, 64, 2048], F32,
                                     kind="ExternalInput").ap()
        gatew_h = nc.dram_tensor("gatew", [D, EXP], F32, kind="ExternalInput").ap()
        gbias_h = nc.dram_tensor("gbias_rep", [128, EXP], F32,
                                 kind="ExternalInput").ap()
        upw_h = nc.dram_tensor("upw", [EXP, D, II], F32, kind="ExternalInput").ap()
        downw_h = nc.dram_tensor("downw", [EXP, II, D], F32,
                                 kind="ExternalInput").ap()
        swup_h = nc.dram_tensor("swup", [D, II], F32, kind="ExternalInput").ap()
        swdown_h = nc.dram_tensor("swdown", [II, D], F32, kind="ExternalInput").ap()
        upb_h = nc.dram_tensor("upb_fm", [EXP, 128, II // 128], F32,
                               kind="ExternalInput").ap()
        sbup_h = nc.dram_tensor("sbup_fm", [128, II // 128], F32,
                                kind="ExternalInput").ap()
        downb_h = nc.dram_tensor("downb", [EXP, D], F32, kind="ExternalInput").ap()
        lts_h = nc.dram_tensor("lts", [128, 128], BF16, kind="ExternalInput").ap()
        onec_h = nc.dram_tensor("onec", [128, 1], BF16, kind="ExternalInput").ap()
        lts8_h = nc.dram_tensor("lts8", [EXP * cfg.sct, EXP * cfg.sct], BF16,
                                kind="ExternalInput").ap()
        x2_sc = nc.dram_tensor("x2_sc", [cfg.t_core, D], F32).ap()
        h3hi_sc = nc.dram_tensor("h3hi_sc", [cfg.nt, 128, D], BF16).ap()
        h3lo_sc = nc.dram_tensor("h3lo_sc", [cfg.nt, 128, D], BF16).ap()
        h3aug_sc = nc.dram_tensor("h3aug_sc", [cfg.nt, 128, 1032], F32).ap()
        compact_d = [nc.dram_tensor(f"compact_d{e}", [cfg.cap + 1, 1032],
                                    F32).ap() for e in range(EXP)]
        eo_d = [nc.dram_tensor(f"eo_d{e}", [cfg.cap + 1, D], F32).ap()
                for e in range(EXP)]

        gb_rep = al("gb_rep", [128, D])
        sbd_rep = al("sbd_rep", [128, D])
        gbias = al("gbias_sb", [128, EXP])
        w_all = al("w_all", [128, cfg.nt * EXP])
        se_all = al("se_all", [128, cfg.nt * EXP])
        downb_sb = al("downb_sb", [8, D])
        mr1 = al("mr1", [128, 8])
        mr2 = al("mr2", [128, 8])
        gsc = al("gsc", [128, 48])
        x3tm = al("x3tm", [128, D])

        t_gb = p.add("sync", lambda s: s.dma_start(out=gbias[:], in_=gbias_h),
                     inc="LDG", incby=16)
        t_db = p.add("sync", lambda s: s.dma_start(out=downb_sb[:], in_=downb_h),
                     inc="LDD", incby=16)
        t_sbd0 = p.add("sync", lambda s: s.dma_start(
            out=sbd_rep[:], in_=repsB_h[4]), inc="LDS", incby=16)
        t_sbd = p.add("vector", lambda v: v.tensor_scalar_mul(sbd_rep[:], sbd_rep[:],
                                                              0.1),
                      wait=[t_sbd0], inc="VE")
        t_mr = p.add("vector", lambda v: v.memset(mr1[:], -1e30), inc="VE")
        t_mr = p.add("vector", lambda v: v.memset(mr2[:], -1e30), wait=[t_mr],
                     inc="VE")

        x2w_toks = [None] * cfg.nt
        h3_last = None
        y_tok = None

        s1 = contextlib.ExitStack()
        with s1:
            a1 = lambda n, sh, dt=F32: s1.enter_context(nc.sbuf_tensor(n, sh, dt))
            ca1_rep = a1("ca1_rep", [128, D])
            sa_rep = a1("sa_rep", [128, D])
            cb1_rep = a1("cb1_rep", [128, D])
            sbm_rep = a1("sbm_rep", [128, D])
            ga_rep = a1("ga_rep", [128, D])
            with contextlib.ExitStack() as s1a:
                bufs["modrow"] = s1a.enter_context(
                    nc.sbuf_tensor("modrowB", [1, 6 * D], F32))
                bufs["vrows"] = s1a.enter_context(
                    nc.sbuf_tensor("vrowsB", [1, 6 * D], F32))
                bufs["wmstage"] = s1a.enter_context(
                    nc.sbuf_tensor("wmstageB", [128, 12 * 512], F32))
                bufs["grow1"] = s1a.enter_context(
                    nc.sbuf_tensor("grow1B", [1, D], F32))
                bufs["grow2"] = s1a.enter_context(
                    nc.sbuf_tensor("grow2B", [1, D], F32))
                t_mod = emit_mod_phase(p, bufs, 12, [
                    (1, "plus1", ca1_rep), (0, "plain", sa_rep),
                    (4, "plus1", cb1_rep), (3, "plain", sbm_rep),
                    (2, "gate", ga_rep), (5, "gate", gb_rep)])
            reps = a1("repsB_sb", [128, 4 * D])
            bq_rep, bo_rep = reps[:, 0:D], reps[:, D:2 * D]
            lng_rep, lnb_rep = reps[:, 2 * D:3 * D], reps[:, 3 * D:4 * D]
            t_reps = p.add("sync", lambda s: s.dma_start(
                out=reps[:].rearrange("p (n d) -> p n d", n=4),
                in_=repsB_h[0:4].transpose([1, 0, 2])),
                wait=[t_mod], inc="LDR", incby=16)

            st_f32 = a1("st_f32", [128, 1024])
            st_hi = a1("st_hi", [128, 1024], BF16)
            st_lo = a1("st_lo", [128, 1024], BF16)
            with contextlib.ExitStack() as s1b:
                sp_acc = s1b.enter_context(nc.sbuf_tensor("sp_acc", [64, 2048], F32))
                if sp_src is None:
                    sp_buf = s1b.enter_context(
                        nc.sbuf_tensor("sp_buf", [64, 2048], F32))
                    t_acc = None
                    for c in range(cfg.n_cores):
                        t_l = p.add("sync", lambda s, c=c: s.dma_start(
                            out=sp_buf[:], in_=spart_h[c]),
                            wait=[t_acc, t_mod], inc="SPB", incby=16)
                        if c == 0:
                            t_acc = p.add("vector", lambda v: v.tensor_copy(
                                sp_acc[:], sp_buf[:]), wait=[t_l], inc="VE")
                        else:
                            t_acc = p.add("vector", lambda v: v.tensor_add(
                                sp_acc[:], sp_acc[:], sp_buf[:]),
                                wait=[t_l], inc="VE")
                else:
                    sp_ap, sp_tok = sp_src
                    t_acc = p.add("sync", lambda s: s.dma_start(
                        out=sp_acc[:], in_=sp_ap),
                        wait=[sp_tok, t_mod], inc="SPB", incby=16)
                t_st0 = p.add("vector", lambda v: v.tensor_tensor(
                    st_f32[0:64, :], sp_acc[:, 0:1024], sp_acc[:, 1024:2048],
                    ALU.mult), wait=[t_acc], inc="VE")
            t_st1 = p.add("sync", lambda s: s.dma_start(
                out=st_f32[64:128, :], in_=st_f32[0:64, :]),
                wait=[t_st0], inc="STR", incby=16)
            t_st = emit_split(p, st_f32[:], st_hi[:], st_lo[:], t_st1)

            wqhi = a1("wqhi", [128, NCHUNK * D], BF16)
            wqlo = a1("wqlo", [128, NCHUNK * D], BF16)
            wohi = a1("wohi", [128, NCHUNK * D], BF16)
            wolo = a1("wolo", [128, NCHUNK * D], BF16)
            gwhi = a1("gwhi", [128, NCHUNK * EXP], BF16)
            gwlo = a1("gwlo", [128, NCHUNK * EXP], BF16)
            bufs["wstage"] = a1("wstageB", [128, D])
            bufs["wstage_free"] = None
            t_w = emit_wload(p, bufs, wq_h, wqhi, wqlo, True, t_mod,
                             dst_free=t_st)
            t_w = emit_wload(p, bufs, wo_h, wohi, wolo, False, t_w)
            t_w = emit_wload(p, bufs, gatew_h, gwhi, gwlo, False, t_w, ncols=EXP)

            h1hi = a1("h1hiB", [128, D], BF16)
            h1lo = a1("h1loB", [128, D], BF16)
            qtm = a1("qtm", [128, D])
            qfhi = a1("qfhi", [128, D], BF16)
            qflo = a1("qflo", [128, D], BF16)
            attm = a1("attm", [128, D])
            lfhi = a1("lfhi", [128, D], BF16)
            lflo = a1("lflo", [128, D], BF16)
            x2tm = a1("x2tm", [128, D])
            h3tm = a1("h3tm", [128, 1032])
            h3hi = a1("h3hi", [128, D], BF16)
            h3lo = a1("h3lo", [128, D], BF16)

            h1_free = None
            qtm_free = None
            qf_free = None
            attm_free = None
            lf_free = None
            x2_free = None
            h3tm_free = []
            h3p_free = None
            gate_free = None

            nt_eff = 0 if (B_STAGE is not None and B_STAGE < 2) else cfg.nt
            for t in range(nt_eff):
                par = t % 2
                xb, t_x = emit_x_load(p, bufs, t)
                h1 = bufs["h1tm"][par]
                t_h1 = emit_ln(p, bufs, par, xb[:], h1[:], ca1_rep, sa_rep, t_x,
                               extra=[t_mod, bufs["h1tm_free"][par]])
                t_hi, t_lo = emit_transpose_split(
                    p, bufs, lambda j, h1=h1: h1[:, j * 128:(j + 1) * 128],
                    h1hi, h1lo, t_h1, dst_free=h1_free)
                bufs["h1tm_free"][par] = t_lo
                if B_SUB is not None and B_SUB < 2:
                    h1_free = t_lo
                    continue
                toks_q = []
                for ng in range(2):
                    prs = pairs3(
                        lambda j: h1hi[:, j * 128:(j + 1) * 128],
                        lambda j: h1lo[:, j * 128:(j + 1) * 128],
                        lambda j, ng=ng: wqhi[:, j * D + ng * 512:
                                              j * D + ng * 512 + 512],
                        lambda j, ng=ng: wqlo[:, j * D + ng * 512:
                                              j * D + ng * 512 + 512],
                        range(NCHUNK))
                    toks_q.append(mm_group(
                        p, bufs["pb"][2 + ng][:], prs,
                        [t_hi, t_lo, t_w, bufs["pbfree"][2 + ng]]))
                t_ev = None
                for ng in range(2):
                    t_ev = p.add("vector", lambda v, ng=ng: v.tensor_add(
                        qtm[:, ng * 512:(ng + 1) * 512], bufs["pb"][2 + ng][:],
                        bq_rep[:, ng * 512:ng * 512 + 512]),
                        wait=[toks_q[ng], qtm_free, t_reps], inc="VE")
                    bufs["pbfree"][2 + ng] = t_ev
                if B_SUB is not None and B_SUB < 3:
                    h1_free = toks_q[1]
                    qtm_free = t_ev
                    continue
                t_kn = emit_kernel_norm(p, bufs, par, qtm[:], qtm[:], 1.0, t_ev)
                t_qhi, t_qlo = emit_transpose_split(
                    p, bufs, lambda j: qtm[:, j * 128:(j + 1) * 128],
                    qfhi, qflo, t_kn, dst_free=qf_free)
                qtm_free = t_qlo
                if B_SUB is not None and B_SUB < 4:
                    h1_free = toks_q[1]
                    qf_free = t_qlo
                    continue
                tok_at = [None, None]
                for hb in range(2):
                    bank = bufs["pb"][4 + hb]
                    r0 = hb * 64
                    n_i = 0
                    for sslot in range(8):
                        h = 2 * sslot + hb
                        oc = sslot * 64
                        for (qa, sa_) in ((qfhi, st_hi), (qfhi, st_lo),
                                          (qflo, st_hi)):
                            first = n_i == 0
                            last = n_i == 23
                            w = ([t_qhi, t_qlo, t_st, bufs["pbfree"][4 + hb]]
                                 if first else [])
                            tok = p.add("tensor", lambda tt, qa=qa, sa_=sa_, h=h,
                                        r0=r0, oc=oc, first=first,
                                        last=last, bank=bank: tt.matmul(
                                bank[:, oc:oc + 64],
                                qa[r0:r0 + 64, (h // 2) * 128:(h // 2) * 128 + 128],
                                sa_[r0:r0 + 64, h * 64:(h + 1) * 64],
                                start=first, stop=last),
                                wait=w, inc="MM" if last else None)
                            n_i += 1
                    tok_at[hb] = tok
                qf_free = tok_at[1]
                t_evat = None
                for hb in range(2):
                    dst = attm[:].rearrange("p (s x) -> p s x", x=128)[
                        :, :, hb * 64:(hb + 1) * 64]
                    src = bufs["pb"][4 + hb][:].rearrange(
                        "p (s x) -> p s x", x=64)
                    t_evat = p.add("vector", lambda v, dst=dst, src=src:
                                   v.tensor_copy(dst, src),
                        wait=[tok_at[hb], attm_free], inc="VE")
                    bufs["pbfree"][4 + hb] = t_evat
                if B_SUB is not None and B_SUB < 5:
                    h1_free = toks_q[1]
                    continue
                t_ln = emit_ln(p, bufs, par, attm[:], attm[:], lng_rep, lnb_rep,
                               t_evat)
                t_lhi, t_llo = emit_transpose_split(
                    p, bufs, lambda j: attm[:, j * 128:(j + 1) * 128],
                    lfhi, lflo, t_ln, dst_free=lf_free)
                attm_free = t_llo
                toks_o = []
                for ng in range(2):
                    prs = pairs3(
                        lambda j: lfhi[:, j * 128:(j + 1) * 128],
                        lambda j: lflo[:, j * 128:(j + 1) * 128],
                        lambda j, ng=ng: wohi[:, j * D + ng * 512:
                                              j * D + ng * 512 + 512],
                        lambda j, ng=ng: wolo[:, j * D + ng * 512:
                                              j * D + ng * 512 + 512],
                        range(NCHUNK))
                    toks_o.append(mm_group(
                        p, bufs["pb"][2 + ng][:], prs,
                        [t_lhi, t_llo, bufs["pbfree"][2 + ng]]))
                lf_free = toks_o[1]
                if B_SUB is not None and B_SUB < 6:
                    h1_free = toks_q[1]
                    continue
                t_x2 = None
                for ng in range(2):
                    sl = slice(ng * 512, (ng + 1) * 512)
                    t_a = p.add("vector", lambda v, ng=ng, sl=sl: v.tensor_add(
                        x2tm[:, sl], bufs["pb"][2 + ng][:], bo_rep[:, sl]),
                        wait=[toks_o[ng], x2_free, t_reps], inc="VE")
                    bufs["pbfree"][2 + ng] = t_a
                    t_b = p.add("vector", lambda v, sl=sl: v.tensor_tensor(
                        x2tm[:, sl], x2tm[:, sl], ga_rep[:, sl], ALU.mult),
                        wait=[t_a], inc="VE")
                    t_x2 = p.add("vector", lambda v, sl=sl, xb=xb: v.tensor_add(
                        x2tm[:, sl], x2tm[:, sl], xb[:, sl]),
                        wait=[t_b, t_x], inc="VE")
                bufs["xbuf_free"][par] = t_x2
                t_x2w = p.add("gpsimd", lambda g, t=t: g.dma_start(
                    out=x2_sc[t * 128:(t + 1) * 128, :], in_=x2tm[:]),
                    wait=[t_x2], inc="GX2", incby=16)
                x2w_toks[t] = t_x2w
                x2_free = t_x2w
                if B_SUB is not None and B_SUB < 7:
                    h1_free = toks_q[1]
                    continue
                t_h3 = emit_ln(p, bufs, par, x2tm[:], h3tm[:, 0:D], cb1_rep,
                               sbm_rep, t_x2, extra=h3tm_free)
                t3hi, t3lo = emit_transpose_split(
                    p, bufs, lambda j: h3tm[:, j * 128:(j + 1) * 128],
                    h3hi, h3lo, t_h3, dst_free=h3p_free)
                t_w1 = p.add("gpsimd", lambda g, t=t: g.dma_start(
                    out=h3hi_sc[t], in_=h3hi[:]), wait=[t3hi, t3lo, h3_last],
                    inc="GH3", incby=16)
                t_w2 = p.add("gpsimd", lambda g, t=t: g.dma_start(
                    out=h3lo_sc[t], in_=h3lo[:]), wait=[t3lo, t_w1],
                    inc="GH3", incby=16)
                h3_last = t_w2
                if B_STAGE is not None and B_STAGE < 3:
                    h1_free = toks_q[1]
                    h3p_free = [t_w1, t_w2]
                    continue
                prs = pairs3(
                    lambda j: h3hi[:, j * 128:(j + 1) * 128],
                    lambda j: h3lo[:, j * 128:(j + 1) * 128],
                    lambda j: gwhi[:, j * EXP:(j + 1) * EXP],
                    lambda j: gwlo[:, j * EXP:(j + 1) * EXP],
                    range(NCHUNK))
                tok_g = mm_group(p, bufs["pb"][6][:, 0:EXP], prs,
                                 [t3hi, t3lo, t_w, bufs["pbfree"][6]])
                sS, s2, s3 = gsc[:, 0:8], gsc[:, 8:16], gsc[:, 16:24]
                msk, t8, den = gsc[:, 24:32], gsc[:, 32:40], gsc[:, 40:41]
                t_g1 = p.add("vector", lambda v: v.tensor_add(
                    sS, bufs["pb"][6][:, 0:EXP], gbias[:]),
                    wait=[tok_g, t_gb, gate_free], inc="VE")
                bufs["pbfree"][6] = t_g1
                t_g2 = p.add("scalar", lambda s: s.activation(sS, sS, AF.Sigmoid),
                             wait=[t_g1], inc="AC")
                t_g3 = p.add("vector", lambda v: v.max(t8, sS), wait=[t_g2],
                             inc="VE")
                t_g4 = p.add("vector", lambda v: v.tensor_copy(mr1[:, 0:1],
                                                               t8[:, 0:1]),
                             wait=[t_g3, t_mr], inc="VE")
                t_g5 = p.add("vector", lambda v: v.tensor_copy(mr2[:, 0:1],
                                                               t8[:, 1:2]),
                             wait=[t_g3], inc="VE")
                t_g6 = p.add("vector", lambda v: v.match_replace(
                    s2, mr1[:], sS, -1e30), wait=[t_g4, t_g2], inc="VE")
                t_g7 = p.add("vector", lambda v: v.match_replace(
                    s3, mr2[:], s2, -1e30), wait=[t_g5, t_g6], inc="VE")
                t_g8 = p.add("vector", lambda v: v.tensor_tensor(
                    msk, sS, s3, ALU.not_equal), wait=[t_g7], inc="VE")
                t_g9 = p.add("vector", lambda v: v.tensor_add(
                    den, t8[:, 0:1], t8[:, 1:2]), wait=[t_g3], inc="VE")
                t_g10 = p.add("vector", lambda v: v.tensor_scalar_add(
                    den, den, 1e-6), wait=[t_g9], inc="VE")
                t_g11 = p.add("vector", lambda v: v.reciprocal(den, den),
                              wait=[t_g10], inc="VE")
                wsl = w_all[:, t * EXP:(t + 1) * EXP]
                t_g12 = p.add("vector", lambda v, wsl=wsl: v.tensor_tensor(
                    wsl, sS, msk, ALU.mult), wait=[t_g8], inc="VE")
                t_g13 = p.add("vector", lambda v, wsl=wsl: v.tensor_scalar(
                    wsl, wsl, den, None, ALU.mult), wait=[t_g12, t_g11], inc="VE")
                p.add("vector", lambda v, t=t, wsl=wsl: v.tensor_scalar_mul(
                    se_all[:, t * EXP:(t + 1) * EXP], wsl, 0.1),
                    wait=[t_g13], inc="VE")
                t_wc = p.add("vector", lambda v, wsl=wsl: v.tensor_copy(
                    h3tm[:, D:D + EXP], wsl), wait=[t_g13] + h3tm_free, inc="VE")
                t_w3 = p.add("gpsimd", lambda g, t=t: g.dma_start(
                    out=h3aug_sc[t], in_=h3tm[:]),
                    wait=[t_h3, t_wc, t_w2], inc="GH3", incby=16)
                h3tm_free = [t3lo, t_w3]
                h3_last = t_w3
                gate_free = t_g13
                h1_free = toks_q[1]
                h3p_free = [tok_g, t_w1, t_w2]

        bar = p.barrier_toks()
        for e in P.ENGS:
            p.waitonly(e, bar)
        bufs["pbfree"] = [None] * 8
        bufs["slot_free"] = {}

        s2s = contextlib.ExitStack()
        with s2s:
            a2 = lambda n, sh, dt=F32: s2s.enter_context(nc.sbuf_tensor(n, sh, dt))
            h3shi = a2("h3shi", [128, cfg.sct * D], BF16)
            h3slo = a2("h3slo", [128, cfg.sct * D], BF16)
            acc = a2("ffn_acc", [128, cfg.sct * D])
            hacthi = a2("hacthi", [128, ILN * cfg.sct * 128], BF16)
            hactlo = a2("hactlo", [128, ILN * cfg.sct * 128], BF16)
            uphi = a2("uphi", [128, NCHUNK * IG], BF16)
            uplo = a2("uplo", [128, NCHUNK * IG], BF16)
            dnhi = a2("dnhi", [128, ILN * D], BF16)
            dnlo = a2("dnlo", [128, ILN * D], BF16)
            fstage = a2("fstage", [128, 2048])
            tfull = a2("tfull", [128, 512])
            sbup_sb = a2("sbup_sb", [128, II // 128])
            upb_sb = a2("upb_sb", [128, II // 128])
            wt_t = a2("wt_t", [8, 128])
            CT = cfg.cap_tiles
            CAP = cfg.cap
            NW = EXP * cfg.sct
            lts_sb = a2("lts_sb", [128, 128], BF16)
            onec_sb = a2("onec_sb", [128, 1], BF16)
            lts8_sb = a2("lts8_sb", [NW, NW], BF16)
            maskF = a2("maskF", [128, NW], BF16)
            gposf = a2("gposf", [128, NW])
            idxf = a2("idxf", [128, NW])
            idx_i = s2s.enter_context(nc.sbuf_tensor("idx_i", [128, NW],
                                                      mybir.dt.int32))
            c1row = a2("c1row", [1, NW])
            ct_sb = a2("ct_sb", [NW, 1], BF16)
            basec = a2("basec", [NW, 1])
            baser = a2("baser", [1, NW])
            aug_sb = [a2("aug_sb0", [128, 1032])]
            c_sb = [a2(f"c_sb{i}", [128, 1032]) for i in range(2)]
            se_c = a2("se_c", [128, CT])
            eo_acc = a2("eo_acc", [128, CT * D])
            gsc_t = [a2("gsc_t0", [128, D])]
            ztile = fstage

            t_sbup = p.add("sync", lambda s: s.dma_start(
                out=sbup_sb[:], in_=sbup_h), inc="LDU", incby=16)
            t_lts = p.add("sync", lambda s: s.dma_start(
                out=lts_sb[:], in_=lts_h), wait=[t_sbup], inc="LDU", incby=16)
            t_lts = p.add("sync", lambda s: s.dma_start(
                out=onec_sb[:], in_=onec_h), wait=[t_lts], inc="LDU", incby=16)
            t_lts = p.add("sync", lambda s: s.dma_start(
                out=lts8_sb[:], in_=lts8_h), wait=[t_lts], inc="LDU", incby=16)
            t_zt = p.add("vector", lambda v: v.memset(ztile[:, 0:1032], 0.0),
                         inc="VE")
            t_zf = t_zt
            for e in range(EXP):
                for ct in range(CT):
                    t_zf = p.add("gpsimd", lambda g, e=e, ct=ct: g.dma_start(
                        out=compact_d[e][ct * 128:(ct + 1) * 128, :],
                        in_=ztile[:, 0:1032]), wait=[t_zt, t_zf],
                        inc="GZ", incby=16)
                t_zf = p.add("gpsimd", lambda g, e=e: g.dma_start(
                    out=compact_d[e][CAP:CAP + 1, :], in_=ztile[0:1, 0:1032]),
                    wait=[t_zt, t_zf], inc="GZ", incby=16)
                t_zf = p.add("gpsimd", lambda g, e=e: g.dma_start(
                    out=eo_d[e][CAP:CAP + 1, :], in_=ztile[0:1, 0:D]),
                    wait=[t_zt, t_zf], inc="GZ", incby=16)

            fstage_free = None
            wt_free = [None]
            up_free = None
            dn_free = None
            hact_free = None
            tfull_free = None
            acc_free = None
            h3s_free = None
            upb_free = None
            x3_free = None
            n_tg = max(1, cfg.sct // 4)
            tpg = min(4, cfg.sct)

            rt_free = None
            cpt_free = None
            fstage_free = t_zf
            c_free = [[], []]
            se_free = None
            eo_free = None
            cfm_free = None
            gat_free = None
            gsc_free = [None]
            nsc_eff = 0 if (B_STAGE is not None and B_STAGE < 4) else cfg.nsc
            for sc in range(nsc_eff):
                t0 = sc * cfg.sct
                t_l1 = p.add("sync", lambda s, t0=t0: s.dma_start(
                    out=h3shi[:].rearrange("p (a c) -> p a c", a=cfg.sct),
                    in_=h3hi_sc[t0:t0 + cfg.sct].transpose([1, 0, 2])),
                    wait=[h3_last, h3s_free], inc="H3S", incby=16)
                t_h3s = p.add("sync", lambda s, t0=t0: s.dma_start(
                    out=h3slo[:].rearrange("p (a c) -> p a c", a=cfg.sct),
                    in_=h3lo_sc[t0:t0 + cfg.sct].transpose([1, 0, 2])),
                    wait=[h3_last, h3s_free, t_l1], inc="H3S", incby=16)
                t_acc0 = p.add("vector", lambda v: v.memset(acc[:], 0.0),
                               wait=[acc_free], inc="VE")
                acc_tok = [[t_acc0, t_acc0] for _ in range(cfg.sct)]

                def h3s_mov(buf, j, tg):
                    a = buf[:].rearrange("p (a j c) -> p a j c", a=cfg.sct,
                                         j=NCHUNK)
                    return a[:, tg * tpg:(tg + 1) * tpg, j, :]

                for g in range(IGN):
                    src_u = swup_h[:, g * IG:(g + 1) * IG]
                    t_u = emit_block_load(p, fstage, src_u, NCHUNK, IG,
                                          uphi, uplo, fstage_free, up_free)
                    fstage_free = t_u
                    hact_last = None
                    up_mm_last = None
                    for tg in range(n_tg):
                        for il in range(ILN):
                            bi = (tg * ILN + il) % 2
                            bank = bufs["pb"][bi]
                            prs = pairs3(
                                lambda j, il=il: uphi[
                                    :, j * IG + il * 128: j * IG + il * 128 + 128],
                                lambda j, il=il: uplo[
                                    :, j * IG + il * 128: j * IG + il * 128 + 128],
                                lambda j, tg=tg: h3s_mov(h3shi, j, tg),
                                lambda j, tg=tg: h3s_mov(h3slo, j, tg),
                                range(NCHUNK))
                            tok_up = mm_group(
                                p, bank[:, 0:tpg * 128], prs,
                                [t_u, t_h3s, bufs["pbfree"][bi]])
                            up_mm_last = tok_up
                            t_gl = p.add("scalar", lambda s, bank=bank, il=il,
                                         g=g, tpg=tpg: s.activation(
                                tfull[:, 0:tpg * 128], bank[:, 0:tpg * 128],
                                AF.Gelu,
                                bias=sbup_sb[:, g * ILN + il:g * ILN + il + 1]),
                                wait=[tok_up, tfull_free, t_sbup], inc="AC")
                            bufs["pbfree"][bi] = t_gl
                            hh = hacthi[:].rearrange(
                                "p (a il c) -> p a il c", a=cfg.sct, il=ILN)
                            hl = hactlo[:].rearrange(
                                "p (a il c) -> p a il c", a=cfg.sct, il=ILN)
                            hdst_hi = hh[:, tg * tpg:(tg + 1) * tpg, il, :]
                            hdst_lo = hl[:, tg * tpg:(tg + 1) * tpg, il, :]
                            t_hh = p.add("vector", lambda v, hdst_hi=hdst_hi,
                                         tpg=tpg: v.tensor_copy(
                                hdst_hi, tfull[:, 0:tpg * 128].rearrange(
                                    "p (a c) -> p a c", a=tpg)),
                                wait=[t_gl, hact_free], inc="VE")
                            t_hl = p.add("vector", lambda v, hdst_hi=hdst_hi,
                                         hdst_lo=hdst_lo, tpg=tpg:
                                         v.tensor_tensor(
                                hdst_lo, tfull[:, 0:tpg * 128].rearrange(
                                    "p (a c) -> p a c", a=tpg),
                                hdst_hi, ALU.subtract),
                                wait=[t_hh], inc="VE")
                            tfull_free = t_hl
                            hact_last = t_hl
                    up_free = up_mm_last
                    src_d = swdown_h[g * IG:(g + 1) * IG, :]
                    t_dn = emit_block_load(p, fstage, src_d, ILN, D,
                                           dnhi, dnlo, fstage_free, dn_free)
                    fstage_free = t_dn
                    dn_mm_last = None
                    for tl in range(cfg.sct):
                        for ng in range(2):
                            bi = 2 + (tl % 2) * 2 + ng
                            bank = bufs["pb"][bi]
                            prs = pairs3(
                                lambda j, tl=tl: hacthi[
                                    :, (tl * ILN + j) * 128:
                                    (tl * ILN + j) * 128 + 128],
                                lambda j, tl=tl: hactlo[
                                    :, (tl * ILN + j) * 128:
                                    (tl * ILN + j) * 128 + 128],
                                lambda j, ng=ng: dnhi[
                                    :, j * D + ng * 512: j * D + ng * 512 + 512],
                                lambda j, ng=ng: dnlo[
                                    :, j * D + ng * 512: j * D + ng * 512 + 512],
                                range(ILN))
                            tok_dn = mm_group(
                                p, bank[:], prs,
                                [t_dn, hact_last, bufs["pbfree"][bi]])
                            dn_mm_last = tok_dn
                            asl = acc[:, tl * D + ng * 512:
                                      tl * D + ng * 512 + 512]
                            t_ae = p.add("vector", lambda v, asl=asl, bank=bank:
                                         v.scalar_tensor_tensor(
                                asl, bank[:], 0.1, asl, ALU.mult, ALU.add),
                                wait=[tok_dn, acc_tok[tl][ng]], inc="VE")
                            acc_tok[tl][ng] = t_ae
                            bufs["pbfree"][bi] = t_ae
                    dn_free = dn_mm_last
                    hact_free = dn_mm_last
                cfm_free = up_free

                wv = w_all[:, t0 * EXP:(t0 + cfg.sct) * EXP].rearrange(
                    "p (t e) -> p t e", e=EXP).transpose([0, 2, 1])
                t_m = p.add("vector", lambda v, wv=wv: v.tensor_single_scalar(
                    maskF[:].rearrange("p (e t) -> p e t", e=EXP), wv, 0.0,
                    ALU.is_gt), wait=[rt_free], inc="VE")
                t_p1 = p.add("tensor", lambda t: t.matmul(
                    bufs["pb"][6][:, 0:NW], lts_sb[:], maskF[:],
                    start=True, stop=True),
                    wait=[t_m, t_lts, bufs["pbfree"][6]], inc="MM")
                t_c1 = p.add("tensor", lambda t: t.matmul(
                    bufs["pb"][7][0:1, 0:NW], onec_sb[:], maskF[:],
                    start=True, stop=True),
                    wait=[t_m, t_lts, bufs["pbfree"][7]], inc="MM")
                t_ec1 = p.add("vector", lambda v: v.tensor_copy(
                    c1row[0:1, 0:NW], bufs["pb"][7][0:1, 0:NW]),
                    wait=[t_c1, rt_free], inc="VE")
                bufs["pbfree"][7] = t_ec1
                t_ct = p.add("tensor", lambda t: t.transpose(
                    bufs["pb"][7][0:NW, 0:1], c1row[0:1, 0:NW],
                    bufs["ident"][0:1, 0:1]),
                    wait=[t_ec1, bufs["pbfree"][7]], inc="MM")
                t_ect = p.add("vector", lambda v: v.tensor_copy(
                    ct_sb[:], bufs["pb"][7][0:NW, 0:1]),
                    wait=[t_ct, rt_free], inc="VE")
                bufs["pbfree"][7] = t_ect
                t_b = p.add("tensor", lambda t: t.matmul(
                    bufs["pb"][7][0:NW, 0:1], lts8_sb[:], ct_sb[:],
                    start=True, stop=True),
                    wait=[t_ect, t_lts, bufs["pbfree"][7]], inc="MM")
                t_eb = p.add("vector", lambda v: v.tensor_copy(
                    basec[:], bufs["pb"][7][0:NW, 0:1]),
                    wait=[t_b, rt_free], inc="VE")
                bufs["pbfree"][7] = t_eb
                t_bt = p.add("tensor", lambda t: t.transpose(
                    bufs["pb"][7][0:1, 0:NW], basec[:],
                    bufs["ident"][0:NW, 0:NW]),
                    wait=[t_eb, bufs["pbfree"][7]], inc="MM")
                t_ebr = p.add("vector", lambda v: v.tensor_copy(
                    baser[0:1, 0:NW], bufs["pb"][7][0:1, 0:NW]),
                    wait=[t_bt, rt_free], inc="VE")
                bufs["pbfree"][7] = t_ebr
                t_bb = p.add("tensor", lambda t: t.matmul(
                    bufs["pb"][7][:, 0:NW], bufs["ones1"][:],
                    baser[0:1, 0:NW], start=True, stop=True),
                    wait=[t_ebr, bufs["tok_ones"], bufs["pbfree"][7]], inc="MM")
                t_gp0 = p.add("vector", lambda v: v.tensor_copy(
                    gposf[:], bufs["pb"][6][:, 0:NW]),
                    wait=[t_p1, rt_free], inc="VE")
                bufs["pbfree"][6] = t_gp0
                t_gp = p.add("vector", lambda v: v.tensor_add(
                    gposf[:], gposf[:], bufs["pb"][7][:, 0:NW]),
                    wait=[t_gp0, t_bb], inc="VE")
                bufs["pbfree"][7] = t_gp
                t_i1 = p.add("vector", lambda v: v.tensor_scalar_add(
                    idxf[:], gposf[:], float(-CAP)), wait=[t_gp, rt_free],
                    inc="VE")
                t_i2 = p.add("vector", lambda v: v.tensor_tensor(
                    idxf[:], idxf[:], maskF[:], ALU.mult), wait=[t_i1, t_m],
                    inc="VE")
                t_i3 = p.add("vector", lambda v: v.tensor_scalar_add(
                    idxf[:], idxf[:], float(CAP)), wait=[t_i2], inc="VE")
                t_idx = p.add("vector", lambda v: v.tensor_copy(
                    idx_i[:], idxf[:]), wait=[t_i3, rt_free], inc="VE")

                aug_free = [[]]
                sc_toks = []
                for tl in range(cfg.sct):
                    par = 0
                    t_al = p.add("sync", lambda s, t0=t0, tl=tl, par=par:
                                 s.dma_start(out=aug_sb[par][:],
                                             in_=h3aug_sc[t0 + tl]),
                                 wait=[h3_last] + aug_free[par],
                                 inc=f"AUG{par}", incby=16)
                    aug_free[par] = []
                    for e in range(EXP):
                        col = e * cfg.sct + tl
                        t_sct = p.add("gpsimd", lambda g, e=e, col=col, par=par:
                                      g.indirect_dma_start(
                            out=compact_d[e],
                            out_offset=IndirectOffsetOnAxis(
                                idx_i[:, col:col + 1], 0),
                            in_=aug_sb[par][:], in_offset=None),
                            wait=[t_al, t_idx, t_zf, cpt_free],
                            inc="GSC", incby=16)
                        aug_free[par].append(t_sct)
                        sc_toks.append(t_sct)
                t_scall = sc_toks[-1]

                eo_toks = []
                for e in range(EXP):
                    t_upb = p.add("sync", lambda s, e=e: s.dma_start(
                        out=upb_sb[:], in_=upb_h[e]),
                        wait=[upb_free], inc="UPB", incby=16)
                    tf_hi = tf_lo = None
                    t_se = None
                    for ct in range(CT):
                        cpar = ct % 2
                        tok = p.add("sync", lambda s, e=e, ct=ct, cpar=cpar:
                                    s.dma_start(
                            out=c_sb[cpar][:],
                            in_=compact_d[e][ct * 128:(ct + 1) * 128, :]),
                            wait=[t_scall] + c_free[cpar],
                            inc=f"CPL{cpar}", incby=16)
                        t_se = p.add("vector", lambda v, ct=ct, e=e, cpar=cpar:
                                     v.tensor_scalar_mul(
                            se_c[:, ct:ct + 1],
                            c_sb[cpar][:, D + e:D + e + 1],
                            0.1), wait=[tok, se_free, t_se], inc="VE")
                        tf_hi, tf_lo = emit_transpose_split(
                            p, bufs,
                            lambda j, cpar=cpar: c_sb[cpar][:, j * 128:
                                                          j * 128 + 128],
                            h3shi[:, ct * D:(ct + 1) * D],
                            h3slo[:, ct * D:(ct + 1) * D],
                            tok, dst_free=[cfm_free])
                        c_free[cpar] = [tf_lo, t_se]
                    t_ea0 = p.add("vector", lambda v: v.memset(eo_acc[:], 0.0),
                                  wait=[eo_free], inc="VE")
                    ea_tok = [[t_ea0, t_ea0] for _ in range(CT)]
                    cfm_hi3 = h3shi[:, 0:CT * D].rearrange(
                        "p (ct j c) -> p ct j c", ct=CT, j=NCHUNK)
                    cfm_lo3 = h3slo[:, 0:CT * D].rearrange(
                        "p (ct j c) -> p ct j c", ct=CT, j=NCHUNK)
                    up_last = None
                    for g in range(IGN):
                        t_u = emit_block_load(
                            p, fstage, upw_h[e][:, g * IG:(g + 1) * IG],
                            NCHUNK, IG, uphi, uplo, fstage_free, up_free)
                        fstage_free = t_u
                        for il in range(ILN):
                            bi = il % 2
                            bank = bufs["pb"][bi]
                            prs = pairs3(
                                lambda j, il=il: uphi[
                                    :, j * IG + il * 128:
                                    j * IG + il * 128 + 128],
                                lambda j, il=il: uplo[
                                    :, j * IG + il * 128:
                                    j * IG + il * 128 + 128],
                                lambda j: cfm_hi3[:, :, j, :],
                                lambda j: cfm_lo3[:, :, j, :],
                                range(NCHUNK))
                            tok_up = mm_group(
                                p, bank[:, 0:CT * 128], prs,
                                [t_u, tf_hi, tf_lo, bufs["pbfree"][bi]])
                            up_last = tok_up
                            t_gl = p.add("scalar", lambda s, bank=bank, il=il,
                                         g=g: s.activation(
                                tfull[:, 0:CT * 128], bank[:, 0:CT * 128],
                                AF.Gelu,
                                bias=upb_sb[:, g * ILN + il:g * ILN + il + 1]),
                                wait=[tok_up, tfull_free, t_upb], inc="AC")
                            bufs["pbfree"][bi] = t_gl
                            last_gelu = t_gl
                            t_hh = p.add("vector", lambda v, il=il:
                                         v.tensor_copy(
                                hacthi[:, il * CT * 128:(il + 1) * CT * 128],
                                tfull[:, 0:CT * 128]),
                                wait=[t_gl, hact_free], inc="VE")
                            t_hl = p.add("vector", lambda v, il=il:
                                         v.tensor_tensor(
                                hactlo[:, il * CT * 128:(il + 1) * CT * 128],
                                tfull[:, 0:CT * 128],
                                hacthi[:, il * CT * 128:(il + 1) * CT * 128],
                                ALU.subtract), wait=[t_hh], inc="VE")
                            tfull_free = t_hl
                            hact_last = t_hl
                        up_free = up_last
                        t_dn = emit_block_load(
                            p, fstage, downw_h[e][g * IG:(g + 1) * IG, :],
                            ILN, D, dnhi, dnlo, fstage_free, dn_free)
                        fstage_free = t_dn
                        dn_last = None
                        for ct in range(CT):
                            for ng in range(2):
                                bi = 2 + (ct % 2) * 2 + ng
                                bank = bufs["pb"][bi]
                                prs = pairs3(
                                    lambda j, ct=ct: hacthi[
                                        :, (j * CT + ct) * 128:
                                        (j * CT + ct) * 128 + 128],
                                    lambda j, ct=ct: hactlo[
                                        :, (j * CT + ct) * 128:
                                        (j * CT + ct) * 128 + 128],
                                    lambda j, ng=ng: dnhi[
                                        :, j * D + ng * 512:
                                        j * D + ng * 512 + 512],
                                    lambda j, ng=ng: dnlo[
                                        :, j * D + ng * 512:
                                        j * D + ng * 512 + 512],
                                    range(ILN))
                                tok_dn = mm_group(
                                    p, bank[:], prs,
                                    [t_dn, hact_last, bufs["pbfree"][bi]])
                                dn_last = tok_dn
                                esl = eo_acc[:, ct * D + ng * 512:
                                             ct * D + ng * 512 + 512]
                                t_ae = p.add("vector", lambda v, esl=esl,
                                             bank=bank: v.tensor_add(
                                    esl, esl, bank[:]),
                                    wait=[tok_dn, ea_tok[ct][ng]], inc="VE")
                                ea_tok[ct][ng] = t_ae
                                bufs["pbfree"][bi] = t_ae
                        dn_free = dn_last
                        hact_free = dn_last
                    upb_free = last_gelu
                    eo_w = []
                    for ct in range(CT):
                        t_s2 = p.add("vector", lambda v, ct=ct:
                                     v.tensor_scalar(
                            eo_acc[:, ct * D:(ct + 1) * D],
                            eo_acc[:, ct * D:(ct + 1) * D],
                            se_c[:, ct:ct + 1], None, ALU.mult),
                            wait=[ea_tok[ct][0], ea_tok[ct][1], t_se],
                            inc="VE")
                        t_wr = p.add("gpsimd", lambda g, e=e, ct=ct:
                                     g.dma_start(
                            out=eo_d[e][ct * 128:(ct + 1) * 128, :],
                            in_=eo_acc[:, ct * D:(ct + 1) * D]),
                            wait=[t_s2, t_zf, gat_free], inc="GEO", incby=16)
                        eo_w.append(t_wr)
                    eo_free = eo_w[-1]
                    eo_toks.extend(eo_w)
                    se_free = t_s2
                    cfm_free = up_last
                t_eoall = eo_toks[-1]

                gat_toks = []
                for tl in range(cfg.sct):
                    for e in range(EXP):
                        par = 0
                        col = e * cfg.sct + tl
                        t_gt = p.add("gpsimd", lambda g, e=e, col=col, par=par:
                                     g.indirect_dma_start(
                            out=gsc_t[par][:], out_offset=None,
                            in_=eo_d[e],
                            in_offset=IndirectOffsetOnAxis(
                                idx_i[:, col:col + 1], 0)),
                            wait=[t_eoall, gsc_free[par]],
                            inc=f"GG{par}", incby=16)
                        t_ad = p.add("vector", lambda v, tl=tl, par=par:
                                     v.tensor_add(
                            acc[:, tl * D:(tl + 1) * D],
                            acc[:, tl * D:(tl + 1) * D], gsc_t[par][:]),
                            wait=[t_gt, acc_tok[tl][0], acc_tok[tl][1]],
                            inc="VE")
                        gsc_free[par] = t_ad
                        acc_tok[tl][0] = t_ad
                        acc_tok[tl][1] = t_ad
                        gat_toks.append(t_ad)
                gat_free = gat_toks[-1]
                rt_free = gat_free
                cpt_free = tf_lo
                h3s_free = up_free

                for tl in range(cfg.sct):
                    t = t0 + tl
                    par = tl % 2
                    st = bufs["stats"][par]
                    t_tr = p.add("tensor", lambda tt, t=t: tt.transpose(
                        bufs["pb"][7][0:8, 0:128],
                        w_all[:, t * EXP:(t + 1) * EXP], bufs["ident"][:]),
                        wait=[gate_free, bufs["pbfree"][7]], inc="MM")
                    t_wt = p.add("vector", lambda v: v.tensor_copy(
                        wt_t[:], bufs["pb"][7][0:8, 0:128]),
                        wait=[t_tr, wt_free[0]], inc="VE")
                    bufs["pbfree"][7] = t_wt
                    t_wdb = None
                    for ng in range(2):
                        tok = p.add("tensor", lambda tt, ng=ng: tt.matmul(
                            bufs["pb"][6][:, 0:512],
                            wt_t[:],
                            downb_sb[:, ng * 512:(ng + 1) * 512],
                            start=True, stop=True),
                            wait=[t_wt, t_db, bufs["pbfree"][6]],
                            inc="MM")
                        asl = acc[:, tl * D + ng * 512:tl * D + ng * 512 + 512]
                        t_wdb = p.add("vector", lambda v, asl=asl:
                                      v.scalar_tensor_tensor(
                            asl, bufs["pb"][6][:, 0:512],
                            0.1, asl, ALU.mult, ALU.add),
                            wait=[tok, acc_tok[tl][ng]], inc="VE")
                        acc_tok[tl][ng] = t_wdb
                        bufs["pbfree"][6] = t_wdb
                    wt_free[0] = t_wdb
                    t_sb2 = None
                    for ng in range(2):
                        asl = acc[:, tl * D + ng * 512:tl * D + ng * 512 + 512]
                        t_sb2 = p.add("vector", lambda v, asl=asl, ng=ng:
                                      v.tensor_add(
                            asl, asl, sbd_rep[:, ng * 512:(ng + 1) * 512]),
                            wait=[t_sbd, acc_tok[tl][ng]], inc="VE")
                        acc_tok[tl][ng] = t_sb2
                    accsl = acc[:, tl * D:(tl + 1) * D]
                    t_mx = p.add("vector", lambda v, accsl=accsl, st=st:
                                 v.reduce_max(st[:, 0:1], accsl, axis=AX.X,
                                              apply_absolute_value=True),
                        wait=[acc_tok[tl][0], acc_tok[tl][1],
                              bufs["stats_free"][par]], inc="VE")
                    t_mx = p.add("vector", lambda v, st=st: v.tensor_scalar_add(
                        st[:, 0:1], st[:, 0:1], 1e-6), wait=[t_mx], inc="VE")
                    t_mx = p.add("vector", lambda v, st=st: v.reciprocal(
                        st[:, 1:2], st[:, 0:1]), wait=[t_mx], inc="VE")
                    bufs["stats_free"][par] = t_mx
                    xb = bufs["xbuf"][par]
                    t_xr = p.add("sync", lambda s, t=t, xb=xb: s.dma_start(
                        out=xb[:], in_=x2_sc[t * 128:(t + 1) * 128, :]),
                        wait=[x2w_toks[t], bufs["xbuf_free"][par]],
                        inc=f"XB{par}", incby=16)
                    t_f1 = p.add("vector", lambda v, accsl=accsl, st=st:
                                 v.tensor_scalar(x3tm[:], accsl, st[:, 1:2],
                                                 None, ALU.mult),
                        wait=[t_mx, x3_free], inc="VE")
                    t_f2 = p.add("vector", lambda v: v.tensor_tensor(
                        x3tm[:], x3tm[:], gb_rep[:], ALU.mult),
                        wait=[t_f1], inc="VE")
                    t_f3 = p.add("vector", lambda v, xb=xb: v.tensor_add(
                        x3tm[:], x3tm[:], xb[:]), wait=[t_f2, t_xr], inc="VE")
                    bufs["xbuf_free"][par] = t_f3
                    y_tok = p.add("gpsimd", lambda g, t=t: g.dma_start(
                        out=y_h[t * 128:(t + 1) * 128, :], in_=x3tm[:]),
                        wait=[t_f3], inc="GY", incby=16)
                    x3_free = y_tok
                acc_free = y_tok

            if B_STAGE is not None and B_STAGE < 4:
                bar2 = p.barrier_toks()
                t_z = p.add("vector", lambda v: v.memset(x3tm[:], 0.0),
                            wait=bar2, inc="VE")
                p.add("gpsimd", lambda g: g.dma_start(
                    out=y_h[0:128, :], in_=x3tm[:]), wait=[t_z],
                    inc="GY", incby=16)
            p.waitonly("gpsimd", [("GY", p.now("GY")), ("GX2", p.now("GX2")),
                                  ("GH3", p.now("GH3"))])


def build_B(cfg):
    nc = bass.Bass("TRN2", target_bir_lowering=False)
    p = P(nc)
    bufs = {}
    es = contextlib.ExitStack()
    with es:
        setup_common(nc, p, cfg, bufs, es)
        _emit_B(nc, p, bufs, cfg, es, sp_src=None)
        with nc.Block() as block:
            p.finalize(block)
    return nc


def build_fused(cfg):
    nc = bass.Bass("TRN2", target_bir_lowering=False)
    p = P(nc)
    bufs = {}
    es = contextlib.ExitStack()
    with es:
        setup_common(nc, p, cfg, bufs, es)
        sp_loc = nc.dram_tensor("sp_loc", [64, 2048], F32).ap()
        sp_sum = nc.dram_tensor("sp_sum", [64, 2048], F32).ap()
        with contextlib.ExitStack() as esa:
            stev, t_e = _emit_A(nc, p, bufs, cfg, esa)
            t_w = p.add("gpsimd", lambda g: g.dma_start(
                out=sp_loc, in_=stev[0:64, :]),
                wait=[t_e], inc="GSP", incby=16)
            rg = [list(range(cfg.n_cores))]
            t_cc = p.add("gpsimd", lambda g: g.collective_compute(
                "AllReduce", ALU.add, replica_groups=rg,
                ins=[sp_loc], outs=[sp_sum]),
                wait=[t_w], inc="CC")
            bar = p.barrier_toks()
            for e in P.ENGS:
                p.waitonly(e, bar)
        bufs["pbfree"] = [None] * 8
        bufs["slot_free"] = {}
        bufs["wstage_free"] = None
        _emit_B(nc, p, bufs, cfg, es, sp_src=(sp_sum, t_cc))
        with nc.Block() as block:
            p.finalize(block)
    return nc



def _rep(v):
    v = np.ascontiguousarray(np.asarray(v, np.float32).reshape(-1))
    return np.ascontiguousarray(np.broadcast_to(v[None, :], (128, v.shape[0])))


def kernel(**inputs):
    return _run(inputs, Cfg(8, 2048), run_bass_kernel_spmd)


def _run(inputs, cfg, run_fn):
    f = lambda k: np.ascontiguousarray(np.asarray(inputs[k], np.float32))
    x = f("L0_attn_x").reshape(cfg.n_cores * cfg.t_core, D)
    wq_t = np.ascontiguousarray(f("L0_attn_Wq").T)
    wk = f("L0_attn_Wk")
    wv = f("L0_attn_Wv")
    wo_t = np.ascontiguousarray(f("L0_attn_Wo").T)
    bk = f("L0_attn_bk")
    bv = f("L0_attn_bv")
    z128 = np.zeros((128, D), np.float32)
    repsA = np.ascontiguousarray(np.stack(
        [_rep(bk[0]), _rep(bk[1]), _rep(bv[0]), _rep(bv[1]), z128, z128]))
    repsB = np.ascontiguousarray(np.stack(
        [_rep(f("L0_attn_bq")), _rep(f("L0_attn_bo")), _rep(f("L0_attn_ln_g")),
         _rep(f("L0_attn_ln_b")), _rep(f("L0_ffn_sb_down"))]))
    c_col = np.ascontiguousarray(f("L0_c").reshape(NCHUNK, 128).T)
    wmod = f("L0_wmod")
    ident = np.eye(128, dtype=np.float32)
    ones1 = np.ones((1, 128), np.float32)
    gbias_rep = _rep(f("L0_ffn_bias"))
    upb_fm = np.ascontiguousarray(
        f("L0_ffn_up_b").reshape(EXP, II // 128, 128).transpose(0, 2, 1))
    sbup_fm = np.ascontiguousarray(f("L0_ffn_sb_up").reshape(II // 128, 128).T)
    import ml_dtypes
    bf = ml_dtypes.bfloat16
    lts = np.triu(np.ones((128, 128), np.float32), 1).astype(bf)
    onec = np.ones((128, 1), np.float32).astype(bf)
    nw = EXP * cfg.sct
    l8 = np.zeros((nw, nw), np.float32)
    for e in range(EXP):
        l8[e * cfg.sct:(e + 1) * cfg.sct, e * cfg.sct:(e + 1) * cfg.sct] = \
            np.triu(np.ones((cfg.sct, cfg.sct)), 1)
    lts8 = l8.astype(bf)

    ncF = build_fused(cfg)
    in_maps = []
    for ci in range(cfg.n_cores):
        in_maps.append({
            "xs": x[ci * cfg.t_core:(ci + 1) * cfg.t_core],
            "wk0t": np.ascontiguousarray(wk[0].T),
            "wk1t": np.ascontiguousarray(wk[1].T),
            "wv0t": np.ascontiguousarray(wv[0].T),
            "wv1t": np.ascontiguousarray(wv[1].T),
            "repsA": repsA,
            "wqt": wq_t, "wot": wo_t, "repsB": repsB,
            "gatew": f("L0_ffn_gate_w"),
            "gbias_rep": gbias_rep, "upw": f("L0_ffn_up_w"),
            "downw": f("L0_ffn_down_w"), "swup": f("L0_ffn_sw_up"),
            "swdown": f("L0_ffn_sw_down"), "upb_fm": upb_fm,
            "sbup_fm": sbup_fm, "downb": f("L0_ffn_down_b"),
            "c_col": c_col, "wmod": wmod, "ident": ident, "ones1": ones1,
            "lts": lts, "onec": onec, "lts8": lts8,
        })
    res = run_fn(ncF, in_maps, core_ids=list(range(cfg.n_cores)))
    y = np.concatenate([np.asarray(r["y"]) for r in res.results], axis=0)
    return y.reshape(1, cfg.n_cores * cfg.t_core, D).astype(np.float32)


# version 08 (final)
# speedup vs baseline: 1.2322x; 1.0165x over previous
import contextlib
import numpy as np
import concourse.bass as bass
import concourse.mybir as mybir
from concourse.bass_core_rust import IndirectOffsetOnAxis
from concourse.bass_utils import run_bass_kernel_spmd

F32 = mybir.dt.float32
BF16 = mybir.dt.bfloat16
AF = mybir.ActivationFunctionType
ALU = mybir.AluOpType
AX = mybir.AxisListType

D = 1024
NCHUNK = 8
H = 16
HD = 64
KK = 2
EXP = 8
II = 4096
IG = 256
IGN = II // IG
ILN = IG // 128


B_STAGE = None
B_SUB = None


class Cfg:
    def __init__(self, n_cores=8, t_core=2048):
        self.n_cores = n_cores
        self.t_core = t_core
        self.nt = t_core // 128
        self.sct = min(8, self.nt)
        self.nsc = self.nt // self.sct
        self.scale = float(n_cores * t_core) ** -0.5
        self.cap_tiles = max(1, (self.sct * 128 * 2 * 2) // (EXP * 128))
        self.cap = self.cap_tiles * 128


class P:
    ENGS = ("sync", "tensor", "vector", "scalar", "gpsimd")

    def __init__(self, nc):
        self.nc = nc
        self.q = {e: [] for e in self.ENGS}
        self.semh = {}
        self.cnt = {}

    def sem(self, name):
        if name not in self.semh:
            self.semh[name] = self.nc.alloc_semaphore(name="sem_" + name)
            self.cnt[name] = 0
        return self.semh[name]

    def add(self, eng, fn, wait=(), inc=None, incby=1):
        seen = {}
        for t in wait:
            if t is None:
                continue
            nm, v = t
            if v > 0 and seen.get(nm, -1) < v:
                seen[nm] = v
        wl = [(self.sem(nm), v) for nm, v in seen.items()]
        tok = None
        semh = None
        if inc is not None:
            semh = self.sem(inc)
            self.cnt[inc] += incby
            tok = (inc, self.cnt[inc])

        def run(engine, wl=wl, fn=fn, semh=semh, incby=incby):
            for (sh, v) in wl:
                engine.wait_ge(sh, v)
            inst = fn(engine)
            if semh is not None and inst is not None:
                inst.then_inc(semh, incby)

        self.q[eng].append(run)
        return tok

    def waitonly(self, eng, wait):
        self.add(eng, lambda e: None, wait=wait)

    def now(self, name):
        return self.cnt.get(name, 0)

    def barrier_toks(self):
        return [(n, self.cnt[n]) for n in self.cnt]

    def finalize(self, block):
        for e in self.ENGS:
            fns = self.q[e]
            if fns:
                getattr(block, e)(lambda eng, fns=fns: [f(eng) for f in fns])


def _ap(x):
    return x[:] if not isinstance(x, bass.AP) else x


def pairs3(lhs_hi, lhs_lo, rhs_hi, rhs_lo, idx):
    out = []
    for j in idx:
        out += [(lhs_hi(j), rhs_hi(j)), (lhs_hi(j), rhs_lo(j)), (lhs_lo(j), rhs_hi(j))]
    return out


def mm_group(p, out_ap, pairs, wait_first, inc=True):
    n = len(pairs)
    tok = None
    for i, (la, ra) in enumerate(pairs):
        first, last = i == 0, i == n - 1
        tok = p.add("tensor",
                    lambda t, la=la, ra=ra, first=first, last=last:
                    t.matmul(out_ap, la, ra, start=first, stop=last),
                    wait=wait_first if first else (),
                    inc="MM" if (last and inc) else None)
    return tok



def emit_mod_phase(p, bufs, n_groups, want):
    csilu, mrow, wmst = bufs["csilu"], bufs["modrow"], bufs["wmstage"]
    pb = bufs["pb"]
    wmod = bufs["wmod_hbm"]
    t_sil = p.add("scalar", lambda s: s.activation(csilu[:], bufs["ccol"][:], AF.Silu),
                  wait=[bufs["tok_consts"]], inc="AC")
    t_zero = p.add("vector", lambda v: v.memset(mrow[0:1, 0:n_groups * 512], 0.0), inc="VE")
    acc_tok = t_zero
    stage_free = None
    for j in range(NCHUNK):
        t_dma = p.add("sync", lambda s, j=j: s.dma_start(
            out=wmst[:, 0:n_groups * 512],
            in_=wmod[j * 128:(j + 1) * 128, 0:n_groups * 512]),
            wait=[stage_free], inc="WMS", incby=16)
        for g in range(n_groups):
            bank = pb[6 + g % 2]
            t_mm = p.add("tensor", lambda t, j=j, g=g, bank=bank: t.matmul(
                bank[0:1, 0:512], csilu[:, j:j + 1],
                wmst[:, g * 512:(g + 1) * 512], start=True, stop=True),
                wait=[t_dma, t_sil, acc_tok, bufs["pbfree"][6 + g % 2]], inc="MM")
            bufs["pbfree"][6 + g % 2] = None
            acc_tok = p.add("vector", lambda v, g=g, bank=bank: v.tensor_add(
                mrow[0:1, g * 512:(g + 1) * 512], mrow[0:1, g * 512:(g + 1) * 512],
                bank[0:1, 0:512]), wait=[t_mm, acc_tok], inc="VE")
            bufs["pbfree"][6 + g % 2] = acc_tok
        stage_free = acc_tok
    last = acc_tok
    for slot, kind, rep in want:
        rep = _ap(rep)
        src = mrow[0:1, slot * 1024:(slot + 1) * 1024]
        dst = bufs["vrows"][0:1, slot * 1024:(slot + 1) * 1024]
        if kind == "plain":
            t_row = p.add("scalar", lambda s, src=src, dst=dst: s.copy(dst, src),
                          wait=[acc_tok], inc="AC")
        elif kind == "plus1":
            t_row = p.add("scalar", lambda s, src=src, dst=dst: s.add(dst, src, 1.0),
                          wait=[acc_tok], inc="AC")
        else:
            ab, sg = bufs["grow1"], bufs["grow2"]
            t1 = p.add("scalar", lambda s, src=src, ab=ab: s.activation(
                ab[0:1, :], src, AF.Abs), wait=[acc_tok, bufs["grow_free"]], inc="AC")
            t2 = p.add("vector", lambda v, ab=ab: v.tensor_scalar_max(
                ab[0:1, :], ab[0:1, :], 0.25), wait=[t1], inc="VE")
            t3 = p.add("vector", lambda v, src=src, sg=sg: v.tensor_single_scalar(
                sg[0:1, :], src, 0.0, ALU.is_ge),
                wait=[acc_tok, bufs["grow_free"]], inc="VE")
            t4 = p.add("vector", lambda v, sg=sg: v.tensor_scalar(
                sg[0:1, :], sg[0:1, :], 2.0, -1.0, ALU.mult, ALU.add),
                wait=[t3], inc="VE")
            t_row = p.add("vector", lambda v, ab=ab, sg=sg, dst=dst: v.tensor_tensor(
                dst, ab[0:1, :], sg[0:1, :], ALU.mult), wait=[t2, t4], inc="VE")
            bufs["grow_free"] = t_row
        for n in range(2):
            bi = 6 + n
            t_mm = p.add("tensor", lambda t, dst=dst, n=n, bi=bi: t.matmul(
                bufs["pb"][bi][:, 0:512], bufs["ones1"][:],
                dst[0:1, n * 512:(n + 1) * 512], start=True, stop=True),
                wait=[t_row, bufs["tok_ones"], bufs["pbfree"][bi]], inc="MM")
            last = p.add("vector", lambda v, rep=rep, n=n, bi=bi: v.tensor_copy(
                rep[:, n * 512:(n + 1) * 512], bufs["pb"][bi][:, 0:512]),
                wait=[t_mm], inc="VE")
            bufs["pbfree"][bi] = last
    return last


def emit_ln(p, bufs, par, in_ap, out_ap, mul_rep, add_rep, tok_in, extra=()):
    st, xc, sq = bufs["stats"][par], bufs["scr"][par], bufs["scr2"][par]
    mul_rep = None if mul_rep is None else _ap(mul_rep)
    add_rep = None if add_rep is None else _ap(add_rep)
    w = [tok_in] + list(extra)
    t0 = p.add("vector", lambda v: v.reduce_sum(st[:, 0:1], in_ap, axis=AX.X),
               wait=w + [bufs["stats_free"][par]], inc="VE")
    t1 = p.add("vector", lambda v: v.tensor_scalar_mul(st[:, 1:2], st[:, 0:1], 1.0 / D),
               wait=[t0], inc="VE")
    t2 = p.add("vector", lambda v: v.tensor_scalar(xc[:], in_ap, st[:, 1:2], None,
                                                   ALU.subtract),
               wait=[t1, bufs["scr_free"][par]], inc="VE")
    t3 = p.add("scalar", lambda s: s.activation(sq[:], xc[:], AF.Square),
               wait=[t2, bufs["scr2_free"][par]], inc="AC")
    t4 = p.add("vector", lambda v: v.reduce_sum(st[:, 2:3], sq[:], axis=AX.X),
               wait=[t3], inc="VE")
    t5 = p.add("vector", lambda v: v.tensor_scalar(st[:, 3:4], st[:, 2:3], 1.0 / D,
                                                   1e-5, ALU.mult, ALU.add),
               wait=[t4], inc="VE")
    t6 = p.add("scalar", lambda s: s.activation(st[:, 4:5], st[:, 3:4], AF.Sqrt),
               wait=[t5], inc="AC")
    t7 = p.add("vector", lambda v: v.reciprocal(st[:, 5:6], st[:, 4:5]),
               wait=[t6], inc="VE")
    if mul_rep is not None:
        t8 = p.add("vector", lambda v: v.scalar_tensor_tensor(
            out_ap, xc[:], st[:, 5:6], mul_rep, ALU.mult, ALU.mult),
            wait=[t7], inc="VE")
    else:
        t8 = p.add("vector", lambda v: v.tensor_scalar(out_ap, xc[:], st[:, 5:6],
                                                       None, ALU.mult),
                   wait=[t7], inc="VE")
    if add_rep is not None:
        t8 = p.add("vector", lambda v: v.tensor_add(out_ap, out_ap, add_rep),
                   wait=[t8], inc="VE")
    bufs["stats_free"][par] = t8
    bufs["scr_free"][par] = t8
    bufs["scr2_free"][par] = t4
    return t8


def emit_transpose_split(p, bufs, src_fn, hi, lo, tok_in, dst_free=None):
    pb = bufs["pb"]
    if dst_free is None:
        dst_free = []
    elif not isinstance(dst_free, list):
        dst_free = [dst_free]
    hi_tok = lo_tok = None
    for j in range(NCHUNK):
        bi = j % 2
        slot = pb[bi][:, 0:128]
        key = ("tr", bi)
        t_tr = p.add("tensor", lambda t, slot=slot, j=j: t.transpose(
            slot, src_fn(j), bufs["ident"][:]),
            wait=[tok_in, bufs["slot_free"].get(key), bufs["tok_ident"]], inc="MM")
        t_hi = p.add("scalar", lambda s, slot=slot, j=j: s.copy(
            hi[:, j * 128:(j + 1) * 128], slot),
            wait=[t_tr] + dst_free, inc="AC")
        t_lo = p.add("vector", lambda v, slot=slot, j=j: v.tensor_tensor(
            lo[:, j * 128:(j + 1) * 128], slot, hi[:, j * 128:(j + 1) * 128],
            ALU.subtract),
            wait=[t_hi] + dst_free, inc="VE")
        bufs["slot_free"][key] = t_lo
        hi_tok, lo_tok = t_hi, t_lo
    return hi_tok, lo_tok


def emit_split(p, src_ap, hi_ap, lo_ap, tok_in, extra=()):
    t_hi = p.add("vector", lambda v: v.tensor_copy(hi_ap, src_ap),
                 wait=[tok_in] + list(extra), inc="VE")
    t_lo = p.add("vector", lambda v: v.tensor_tensor(lo_ap, src_ap, hi_ap,
                                                     ALU.subtract),
                 wait=[t_hi], inc="VE")
    return t_lo


def emit_wload(p, bufs, hbm_ap, whi, wlo, fold_ident, prev_free, ncols=D,
               dst_free=None):
    ws = bufs["wstage"]
    tok = None
    for j in range(NCHUNK):
        t_dma = p.add("sync", lambda s, j=j: s.dma_start(
            out=ws[:, 0:ncols], in_=hbm_ap[j * 128:(j + 1) * 128, 0:ncols]),
            wait=[prev_free, bufs["wstage_free"]], inc="WST", incby=16)
        t_f = t_dma
        if fold_ident:
            t_f = p.add("vector", lambda v, j=j: v.tensor_add(
                ws[:, j * 128:(j + 1) * 128], ws[:, j * 128:(j + 1) * 128],
                bufs["ident"][:]), wait=[t_dma, bufs["tok_ident"]], inc="VE")
        t_hi = p.add("vector", lambda v, j=j: v.tensor_copy(
            whi[:, j * ncols:(j + 1) * ncols], ws[:, 0:ncols]),
            wait=[t_f, dst_free], inc="VE")
        tok = p.add("vector", lambda v, j=j: v.tensor_tensor(
            wlo[:, j * ncols:(j + 1) * ncols], ws[:, 0:ncols],
            whi[:, j * ncols:(j + 1) * ncols], ALU.subtract),
            wait=[t_hi, dst_free], inc="VE")
        bufs["wstage_free"] = tok
    return tok


def emit_block_load(p, bufs, src, nj, ncols, hi, lo, spar, dst_free):
    fst = bufs["fst"][spar]
    t_d = p.add("sync", lambda s: s.dma_start(
        out=fst[:].rearrange("p (j c) -> p j c", j=nj),
        in_=src.rearrange("(j p) c -> j p c", j=nj).transpose([1, 0, 2])),
        wait=[bufs["fst_free"][spar]], inc=f"FST{spar}", incby=16)
    t_hi = p.add("scalar", lambda sc: sc.copy(hi[:], fst[:]),
                 wait=[t_d, dst_free], inc="AC")
    t_lo = p.add("vector", lambda v: v.tensor_tensor(
        lo[:], fst[:], hi[:], ALU.subtract),
        wait=[t_hi, dst_free], inc="VE")
    bufs["fst_free"][spar] = t_lo
    return t_lo


def emit_kernel_norm(p, bufs, par, q_ap, out_ap, inv_scale, tok_in):
    t1, sq, hs = bufs["scr"][par], bufs["scr2"][par], bufs["hstats"][par]
    ta = p.add("scalar", lambda s: s.activation(t1[:], q_ap, AF.Silu),
               wait=[tok_in, bufs["scr_free"][par]], inc="AC")
    tb = p.add("vector", lambda v: v.tensor_scalar_add(t1[:], t1[:], 0.5),
               wait=[ta], inc="VE")
    tc = p.add("scalar", lambda s: s.activation(sq[:], t1[:], AF.Square),
               wait=[tb, bufs["scr2_free"][par]], inc="AC")
    td = p.add("vector", lambda v: v.reduce_sum(
        hs[:, 0:H], sq[:].rearrange("p (h d) -> p h d", h=H), axis=AX.X),
        wait=[tc, bufs["hstats_free"][par]], inc="VE")
    te = p.add("scalar", lambda s: s.activation(hs[:, H:2 * H], hs[:, 0:H], AF.Sqrt),
               wait=[td], inc="AC")
    tf = p.add("vector", lambda v: v.tensor_scalar_add(hs[:, H:2 * H],
                                                       hs[:, H:2 * H], 1e-6),
               wait=[te], inc="VE")
    tg = p.add("vector", lambda v: v.reciprocal(hs[:, 2 * H:3 * H], hs[:, H:2 * H]),
               wait=[tf], inc="VE")
    if inv_scale != 1.0:
        tg = p.add("vector", lambda v: v.tensor_scalar_mul(
            hs[:, 2 * H:3 * H], hs[:, 2 * H:3 * H], inv_scale), wait=[tg], inc="VE")
    bc = hs[:, 2 * H:3 * H].unsqueeze(2).broadcast_to((128, H, HD))
    th = p.add("vector", lambda v: v.tensor_tensor(
        out_ap.rearrange("p (h d) -> p h d", h=H),
        t1[:].rearrange("p (h d) -> p h d", h=H), bc, ALU.mult),
        wait=[tg], inc="VE")
    bufs["scr_free"][par] = th
    bufs["scr2_free"][par] = td
    bufs["hstats_free"][par] = th
    return th


def setup_common(nc, p, cfg, bufs, es):
    al = lambda n, sh, dt=F32: es.enter_context(nc.sbuf_tensor(n, sh, dt))
    bufs["pb"] = [es.enter_context(nc.psum_tensor(f"pb{i}", [128, 512], F32))
                  for i in range(8)]
    bufs["pbfree"] = [None] * 8
    bufs["slot_free"] = {}
    bufs["grow_free"] = None
    bufs["ident"] = al("ident_sb", [128, 128])
    bufs["ones1"] = al("ones1_sb", [1, 128])
    bufs["ccol"] = al("ccol_sb", [128, 8])
    bufs["csilu"] = al("csilu_sb", [128, 8])
    bufs["stats"] = [al(f"stats{i}", [128, 8]) for i in range(2)]
    bufs["hstats"] = [al(f"hstats{i}", [128, 3 * H]) for i in range(2)]
    bufs["scr"] = [al(f"scr{i}", [128, D]) for i in range(2)]
    bufs["scr2"] = [al(f"scr2_{i}", [128, D]) for i in range(2)]
    bufs["xbuf"] = [al(f"xbuf{i}", [128, D]) for i in range(2)]
    bufs["h1tm"] = [al(f"h1tm{i}", [128, D]) for i in range(2)]
    for nm in ("stats_free", "scr_free", "scr2_free", "hstats_free",
               "xbuf_free", "h1tm_free"):
        bufs[nm] = [None, None]
    ident_h = nc.dram_tensor("ident", [128, 128], F32, kind="ExternalInput").ap()
    ones1_h = nc.dram_tensor("ones1", [1, 128], F32, kind="ExternalInput").ap()
    ccol_h = nc.dram_tensor("c_col", [128, 8], F32, kind="ExternalInput").ap()
    bufs["x_hbm"] = nc.dram_tensor("xs", [cfg.t_core, D], F32,
                                   kind="ExternalInput").ap()
    bufs["wmod_hbm"] = nc.dram_tensor("wmod", [D, 6 * D], F32,
                                      kind="ExternalInput").ap()
    t_i = p.add("sync", lambda s: s.dma_start(out=bufs["ident"][:], in_=ident_h),
                inc="LDI", incby=16)
    t_o = p.add("sync", lambda s: s.dma_start(out=bufs["ones1"][:], in_=ones1_h),
                inc="LDO", incby=16)
    t_c = p.add("sync", lambda s: s.dma_start(
        out=bufs["ccol"][:], in_=ccol_h), inc="LDC", incby=16)
    bufs["tok_ident"] = t_i
    bufs["tok_ones"] = t_o
    bufs["tok_consts"] = t_c


def emit_x_load(p, bufs, t):
    par = t % 2
    xb = bufs["xbuf"][par]
    tok = p.add("sync", lambda s: s.dma_start(
        out=xb[:], in_=bufs["x_hbm"][t * 128:(t + 1) * 128, :]),
        wait=[bufs["xbuf_free"][par]], inc=f"XB{par}", incby=16)
    return xb, tok



def _emit_A(nc, p, bufs, cfg, es):
    if True:
        al = lambda n, sh, dt=F32: es.enter_context(nc.sbuf_tensor(n, sh, dt))
        wk_h = [nc.dram_tensor(f"wk{k}t", [D, D], F32, kind="ExternalInput").ap()
                for k in range(KK)]
        wv_h = [nc.dram_tensor(f"wv{k}t", [D, D], F32, kind="ExternalInput").ap()
                for k in range(KK)]
        repsA_h = nc.dram_tensor("repsA", [6, 128, D], F32,
                                 kind="ExternalInput").ap()

        reps = al("repsA_sb", [128, 6 * D])
        ca1_rep = reps[:, 4 * D:5 * D]
        sa_rep = reps[:, 5 * D:6 * D]
        h1hi = al("h1hi", [128, D], BF16)
        h1lo = al("h1lo", [128, D], BF16)
        kshi = al("kshi", [128, D], BF16)
        kslo = al("kslo", [128, D], BF16)
        vshi = al("vshi", [128, D], BF16)
        vslo = al("vslo", [128, D], BF16)
        kvtm = al("kvtm", [128, D])
        stev = al("stev", [64, 4 * 512])
        bufs["modrow"] = al("modrowA", [1, 2 * D])
        bufs["vrows"] = al("vrowsA", [1, 2 * D])
        bufs["grow1"] = al("grow1A", [1, D])
        bufs["grow2"] = al("grow2A", [1, D])
        bufs["wmstage"] = al("wmstageA", [128, 4 * 512])

        t_reps = p.add("sync", lambda s: s.dma_start(
            out=reps[:].rearrange("p (n d) -> p n d", n=6),
            in_=repsA_h.transpose([1, 0, 2])), inc="LDR", incby=16)
        t_bvs = t_reps
        for k in range(KK):
            t_bvs = p.add("vector", lambda v, k=k: v.tensor_scalar_mul(
                reps[:, (2 + k) * D:(3 + k) * D], reps[:, (2 + k) * D:(3 + k) * D],
                cfg.scale), wait=[t_bvs], inc="VE")

        t_mod = emit_mod_phase(p, bufs, 4,
                               [(1, "plus1", ca1_rep), (0, "plain", sa_rep)])

        wkhi = al("wkhi", [128, NCHUNK * D], BF16)
        wklo = al("wklo", [128, NCHUNK * D], BF16)
        wvhi = al("wvhi", [128, NCHUNK * D], BF16)
        wvlo = al("wvlo", [128, NCHUNK * D], BF16)
        bufs["wstage"] = al("wstageA", [128, D])
        bufs["wstage_free"] = None

        kv_free = None
        h1_free = None
        ks_free = None
        vs_free = None
        st_open = [False] * 8
        last_states = [None] * 4
        wk_free = None
        for k in range(KK):
          t_w = emit_wload(p, bufs, wk_h[k], wkhi, wklo, True, wk_free,
                           dst_free=wk_free)
          t_w = emit_wload(p, bufs, wv_h[k], wvhi, wvlo, True, t_w,
                           dst_free=wk_free)
          for t in range(cfg.nt):
            par = t % 2
            xb, t_x = emit_x_load(p, bufs, t)
            h1 = bufs["h1tm"][par]
            t_h1 = emit_ln(p, bufs, par, xb[:], h1[:], ca1_rep, sa_rep, t_x,
                           extra=[t_mod, bufs["h1tm_free"][par]])
            bufs["xbuf_free"][par] = t_h1
            t_hi, t_lo = emit_transpose_split(
                p, bufs, lambda j, h1=h1: h1[:, j * 128:(j + 1) * 128],
                h1hi, h1lo, t_h1, dst_free=h1_free)
            bufs["h1tm_free"][par] = t_lo
            lhs_hi = lambda j: h1hi[:, j * 128:(j + 1) * 128]
            lhs_lo = lambda j: h1lo[:, j * 128:(j + 1) * 128]
            if True:
                toks_k = []
                for ng in range(2):
                    prs = pairs3(
                        lhs_hi, lhs_lo,
                        lambda j, ng=ng: wkhi[
                            :, j * D + ng * 512: j * D + ng * 512 + 512],
                        lambda j, ng=ng: wklo[
                            :, j * D + ng * 512: j * D + ng * 512 + 512],
                        range(NCHUNK))
                    toks_k.append(mm_group(
                        p, bufs["pb"][2 + ng][:], prs,
                        [t_hi, t_lo, t_w, bufs["pbfree"][2 + ng]]))
                t_ev = None
                for ng in range(2):
                    t_ev = p.add("vector", lambda v, ng=ng, k=k: v.tensor_add(
                        kvtm[:, ng * 512:(ng + 1) * 512], bufs["pb"][2 + ng][:],
                        reps[:, k * D + ng * 512:k * D + ng * 512 + 512]),
                        wait=[toks_k[ng], kv_free, t_reps], inc="VE")
                    bufs["pbfree"][2 + ng] = t_ev
                t_kn = emit_kernel_norm(p, bufs, par, kvtm[:], kvtm[:],
                                        cfg.scale, t_ev)
                t_ks = emit_split(p, kvtm[:], kshi[:], kslo[:], t_kn,
                                  extra=[ks_free])
                toks_v = []
                for ng in range(2):
                    prs = pairs3(
                        lhs_hi, lhs_lo,
                        lambda j, ng=ng: wvhi[
                            :, j * D + ng * 512: j * D + ng * 512 + 512],
                        lambda j, ng=ng: wvlo[
                            :, j * D + ng * 512: j * D + ng * 512 + 512],
                        range(NCHUNK))
                    toks_v.append(mm_group(
                        p, bufs["pb"][2 + ng][:], prs,
                        [t_hi, t_lo, t_ks, bufs["pbfree"][2 + ng]]))
                t_ev2 = None
                for ng in range(2):
                    t_ev2 = p.add("vector", lambda v, ng=ng, k=k:
                                  v.scalar_tensor_tensor(
                        kvtm[:, ng * 512:(ng + 1) * 512], bufs["pb"][2 + ng][:],
                        cfg.scale,
                        reps[:, (2 + k) * D + ng * 512:(2 + k) * D + ng * 512 + 512],
                        ALU.mult, ALU.add),
                        wait=[toks_v[ng], t_ks, t_bvs], inc="VE")
                    bufs["pbfree"][2 + ng] = t_ev2
                t_vs = emit_split(p, kvtm[:], vshi[:], vslo[:], t_ev2,
                                  extra=[vs_free])
                kv_free = t_vs
                tok_mm = None
                n_i = 0
                for h in range(H):
                    bi = 4 + 2 * k + h // 8
                    c0 = (h % 8) * 64
                    bank = bufs["pb"][bi]
                    for (ka, va) in ((kshi, vshi), (kshi, vslo), (kslo, vshi)):
                        first = not st_open[bi]
                        st_open[bi] = True
                        last_tile = (h == H - 1) and (ka is kslo)
                        bank_last = (h % 8 == 7) and (ka is kslo)
                        last_glob = (t == cfg.nt - 1) and bank_last
                        w = ([t_ks, t_vs, bufs["pbfree"][bi]]
                             if n_i == 0 else [])
                        tok_mm = p.add("tensor", lambda tt, ka=ka, va=va, h=h,
                                       c0=c0, first=first, last=last_glob,
                                       bank=bank: tt.matmul(
                            bank[0:64, c0:c0 + 64],
                            ka[:, h * 64:(h + 1) * 64],
                            va[:, h * 64:(h + 1) * 64],
                            start=first, stop=last),
                            wait=w, inc="MM" if bank_last else None)
                        if bank_last:
                            last_states[bi - 4] = tok_mm
                        n_i += 1
                ks_free = tok_mm
                vs_free = tok_mm
            h1_free = kv_free
          wk_free = last_states[2 * k + 1]

        t_out = None
        t_e = None
        for kb in range(4):
            t_e = p.add("vector", lambda v, kb=kb: v.tensor_copy(
                stev[0:64, kb * 512:(kb + 1) * 512], bufs["pb"][4 + kb][0:64, :]),
                wait=[last_states[kb], t_e], inc="VE")
        return stev, t_e


def build_A(cfg):
    nc = bass.Bass("TRN2", target_bir_lowering=False)
    p = P(nc)
    bufs = {}
    es = contextlib.ExitStack()
    with es:
        setup_common(nc, p, cfg, bufs, es)
        spart_h = nc.dram_tensor("spart", [64, 2048], F32,
                                 kind="ExternalOutput").ap()
        stev, t_e = _emit_A(nc, p, bufs, cfg, es)
        t_out = p.add("gpsimd", lambda g: g.dma_start(
            out=spart_h, in_=stev[0:64, :]),
            wait=[t_e], inc="GSP", incby=16)
        p.waitonly("gpsimd", [t_out])
        with nc.Block() as block:
            p.finalize(block)
    return nc



def _emit_B(nc, p, bufs, cfg, es, sp_src=None):
    if True:
        al = lambda n, sh, dt=F32: es.enter_context(nc.sbuf_tensor(n, sh, dt))

        y_h = nc.dram_tensor("y", [cfg.t_core, D], F32, kind="ExternalOutput").ap()
        wq_h = nc.dram_tensor("wqt", [D, D], F32, kind="ExternalInput").ap()
        wo_h = nc.dram_tensor("wot", [D, D], F32, kind="ExternalInput").ap()
        repsB_h = nc.dram_tensor("repsB", [5, 128, D], F32,
                                 kind="ExternalInput").ap()
        if sp_src is None:
            spart_h = nc.dram_tensor("spart_all", [cfg.n_cores, 64, 2048], F32,
                                     kind="ExternalInput").ap()
        gatew_h = nc.dram_tensor("gatew", [D, EXP], F32, kind="ExternalInput").ap()
        gbias_h = nc.dram_tensor("gbias_rep", [128, EXP], F32,
                                 kind="ExternalInput").ap()
        upw_h = nc.dram_tensor("upw", [EXP, D, II], F32, kind="ExternalInput").ap()
        downw_h = nc.dram_tensor("downw", [EXP, II, D], F32,
                                 kind="ExternalInput").ap()
        swup_h = nc.dram_tensor("swup", [D, II], F32, kind="ExternalInput").ap()
        swdown_h = nc.dram_tensor("swdown", [II, D], F32, kind="ExternalInput").ap()
        upb_h = nc.dram_tensor("upb_fm", [EXP, 128, II // 128], F32,
                               kind="ExternalInput").ap()
        sbup_h = nc.dram_tensor("sbup_fm", [128, II // 128], F32,
                                kind="ExternalInput").ap()
        downb_h = nc.dram_tensor("downb", [EXP, D], F32, kind="ExternalInput").ap()
        lts_h = nc.dram_tensor("lts", [128, 128], BF16, kind="ExternalInput").ap()
        onec_h = nc.dram_tensor("onec", [128, 1], BF16, kind="ExternalInput").ap()
        lts8_h = nc.dram_tensor("lts8", [EXP * cfg.sct, EXP * cfg.sct], BF16,
                                kind="ExternalInput").ap()
        x2_sc = nc.dram_tensor("x2_sc", [cfg.t_core, D], F32).ap()
        h3hi_sc = nc.dram_tensor("h3hi_sc", [cfg.nt, 128, D], BF16).ap()
        h3lo_sc = nc.dram_tensor("h3lo_sc", [cfg.nt, 128, D], BF16).ap()
        h3aug_sc = nc.dram_tensor("h3aug_sc", [cfg.nt, 128, 1032], F32).ap()
        compact_d = [nc.dram_tensor(f"compact_d{e}", [cfg.cap + 1, 1032],
                                    F32).ap() for e in range(EXP)]
        eo_d = [nc.dram_tensor(f"eo_d{e}", [cfg.cap + 1, D], F32).ap()
                for e in range(EXP)]

        gb_rep = al("gb_rep", [128, D])
        sbd_rep = al("sbd_rep", [128, D])
        gbias = al("gbias_sb", [128, EXP])
        w_all = al("w_all", [128, cfg.nt * EXP])
        se_all = al("se_all", [128, cfg.nt * EXP])
        downb_sb = al("downb_sb", [8, D])
        mr1 = al("mr1", [128, 8])
        mr2 = al("mr2", [128, 8])
        gsc = al("gsc", [128, 48])
        x3tm = al("x3tm", [128, D])

        t_gb = p.add("sync", lambda s: s.dma_start(out=gbias[:], in_=gbias_h),
                     inc="LDG", incby=16)
        t_db = p.add("sync", lambda s: s.dma_start(out=downb_sb[:], in_=downb_h),
                     inc="LDD", incby=16)
        t_sbd0 = p.add("sync", lambda s: s.dma_start(
            out=sbd_rep[:], in_=repsB_h[4]), inc="LDS", incby=16)
        t_sbd = p.add("vector", lambda v: v.tensor_scalar_mul(sbd_rep[:], sbd_rep[:],
                                                              0.1),
                      wait=[t_sbd0], inc="VE")
        t_mr = p.add("vector", lambda v: v.memset(mr1[:], -1e30), inc="VE")
        t_mr = p.add("vector", lambda v: v.memset(mr2[:], -1e30), wait=[t_mr],
                     inc="VE")

        x2w_toks = [None] * cfg.nt
        h3_last = None
        y_tok = None

        s1 = contextlib.ExitStack()
        with s1:
            a1 = lambda n, sh, dt=F32: s1.enter_context(nc.sbuf_tensor(n, sh, dt))
            ca1_rep = a1("ca1_rep", [128, D])
            sa_rep = a1("sa_rep", [128, D])
            cb1_rep = a1("cb1_rep", [128, D])
            sbm_rep = a1("sbm_rep", [128, D])
            ga_rep = a1("ga_rep", [128, D])
            with contextlib.ExitStack() as s1a:
                bufs["modrow"] = s1a.enter_context(
                    nc.sbuf_tensor("modrowB", [1, 6 * D], F32))
                bufs["vrows"] = s1a.enter_context(
                    nc.sbuf_tensor("vrowsB", [1, 6 * D], F32))
                bufs["wmstage"] = s1a.enter_context(
                    nc.sbuf_tensor("wmstageB", [128, 12 * 512], F32))
                bufs["grow1"] = s1a.enter_context(
                    nc.sbuf_tensor("grow1B", [1, D], F32))
                bufs["grow2"] = s1a.enter_context(
                    nc.sbuf_tensor("grow2B", [1, D], F32))
                t_mod = emit_mod_phase(p, bufs, 12, [
                    (1, "plus1", ca1_rep), (0, "plain", sa_rep),
                    (4, "plus1", cb1_rep), (3, "plain", sbm_rep),
                    (2, "gate", ga_rep), (5, "gate", gb_rep)])
            reps = a1("repsB_sb", [128, 4 * D])
            bq_rep, bo_rep = reps[:, 0:D], reps[:, D:2 * D]
            lng_rep, lnb_rep = reps[:, 2 * D:3 * D], reps[:, 3 * D:4 * D]
            t_reps = p.add("sync", lambda s: s.dma_start(
                out=reps[:].rearrange("p (n d) -> p n d", n=4),
                in_=repsB_h[0:4].transpose([1, 0, 2])),
                wait=[t_mod], inc="LDR", incby=16)

            st_f32 = a1("st_f32", [128, 1024])
            st_hi = a1("st_hi", [128, 1024], BF16)
            st_lo = a1("st_lo", [128, 1024], BF16)
            with contextlib.ExitStack() as s1b:
                sp_acc = s1b.enter_context(nc.sbuf_tensor("sp_acc", [64, 2048], F32))
                if sp_src is None:
                    sp_buf = s1b.enter_context(
                        nc.sbuf_tensor("sp_buf", [64, 2048], F32))
                    t_acc = None
                    for c in range(cfg.n_cores):
                        t_l = p.add("sync", lambda s, c=c: s.dma_start(
                            out=sp_buf[:], in_=spart_h[c]),
                            wait=[t_acc, t_mod], inc="SPB", incby=16)
                        if c == 0:
                            t_acc = p.add("vector", lambda v: v.tensor_copy(
                                sp_acc[:], sp_buf[:]), wait=[t_l], inc="VE")
                        else:
                            t_acc = p.add("vector", lambda v: v.tensor_add(
                                sp_acc[:], sp_acc[:], sp_buf[:]),
                                wait=[t_l], inc="VE")
                else:
                    sp_ap, sp_tok = sp_src
                    t_acc = p.add("sync", lambda s: s.dma_start(
                        out=sp_acc[:], in_=sp_ap),
                        wait=[sp_tok, t_mod], inc="SPB", incby=16)
                t_st0 = p.add("vector", lambda v: v.tensor_tensor(
                    st_f32[0:64, :], sp_acc[:, 0:1024], sp_acc[:, 1024:2048],
                    ALU.mult), wait=[t_acc], inc="VE")
            t_st1 = p.add("sync", lambda s: s.dma_start(
                out=st_f32[64:128, :], in_=st_f32[0:64, :]),
                wait=[t_st0], inc="STR", incby=16)
            t_st = emit_split(p, st_f32[:], st_hi[:], st_lo[:], t_st1)

            wqhi = a1("wqhi", [128, NCHUNK * D], BF16)
            wqlo = a1("wqlo", [128, NCHUNK * D], BF16)
            wohi = a1("wohi", [128, NCHUNK * D], BF16)
            wolo = a1("wolo", [128, NCHUNK * D], BF16)
            gwhi = a1("gwhi", [128, NCHUNK * EXP], BF16)
            gwlo = a1("gwlo", [128, NCHUNK * EXP], BF16)
            bufs["wstage"] = a1("wstageB", [128, D])
            bufs["wstage_free"] = None
            t_w = emit_wload(p, bufs, wq_h, wqhi, wqlo, True, t_mod,
                             dst_free=t_st)
            t_w = emit_wload(p, bufs, wo_h, wohi, wolo, False, t_w)
            t_w = emit_wload(p, bufs, gatew_h, gwhi, gwlo, False, t_w, ncols=EXP)

            h1hi = a1("h1hiB", [128, D], BF16)
            h1lo = a1("h1loB", [128, D], BF16)
            qtm = a1("qtm", [128, D])
            qfhi = a1("qfhi", [128, D], BF16)
            qflo = a1("qflo", [128, D], BF16)
            attm = a1("attm", [128, D])
            lfhi = a1("lfhi", [128, D], BF16)
            lflo = a1("lflo", [128, D], BF16)
            x2tm = a1("x2tm", [128, D])
            h3tm = a1("h3tm", [128, 1032])
            h3hi = a1("h3hi", [128, D], BF16)
            h3lo = a1("h3lo", [128, D], BF16)

            h1_free = None
            qtm_free = None
            qf_free = None
            attm_free = None
            lf_free = None
            x2_free = None
            h3tm_free = []
            h3p_free = None
            gate_free = None

            nt_eff = 0 if (B_STAGE is not None and B_STAGE < 2) else cfg.nt
            for t in range(nt_eff):
                par = t % 2
                xb, t_x = emit_x_load(p, bufs, t)
                h1 = bufs["h1tm"][par]
                t_h1 = emit_ln(p, bufs, par, xb[:], h1[:], ca1_rep, sa_rep, t_x,
                               extra=[t_mod, bufs["h1tm_free"][par]])
                t_hi, t_lo = emit_transpose_split(
                    p, bufs, lambda j, h1=h1: h1[:, j * 128:(j + 1) * 128],
                    h1hi, h1lo, t_h1, dst_free=h1_free)
                bufs["h1tm_free"][par] = t_lo
                if B_SUB is not None and B_SUB < 2:
                    h1_free = t_lo
                    continue
                toks_q = []
                for ng in range(2):
                    prs = pairs3(
                        lambda j: h1hi[:, j * 128:(j + 1) * 128],
                        lambda j: h1lo[:, j * 128:(j + 1) * 128],
                        lambda j, ng=ng: wqhi[:, j * D + ng * 512:
                                              j * D + ng * 512 + 512],
                        lambda j, ng=ng: wqlo[:, j * D + ng * 512:
                                              j * D + ng * 512 + 512],
                        range(NCHUNK))
                    toks_q.append(mm_group(
                        p, bufs["pb"][2 + ng][:], prs,
                        [t_hi, t_lo, t_w, bufs["pbfree"][2 + ng]]))
                t_ev = None
                for ng in range(2):
                    t_ev = p.add("vector", lambda v, ng=ng: v.tensor_add(
                        qtm[:, ng * 512:(ng + 1) * 512], bufs["pb"][2 + ng][:],
                        bq_rep[:, ng * 512:ng * 512 + 512]),
                        wait=[toks_q[ng], qtm_free, t_reps], inc="VE")
                    bufs["pbfree"][2 + ng] = t_ev
                if B_SUB is not None and B_SUB < 3:
                    h1_free = toks_q[1]
                    qtm_free = t_ev
                    continue
                t_kn = emit_kernel_norm(p, bufs, par, qtm[:], qtm[:], 1.0, t_ev)
                t_qhi, t_qlo = emit_transpose_split(
                    p, bufs, lambda j: qtm[:, j * 128:(j + 1) * 128],
                    qfhi, qflo, t_kn, dst_free=qf_free)
                qtm_free = t_qlo
                if B_SUB is not None and B_SUB < 4:
                    h1_free = toks_q[1]
                    qf_free = t_qlo
                    continue
                tok_at = [None, None]
                for hb in range(2):
                    bank = bufs["pb"][4 + hb]
                    r0 = hb * 64
                    n_i = 0
                    for sslot in range(8):
                        h = 2 * sslot + hb
                        oc = sslot * 64
                        for (qa, sa_) in ((qfhi, st_hi), (qfhi, st_lo),
                                          (qflo, st_hi)):
                            first = n_i == 0
                            last = n_i == 23
                            w = ([t_qhi, t_qlo, t_st, bufs["pbfree"][4 + hb]]
                                 if first else [])
                            tok = p.add("tensor", lambda tt, qa=qa, sa_=sa_, h=h,
                                        r0=r0, oc=oc, first=first,
                                        last=last, bank=bank: tt.matmul(
                                bank[:, oc:oc + 64],
                                qa[r0:r0 + 64, (h // 2) * 128:(h // 2) * 128 + 128],
                                sa_[r0:r0 + 64, h * 64:(h + 1) * 64],
                                start=first, stop=last),
                                wait=w, inc="MM" if last else None)
                            n_i += 1
                    tok_at[hb] = tok
                qf_free = tok_at[1]
                t_evat = None
                for hb in range(2):
                    dst = attm[:].rearrange("p (s x) -> p s x", x=128)[
                        :, :, hb * 64:(hb + 1) * 64]
                    src = bufs["pb"][4 + hb][:].rearrange(
                        "p (s x) -> p s x", x=64)
                    t_evat = p.add("vector", lambda v, dst=dst, src=src:
                                   v.tensor_copy(dst, src),
                        wait=[tok_at[hb], attm_free], inc="VE")
                    bufs["pbfree"][4 + hb] = t_evat
                if B_SUB is not None and B_SUB < 5:
                    h1_free = toks_q[1]
                    continue
                t_ln = emit_ln(p, bufs, par, attm[:], attm[:], lng_rep, lnb_rep,
                               t_evat)
                t_lhi, t_llo = emit_transpose_split(
                    p, bufs, lambda j: attm[:, j * 128:(j + 1) * 128],
                    lfhi, lflo, t_ln, dst_free=lf_free)
                attm_free = t_llo
                toks_o = []
                for ng in range(2):
                    prs = pairs3(
                        lambda j: lfhi[:, j * 128:(j + 1) * 128],
                        lambda j: lflo[:, j * 128:(j + 1) * 128],
                        lambda j, ng=ng: wohi[:, j * D + ng * 512:
                                              j * D + ng * 512 + 512],
                        lambda j, ng=ng: wolo[:, j * D + ng * 512:
                                              j * D + ng * 512 + 512],
                        range(NCHUNK))
                    toks_o.append(mm_group(
                        p, bufs["pb"][2 + ng][:], prs,
                        [t_lhi, t_llo, bufs["pbfree"][2 + ng]]))
                lf_free = toks_o[1]
                if B_SUB is not None and B_SUB < 6:
                    h1_free = toks_q[1]
                    continue
                t_x2 = None
                for ng in range(2):
                    sl = slice(ng * 512, (ng + 1) * 512)
                    t_a = p.add("vector", lambda v, ng=ng, sl=sl: v.tensor_add(
                        x2tm[:, sl], bufs["pb"][2 + ng][:], bo_rep[:, sl]),
                        wait=[toks_o[ng], x2_free, t_reps], inc="VE")
                    bufs["pbfree"][2 + ng] = t_a
                    t_b = p.add("vector", lambda v, sl=sl: v.tensor_tensor(
                        x2tm[:, sl], x2tm[:, sl], ga_rep[:, sl], ALU.mult),
                        wait=[t_a], inc="VE")
                    t_x2 = p.add("vector", lambda v, sl=sl, xb=xb: v.tensor_add(
                        x2tm[:, sl], x2tm[:, sl], xb[:, sl]),
                        wait=[t_b, t_x], inc="VE")
                bufs["xbuf_free"][par] = t_x2
                t_x2w = p.add("gpsimd", lambda g, t=t: g.dma_start(
                    out=x2_sc[t * 128:(t + 1) * 128, :], in_=x2tm[:]),
                    wait=[t_x2], inc="GX2", incby=16)
                x2w_toks[t] = t_x2w
                x2_free = t_x2w
                if B_SUB is not None and B_SUB < 7:
                    h1_free = toks_q[1]
                    continue
                t_h3 = emit_ln(p, bufs, par, x2tm[:], h3tm[:, 0:D], cb1_rep,
                               sbm_rep, t_x2, extra=h3tm_free)
                t3hi, t3lo = emit_transpose_split(
                    p, bufs, lambda j: h3tm[:, j * 128:(j + 1) * 128],
                    h3hi, h3lo, t_h3, dst_free=h3p_free)
                t_w1 = p.add("gpsimd", lambda g, t=t: g.dma_start(
                    out=h3hi_sc[t], in_=h3hi[:]), wait=[t3hi, t3lo, h3_last],
                    inc="GH3", incby=16)
                t_w2 = p.add("gpsimd", lambda g, t=t: g.dma_start(
                    out=h3lo_sc[t], in_=h3lo[:]), wait=[t3lo, t_w1],
                    inc="GH3", incby=16)
                h3_last = t_w2
                if B_STAGE is not None and B_STAGE < 3:
                    h1_free = toks_q[1]
                    h3p_free = [t_w1, t_w2]
                    continue
                prs = pairs3(
                    lambda j: h3hi[:, j * 128:(j + 1) * 128],
                    lambda j: h3lo[:, j * 128:(j + 1) * 128],
                    lambda j: gwhi[:, j * EXP:(j + 1) * EXP],
                    lambda j: gwlo[:, j * EXP:(j + 1) * EXP],
                    range(NCHUNK))
                tok_g = mm_group(p, bufs["pb"][6][:, 0:EXP], prs,
                                 [t3hi, t3lo, t_w, bufs["pbfree"][6]])
                sS, s2, s3 = gsc[:, 0:8], gsc[:, 8:16], gsc[:, 16:24]
                msk, t8, den = gsc[:, 24:32], gsc[:, 32:40], gsc[:, 40:41]
                t_g1 = p.add("vector", lambda v: v.tensor_add(
                    sS, bufs["pb"][6][:, 0:EXP], gbias[:]),
                    wait=[tok_g, t_gb, gate_free], inc="VE")
                bufs["pbfree"][6] = t_g1
                t_g2 = p.add("scalar", lambda s: s.activation(sS, sS, AF.Sigmoid),
                             wait=[t_g1], inc="AC")
                t_g3 = p.add("vector", lambda v: v.max(t8, sS), wait=[t_g2],
                             inc="VE")
                t_g4 = p.add("vector", lambda v: v.tensor_copy(mr1[:, 0:1],
                                                               t8[:, 0:1]),
                             wait=[t_g3, t_mr], inc="VE")
                t_g5 = p.add("vector", lambda v: v.tensor_copy(mr2[:, 0:1],
                                                               t8[:, 1:2]),
                             wait=[t_g3], inc="VE")
                t_g6 = p.add("vector", lambda v: v.match_replace(
                    s2, mr1[:], sS, -1e30), wait=[t_g4, t_g2], inc="VE")
                t_g7 = p.add("vector", lambda v: v.match_replace(
                    s3, mr2[:], s2, -1e30), wait=[t_g5, t_g6], inc="VE")
                t_g8 = p.add("vector", lambda v: v.tensor_tensor(
                    msk, sS, s3, ALU.not_equal), wait=[t_g7], inc="VE")
                t_g9 = p.add("vector", lambda v: v.tensor_add(
                    den, t8[:, 0:1], t8[:, 1:2]), wait=[t_g3], inc="VE")
                t_g10 = p.add("vector", lambda v: v.tensor_scalar_add(
                    den, den, 1e-6), wait=[t_g9], inc="VE")
                t_g11 = p.add("vector", lambda v: v.reciprocal(den, den),
                              wait=[t_g10], inc="VE")
                wsl = w_all[:, t * EXP:(t + 1) * EXP]
                t_g12 = p.add("vector", lambda v, wsl=wsl: v.tensor_tensor(
                    wsl, sS, msk, ALU.mult), wait=[t_g8], inc="VE")
                t_g13 = p.add("vector", lambda v, wsl=wsl: v.tensor_scalar(
                    wsl, wsl, den, None, ALU.mult), wait=[t_g12, t_g11], inc="VE")
                p.add("vector", lambda v, t=t, wsl=wsl: v.tensor_scalar_mul(
                    se_all[:, t * EXP:(t + 1) * EXP], wsl, 0.1),
                    wait=[t_g13], inc="VE")
                t_wc = p.add("vector", lambda v, wsl=wsl: v.tensor_copy(
                    h3tm[:, D:D + EXP], wsl), wait=[t_g13] + h3tm_free, inc="VE")
                t_w3 = p.add("gpsimd", lambda g, t=t: g.dma_start(
                    out=h3aug_sc[t], in_=h3tm[:]),
                    wait=[t_h3, t_wc, t_w2], inc="GH3", incby=16)
                h3tm_free = [t3lo, t_w3]
                h3_last = t_w3
                gate_free = t_g13
                h1_free = toks_q[1]
                h3p_free = [tok_g, t_w1, t_w2]

        bar = p.barrier_toks()
        for e in P.ENGS:
            p.waitonly(e, bar)
        bufs["pbfree"] = [None] * 8
        bufs["slot_free"] = {}

        s2s = contextlib.ExitStack()
        with s2s:
            a2 = lambda n, sh, dt=F32: s2s.enter_context(nc.sbuf_tensor(n, sh, dt))
            h3shi = a2("h3shi", [128, cfg.sct * D], BF16)
            h3slo = a2("h3slo", [128, cfg.sct * D], BF16)
            acc = a2("ffn_acc", [128, cfg.sct * D])
            hacthi = a2("hacthi", [128, ILN * cfg.sct * 128], BF16)
            hactlo = a2("hactlo", [128, ILN * cfg.sct * 128], BF16)
            upP = [(a2(f"uphi{i}", [128, NCHUNK * IG], BF16),
                    a2(f"uplo{i}", [128, NCHUNK * IG], BF16)) for i in range(2)]
            dnhi = a2("dnhi", [128, ILN * D], BF16)
            dnlo = a2("dnlo", [128, ILN * D], BF16)
            bufs["fst"] = [a2(f"fst{i}", [128, 2048]) for i in range(2)]
            bufs["fst_free"] = [None, None]
            fstage = bufs["fst"][0]
            tfull = a2("tfull", [128, 512])
            sbup_sb = a2("sbup_sb", [128, II // 128])
            upb_sb = a2("upb_sb", [128, II // 128])
            wt_t = a2("wt_t", [8, 128])
            CT = cfg.cap_tiles
            CAP = cfg.cap
            NW = EXP * cfg.sct
            lts_sb = a2("lts_sb", [128, 128], BF16)
            onec_sb = a2("onec_sb", [128, 1], BF16)
            lts8_sb = a2("lts8_sb", [NW, NW], BF16)
            maskF = a2("maskF", [128, NW], BF16)
            gposf = a2("gposf", [128, NW])
            idxf = a2("idxf", [128, NW])
            idx_i = s2s.enter_context(nc.sbuf_tensor("idx_i", [128, NW],
                                                      mybir.dt.int32))
            c1row = a2("c1row", [1, NW])
            ct_sb = a2("ct_sb", [NW, 1], BF16)
            basec = a2("basec", [NW, 1])
            baser = a2("baser", [1, NW])
            aug_sb = [a2("aug_sb0", [128, 1032])]
            c_sb = [a2(f"c_sb{i}", [128, 1032]) for i in range(2)]
            se_c = a2("se_c", [128, CT])
            eo_acc = a2("eo_acc", [128, CT * D])
            gsc_t = [a2("gsc_t0", [128, D])]
            ztile = fstage

            t_sbup = p.add("sync", lambda s: s.dma_start(
                out=sbup_sb[:], in_=sbup_h), inc="LDU", incby=16)
            t_lts = p.add("sync", lambda s: s.dma_start(
                out=lts_sb[:], in_=lts_h), wait=[t_sbup], inc="LDU", incby=16)
            t_lts = p.add("sync", lambda s: s.dma_start(
                out=onec_sb[:], in_=onec_h), wait=[t_lts], inc="LDU", incby=16)
            t_lts = p.add("sync", lambda s: s.dma_start(
                out=lts8_sb[:], in_=lts8_h), wait=[t_lts], inc="LDU", incby=16)
            t_zt = p.add("vector", lambda v: v.memset(ztile[:, 0:1032], 0.0),
                         inc="VE")
            t_zf = t_zt
            for e in range(EXP):
                for ct in range(CT):
                    t_zf = p.add("gpsimd", lambda g, e=e, ct=ct: g.dma_start(
                        out=compact_d[e][ct * 128:(ct + 1) * 128, :],
                        in_=ztile[:, 0:1032]), wait=[t_zt, t_zf],
                        inc="GZ", incby=16)
                t_zf = p.add("gpsimd", lambda g, e=e: g.dma_start(
                    out=compact_d[e][CAP:CAP + 1, :], in_=ztile[0:1, 0:1032]),
                    wait=[t_zt, t_zf], inc="GZ", incby=16)
                t_zf = p.add("gpsimd", lambda g, e=e: g.dma_start(
                    out=eo_d[e][CAP:CAP + 1, :], in_=ztile[0:1, 0:D]),
                    wait=[t_zt, t_zf], inc="GZ", incby=16)

            fstage_free = None
            wt_free = [None]
            up_free = [None, None]
            dn_free = None
            hact_free = None
            tfull_free = None
            acc_free = None
            h3s_free = None
            upb_free = None
            x3_free = None
            n_tg = max(1, cfg.sct // 4)
            tpg = min(4, cfg.sct)

            rt_free = None
            cpt_free = None
            bufs["fst_free"][0] = t_zf
            c_free = [[], []]
            se_free = None
            eo_free = None
            cfm_free = None
            gat_free = None
            gsc_free = [None]
            nsc_eff = 0 if (B_STAGE is not None and B_STAGE < 4) else cfg.nsc
            for sc in range(nsc_eff):
                t0 = sc * cfg.sct
                t_l1 = p.add("sync", lambda s, t0=t0: s.dma_start(
                    out=h3shi[:].rearrange("p (a c) -> p a c", a=cfg.sct),
                    in_=h3hi_sc[t0:t0 + cfg.sct].transpose([1, 0, 2])),
                    wait=[h3_last, h3s_free], inc="H3S", incby=16)
                t_h3s = p.add("sync", lambda s, t0=t0: s.dma_start(
                    out=h3slo[:].rearrange("p (a c) -> p a c", a=cfg.sct),
                    in_=h3lo_sc[t0:t0 + cfg.sct].transpose([1, 0, 2])),
                    wait=[h3_last, h3s_free, t_l1], inc="H3S", incby=16)
                t_acc0 = p.add("vector", lambda v: v.memset(acc[:], 0.0),
                               wait=[acc_free], inc="VE")
                acc_tok = [[t_acc0, t_acc0] for _ in range(cfg.sct)]

                def h3s_mov(buf, j, tg):
                    a = buf[:].rearrange("p (a j c) -> p a j c", a=cfg.sct,
                                         j=NCHUNK)
                    return a[:, tg * tpg:(tg + 1) * tpg, j, :]

                for g in range(IGN):
                    upar = g % 2
                    uphi, uplo = upP[upar]
                    src_u = swup_h[:, g * IG:(g + 1) * IG]
                    t_u = emit_block_load(p, bufs, src_u, NCHUNK, IG,
                                          uphi, uplo, upar, up_free[upar])
                    hact_last = None
                    up_mm_last = None
                    for tg in range(n_tg):
                        for il in range(ILN):
                            bi = (tg * ILN + il) % 2
                            bank = bufs["pb"][bi]
                            prs = pairs3(
                                lambda j, il=il: uphi[
                                    :, j * IG + il * 128: j * IG + il * 128 + 128],
                                lambda j, il=il: uplo[
                                    :, j * IG + il * 128: j * IG + il * 128 + 128],
                                lambda j, tg=tg: h3s_mov(h3shi, j, tg),
                                lambda j, tg=tg: h3s_mov(h3slo, j, tg),
                                range(NCHUNK))
                            tok_up = mm_group(
                                p, bank[:, 0:tpg * 128], prs,
                                [t_u, t_h3s, bufs["pbfree"][bi]])
                            up_mm_last = tok_up
                            up_free[upar] = tok_up
                            t_gl = p.add("scalar", lambda s, bank=bank, il=il,
                                         g=g, tpg=tpg: s.activation(
                                tfull[:, 0:tpg * 128], bank[:, 0:tpg * 128],
                                AF.Gelu,
                                bias=sbup_sb[:, g * ILN + il:g * ILN + il + 1]),
                                wait=[tok_up, tfull_free, t_sbup], inc="AC")
                            bufs["pbfree"][bi] = t_gl
                            hh = hacthi[:].rearrange(
                                "p (a il c) -> p a il c", a=cfg.sct, il=ILN)
                            hl = hactlo[:].rearrange(
                                "p (a il c) -> p a il c", a=cfg.sct, il=ILN)
                            hdst_hi = hh[:, tg * tpg:(tg + 1) * tpg, il, :]
                            hdst_lo = hl[:, tg * tpg:(tg + 1) * tpg, il, :]
                            t_hh = p.add("vector", lambda v, hdst_hi=hdst_hi,
                                         tpg=tpg: v.tensor_copy(
                                hdst_hi, tfull[:, 0:tpg * 128].rearrange(
                                    "p (a c) -> p a c", a=tpg)),
                                wait=[t_gl, hact_free], inc="VE")
                            t_hl = p.add("vector", lambda v, hdst_hi=hdst_hi,
                                         hdst_lo=hdst_lo, tpg=tpg:
                                         v.tensor_tensor(
                                hdst_lo, tfull[:, 0:tpg * 128].rearrange(
                                    "p (a c) -> p a c", a=tpg),
                                hdst_hi, ALU.subtract),
                                wait=[t_hh], inc="VE")
                            tfull_free = t_hl
                            hact_last = t_hl
                    src_d = swdown_h[g * IG:(g + 1) * IG, :]
                    t_dn = emit_block_load(p, bufs, src_d, ILN, D,
                                           dnhi, dnlo, g % 2, dn_free)
                    dn_mm_last = None
                    for tl in range(cfg.sct):
                        for ng in range(2):
                            bi = 2 + (tl % 2) * 2 + ng
                            bank = bufs["pb"][bi]
                            prs = pairs3(
                                lambda j, tl=tl: hacthi[
                                    :, (tl * ILN + j) * 128:
                                    (tl * ILN + j) * 128 + 128],
                                lambda j, tl=tl: hactlo[
                                    :, (tl * ILN + j) * 128:
                                    (tl * ILN + j) * 128 + 128],
                                lambda j, ng=ng: dnhi[
                                    :, j * D + ng * 512: j * D + ng * 512 + 512],
                                lambda j, ng=ng: dnlo[
                                    :, j * D + ng * 512: j * D + ng * 512 + 512],
                                range(ILN))
                            tok_dn = mm_group(
                                p, bank[:], prs,
                                [t_dn, hact_last, bufs["pbfree"][bi]])
                            dn_mm_last = tok_dn
                            asl = acc[:, tl * D + ng * 512:
                                      tl * D + ng * 512 + 512]
                            t_ae = p.add("vector", lambda v, asl=asl, bank=bank:
                                         v.scalar_tensor_tensor(
                                asl, bank[:], 0.1, asl, ALU.mult, ALU.add),
                                wait=[tok_dn, acc_tok[tl][ng]], inc="VE")
                            acc_tok[tl][ng] = t_ae
                            bufs["pbfree"][bi] = t_ae
                    dn_free = dn_mm_last
                    hact_free = dn_mm_last
                cfm_free = up_mm_last

                wv = w_all[:, t0 * EXP:(t0 + cfg.sct) * EXP].rearrange(
                    "p (t e) -> p t e", e=EXP).transpose([0, 2, 1])
                t_m = p.add("vector", lambda v, wv=wv: v.tensor_single_scalar(
                    maskF[:].rearrange("p (e t) -> p e t", e=EXP), wv, 0.0,
                    ALU.is_gt), wait=[rt_free], inc="VE")
                t_p1 = p.add("tensor", lambda t: t.matmul(
                    bufs["pb"][6][:, 0:NW], lts_sb[:], maskF[:],
                    start=True, stop=True),
                    wait=[t_m, t_lts, bufs["pbfree"][6]], inc="MM")
                t_c1 = p.add("tensor", lambda t: t.matmul(
                    bufs["pb"][7][0:1, 0:NW], onec_sb[:], maskF[:],
                    start=True, stop=True),
                    wait=[t_m, t_lts, bufs["pbfree"][7]], inc="MM")
                t_ec1 = p.add("vector", lambda v: v.tensor_copy(
                    c1row[0:1, 0:NW], bufs["pb"][7][0:1, 0:NW]),
                    wait=[t_c1, rt_free], inc="VE")
                bufs["pbfree"][7] = t_ec1
                t_ct = p.add("tensor", lambda t: t.transpose(
                    bufs["pb"][7][0:NW, 0:1], c1row[0:1, 0:NW],
                    bufs["ident"][0:1, 0:1]),
                    wait=[t_ec1, bufs["pbfree"][7]], inc="MM")
                t_ect = p.add("vector", lambda v: v.tensor_copy(
                    ct_sb[:], bufs["pb"][7][0:NW, 0:1]),
                    wait=[t_ct, rt_free], inc="VE")
                bufs["pbfree"][7] = t_ect
                t_b = p.add("tensor", lambda t: t.matmul(
                    bufs["pb"][7][0:NW, 0:1], lts8_sb[:], ct_sb[:],
                    start=True, stop=True),
                    wait=[t_ect, t_lts, bufs["pbfree"][7]], inc="MM")
                t_eb = p.add("vector", lambda v: v.tensor_copy(
                    basec[:], bufs["pb"][7][0:NW, 0:1]),
                    wait=[t_b, rt_free], inc="VE")
                bufs["pbfree"][7] = t_eb
                t_bt = p.add("tensor", lambda t: t.transpose(
                    bufs["pb"][7][0:1, 0:NW], basec[:],
                    bufs["ident"][0:NW, 0:NW]),
                    wait=[t_eb, bufs["pbfree"][7]], inc="MM")
                t_ebr = p.add("vector", lambda v: v.tensor_copy(
                    baser[0:1, 0:NW], bufs["pb"][7][0:1, 0:NW]),
                    wait=[t_bt, rt_free], inc="VE")
                bufs["pbfree"][7] = t_ebr
                t_bb = p.add("tensor", lambda t: t.matmul(
                    bufs["pb"][7][:, 0:NW], bufs["ones1"][:],
                    baser[0:1, 0:NW], start=True, stop=True),
                    wait=[t_ebr, bufs["tok_ones"], bufs["pbfree"][7]], inc="MM")
                t_gp0 = p.add("vector", lambda v: v.tensor_copy(
                    gposf[:], bufs["pb"][6][:, 0:NW]),
                    wait=[t_p1, rt_free], inc="VE")
                bufs["pbfree"][6] = t_gp0
                t_gp = p.add("vector", lambda v: v.tensor_add(
                    gposf[:], gposf[:], bufs["pb"][7][:, 0:NW]),
                    wait=[t_gp0, t_bb], inc="VE")
                bufs["pbfree"][7] = t_gp
                t_i1 = p.add("vector", lambda v: v.tensor_scalar_add(
                    idxf[:], gposf[:], float(-CAP)), wait=[t_gp, rt_free],
                    inc="VE")
                t_i2 = p.add("vector", lambda v: v.tensor_tensor(
                    idxf[:], idxf[:], maskF[:], ALU.mult), wait=[t_i1, t_m],
                    inc="VE")
                t_i3 = p.add("vector", lambda v: v.tensor_scalar_add(
                    idxf[:], idxf[:], float(CAP)), wait=[t_i2], inc="VE")
                t_idx = p.add("vector", lambda v: v.tensor_copy(
                    idx_i[:], idxf[:]), wait=[t_i3, rt_free], inc="VE")

                aug_free = [[]]
                sc_toks = []
                for tl in range(cfg.sct):
                    par = 0
                    t_al = p.add("sync", lambda s, t0=t0, tl=tl, par=par:
                                 s.dma_start(out=aug_sb[par][:],
                                             in_=h3aug_sc[t0 + tl]),
                                 wait=[h3_last] + aug_free[par],
                                 inc=f"AUG{par}", incby=16)
                    aug_free[par] = []
                    for e in range(EXP):
                        col = e * cfg.sct + tl
                        t_sct = p.add("gpsimd", lambda g, e=e, col=col, par=par:
                                      g.indirect_dma_start(
                            out=compact_d[e],
                            out_offset=IndirectOffsetOnAxis(
                                idx_i[:, col:col + 1], 0),
                            in_=aug_sb[par][:], in_offset=None),
                            wait=[t_al, t_idx, t_zf, cpt_free],
                            inc="GSC", incby=16)
                        aug_free[par].append(t_sct)
                        sc_toks.append(t_sct)
                t_scall = sc_toks[-1]

                eo_toks = []
                for e in range(EXP):
                    t_upb = p.add("sync", lambda s, e=e: s.dma_start(
                        out=upb_sb[:], in_=upb_h[e]),
                        wait=[upb_free], inc="UPB", incby=16)
                    tf_hi = tf_lo = None
                    t_se = None
                    for ct in range(CT):
                        cpar = ct % 2
                        tok = p.add("sync", lambda s, e=e, ct=ct, cpar=cpar:
                                    s.dma_start(
                            out=c_sb[cpar][:],
                            in_=compact_d[e][ct * 128:(ct + 1) * 128, :]),
                            wait=[t_scall] + c_free[cpar],
                            inc=f"CPL{cpar}", incby=16)
                        t_se = p.add("vector", lambda v, ct=ct, e=e, cpar=cpar:
                                     v.tensor_scalar_mul(
                            se_c[:, ct:ct + 1],
                            c_sb[cpar][:, D + e:D + e + 1],
                            0.1), wait=[tok, se_free, t_se], inc="VE")
                        tf_hi, tf_lo = emit_transpose_split(
                            p, bufs,
                            lambda j, cpar=cpar: c_sb[cpar][:, j * 128:
                                                          j * 128 + 128],
                            h3shi[:, ct * D:(ct + 1) * D],
                            h3slo[:, ct * D:(ct + 1) * D],
                            tok, dst_free=[cfm_free])
                        c_free[cpar] = [tf_lo, t_se]
                    t_ea0 = p.add("vector", lambda v: v.memset(eo_acc[:], 0.0),
                                  wait=[eo_free], inc="VE")
                    ea_tok = [[t_ea0, t_ea0] for _ in range(CT)]
                    cfm_hi3 = h3shi[:, 0:CT * D].rearrange(
                        "p (ct j c) -> p ct j c", ct=CT, j=NCHUNK)
                    cfm_lo3 = h3slo[:, 0:CT * D].rearrange(
                        "p (ct j c) -> p ct j c", ct=CT, j=NCHUNK)
                    up_last = None
                    for g in range(IGN):
                        upar = g % 2
                        uphi, uplo = upP[upar]
                        t_u = emit_block_load(
                            p, bufs, upw_h[e][:, g * IG:(g + 1) * IG],
                            NCHUNK, IG, uphi, uplo, upar, up_free[upar])
                        for il in range(ILN):
                            bi = il % 2
                            bank = bufs["pb"][bi]
                            prs = pairs3(
                                lambda j, il=il: uphi[
                                    :, j * IG + il * 128:
                                    j * IG + il * 128 + 128],
                                lambda j, il=il: uplo[
                                    :, j * IG + il * 128:
                                    j * IG + il * 128 + 128],
                                lambda j: cfm_hi3[:, :, j, :],
                                lambda j: cfm_lo3[:, :, j, :],
                                range(NCHUNK))
                            tok_up = mm_group(
                                p, bank[:, 0:CT * 128], prs,
                                [t_u, tf_hi, tf_lo, bufs["pbfree"][bi]])
                            up_last = tok_up
                            up_free[upar] = tok_up
                            t_gl = p.add("scalar", lambda s, bank=bank, il=il,
                                         g=g: s.activation(
                                tfull[:, 0:CT * 128], bank[:, 0:CT * 128],
                                AF.Gelu,
                                bias=upb_sb[:, g * ILN + il:g * ILN + il + 1]),
                                wait=[tok_up, tfull_free, t_upb], inc="AC")
                            bufs["pbfree"][bi] = t_gl
                            last_gelu = t_gl
                            t_hh = p.add("vector", lambda v, il=il:
                                         v.tensor_copy(
                                hacthi[:, il * CT * 128:(il + 1) * CT * 128],
                                tfull[:, 0:CT * 128]),
                                wait=[t_gl, hact_free], inc="VE")
                            t_hl = p.add("vector", lambda v, il=il:
                                         v.tensor_tensor(
                                hactlo[:, il * CT * 128:(il + 1) * CT * 128],
                                tfull[:, 0:CT * 128],
                                hacthi[:, il * CT * 128:(il + 1) * CT * 128],
                                ALU.subtract), wait=[t_hh], inc="VE")
                            tfull_free = t_hl
                            hact_last = t_hl
                        t_dn = emit_block_load(
                            p, bufs, downw_h[e][g * IG:(g + 1) * IG, :],
                            ILN, D, dnhi, dnlo, g % 2, dn_free)
                        dn_last = None
                        for ct in range(CT):
                            for ng in range(2):
                                bi = 2 + (ct % 2) * 2 + ng
                                bank = bufs["pb"][bi]
                                prs = pairs3(
                                    lambda j, ct=ct: hacthi[
                                        :, (j * CT + ct) * 128:
                                        (j * CT + ct) * 128 + 128],
                                    lambda j, ct=ct: hactlo[
                                        :, (j * CT + ct) * 128:
                                        (j * CT + ct) * 128 + 128],
                                    lambda j, ng=ng: dnhi[
                                        :, j * D + ng * 512:
                                        j * D + ng * 512 + 512],
                                    lambda j, ng=ng: dnlo[
                                        :, j * D + ng * 512:
                                        j * D + ng * 512 + 512],
                                    range(ILN))
                                tok_dn = mm_group(
                                    p, bank[:], prs,
                                    [t_dn, hact_last, bufs["pbfree"][bi]])
                                dn_last = tok_dn
                                esl = eo_acc[:, ct * D + ng * 512:
                                             ct * D + ng * 512 + 512]
                                t_ae = p.add("vector", lambda v, esl=esl,
                                             bank=bank: v.tensor_add(
                                    esl, esl, bank[:]),
                                    wait=[tok_dn, ea_tok[ct][ng]], inc="VE")
                                ea_tok[ct][ng] = t_ae
                                bufs["pbfree"][bi] = t_ae
                        dn_free = dn_last
                        hact_free = dn_last
                    upb_free = last_gelu
                    eo_w = []
                    for ct in range(CT):
                        t_s2 = p.add("vector", lambda v, ct=ct:
                                     v.tensor_scalar(
                            eo_acc[:, ct * D:(ct + 1) * D],
                            eo_acc[:, ct * D:(ct + 1) * D],
                            se_c[:, ct:ct + 1], None, ALU.mult),
                            wait=[ea_tok[ct][0], ea_tok[ct][1], t_se],
                            inc="VE")
                        t_wr = p.add("gpsimd", lambda g, e=e, ct=ct:
                                     g.dma_start(
                            out=eo_d[e][ct * 128:(ct + 1) * 128, :],
                            in_=eo_acc[:, ct * D:(ct + 1) * D]),
                            wait=[t_s2, t_zf, gat_free], inc="GEO", incby=16)
                        eo_w.append(t_wr)
                    eo_free = eo_w[-1]
                    eo_toks.extend(eo_w)
                    se_free = t_s2
                    cfm_free = up_last
                t_eoall = eo_toks[-1]

                gat_toks = []
                for tl in range(cfg.sct):
                    for e in range(EXP):
                        par = 0
                        col = e * cfg.sct + tl
                        t_gt = p.add("gpsimd", lambda g, e=e, col=col, par=par:
                                     g.indirect_dma_start(
                            out=gsc_t[par][:], out_offset=None,
                            in_=eo_d[e],
                            in_offset=IndirectOffsetOnAxis(
                                idx_i[:, col:col + 1], 0)),
                            wait=[t_eoall, gsc_free[par]],
                            inc=f"GG{par}", incby=16)
                        t_ad = p.add("vector", lambda v, tl=tl, par=par:
                                     v.tensor_add(
                            acc[:, tl * D:(tl + 1) * D],
                            acc[:, tl * D:(tl + 1) * D], gsc_t[par][:]),
                            wait=[t_gt, acc_tok[tl][0], acc_tok[tl][1]],
                            inc="VE")
                        gsc_free[par] = t_ad
                        acc_tok[tl][0] = t_ad
                        acc_tok[tl][1] = t_ad
                        gat_toks.append(t_ad)
                gat_free = gat_toks[-1]
                rt_free = gat_free
                cpt_free = tf_lo
                h3s_free = up_last

                for tl in range(cfg.sct):
                    t = t0 + tl
                    par = tl % 2
                    st = bufs["stats"][par]
                    t_tr = p.add("tensor", lambda tt, t=t: tt.transpose(
                        bufs["pb"][7][0:8, 0:128],
                        w_all[:, t * EXP:(t + 1) * EXP], bufs["ident"][:]),
                        wait=[gate_free, bufs["pbfree"][7]], inc="MM")
                    t_wt = p.add("vector", lambda v: v.tensor_copy(
                        wt_t[:], bufs["pb"][7][0:8, 0:128]),
                        wait=[t_tr, wt_free[0]], inc="VE")
                    bufs["pbfree"][7] = t_wt
                    t_wdb = None
                    for ng in range(2):
                        tok = p.add("tensor", lambda tt, ng=ng: tt.matmul(
                            bufs["pb"][6][:, 0:512],
                            wt_t[:],
                            downb_sb[:, ng * 512:(ng + 1) * 512],
                            start=True, stop=True),
                            wait=[t_wt, t_db, bufs["pbfree"][6]],
                            inc="MM")
                        asl = acc[:, tl * D + ng * 512:tl * D + ng * 512 + 512]
                        t_wdb = p.add("vector", lambda v, asl=asl:
                                      v.scalar_tensor_tensor(
                            asl, bufs["pb"][6][:, 0:512],
                            0.1, asl, ALU.mult, ALU.add),
                            wait=[tok, acc_tok[tl][ng]], inc="VE")
                        acc_tok[tl][ng] = t_wdb
                        bufs["pbfree"][6] = t_wdb
                    wt_free[0] = t_wdb
                    t_sb2 = None
                    for ng in range(2):
                        asl = acc[:, tl * D + ng * 512:tl * D + ng * 512 + 512]
                        t_sb2 = p.add("vector", lambda v, asl=asl, ng=ng:
                                      v.tensor_add(
                            asl, asl, sbd_rep[:, ng * 512:(ng + 1) * 512]),
                            wait=[t_sbd, acc_tok[tl][ng]], inc="VE")
                        acc_tok[tl][ng] = t_sb2
                    accsl = acc[:, tl * D:(tl + 1) * D]
                    t_mx = p.add("vector", lambda v, accsl=accsl, st=st:
                                 v.reduce_max(st[:, 0:1], accsl, axis=AX.X,
                                              apply_absolute_value=True),
                        wait=[acc_tok[tl][0], acc_tok[tl][1],
                              bufs["stats_free"][par]], inc="VE")
                    t_mx = p.add("vector", lambda v, st=st: v.tensor_scalar_add(
                        st[:, 0:1], st[:, 0:1], 1e-6), wait=[t_mx], inc="VE")
                    t_mx = p.add("vector", lambda v, st=st: v.reciprocal(
                        st[:, 1:2], st[:, 0:1]), wait=[t_mx], inc="VE")
                    bufs["stats_free"][par] = t_mx
                    xb = bufs["xbuf"][par]
                    t_xr = p.add("sync", lambda s, t=t, xb=xb: s.dma_start(
                        out=xb[:], in_=x2_sc[t * 128:(t + 1) * 128, :]),
                        wait=[x2w_toks[t], bufs["xbuf_free"][par]],
                        inc=f"XB{par}", incby=16)
                    t_f1 = p.add("vector", lambda v, accsl=accsl, st=st:
                                 v.tensor_scalar(x3tm[:], accsl, st[:, 1:2],
                                                 None, ALU.mult),
                        wait=[t_mx, x3_free], inc="VE")
                    t_f2 = p.add("vector", lambda v: v.tensor_tensor(
                        x3tm[:], x3tm[:], gb_rep[:], ALU.mult),
                        wait=[t_f1], inc="VE")
                    t_f3 = p.add("vector", lambda v, xb=xb: v.tensor_add(
                        x3tm[:], x3tm[:], xb[:]), wait=[t_f2, t_xr], inc="VE")
                    bufs["xbuf_free"][par] = t_f3
                    y_tok = p.add("gpsimd", lambda g, t=t: g.dma_start(
                        out=y_h[t * 128:(t + 1) * 128, :], in_=x3tm[:]),
                        wait=[t_f3], inc="GY", incby=16)
                    x3_free = y_tok
                acc_free = y_tok

            if B_STAGE is not None and B_STAGE < 4:
                bar2 = p.barrier_toks()
                t_z = p.add("vector", lambda v: v.memset(x3tm[:], 0.0),
                            wait=bar2, inc="VE")
                p.add("gpsimd", lambda g: g.dma_start(
                    out=y_h[0:128, :], in_=x3tm[:]), wait=[t_z],
                    inc="GY", incby=16)
            p.waitonly("gpsimd", [("GY", p.now("GY")), ("GX2", p.now("GX2")),
                                  ("GH3", p.now("GH3"))])


def build_B(cfg):
    nc = bass.Bass("TRN2", target_bir_lowering=False)
    p = P(nc)
    bufs = {}
    es = contextlib.ExitStack()
    with es:
        setup_common(nc, p, cfg, bufs, es)
        _emit_B(nc, p, bufs, cfg, es, sp_src=None)
        with nc.Block() as block:
            p.finalize(block)
    return nc


def build_fused(cfg):
    nc = bass.Bass("TRN2", target_bir_lowering=False)
    p = P(nc)
    bufs = {}
    es = contextlib.ExitStack()
    with es:
        setup_common(nc, p, cfg, bufs, es)
        sp_loc = nc.dram_tensor("sp_loc", [64, 2048], F32).ap()
        sp_sum = nc.dram_tensor("sp_sum", [64, 2048], F32).ap()
        with contextlib.ExitStack() as esa:
            stev, t_e = _emit_A(nc, p, bufs, cfg, esa)
            t_w = p.add("gpsimd", lambda g: g.dma_start(
                out=sp_loc, in_=stev[0:64, :]),
                wait=[t_e], inc="GSP", incby=16)
            rg = [list(range(cfg.n_cores))]
            t_cc = p.add("gpsimd", lambda g: g.collective_compute(
                "AllReduce", ALU.add, replica_groups=rg,
                ins=[sp_loc], outs=[sp_sum]),
                wait=[t_w], inc="CC")
            bar = p.barrier_toks()
            for e in P.ENGS:
                p.waitonly(e, bar)
        bufs["pbfree"] = [None] * 8
        bufs["slot_free"] = {}
        bufs["wstage_free"] = None
        _emit_B(nc, p, bufs, cfg, es, sp_src=(sp_sum, t_cc))
        with nc.Block() as block:
            p.finalize(block)
    return nc



def _rep(v):
    v = np.ascontiguousarray(np.asarray(v, np.float32).reshape(-1))
    return np.ascontiguousarray(np.broadcast_to(v[None, :], (128, v.shape[0])))


def kernel(**inputs):
    return _run(inputs, Cfg(8, 2048), run_bass_kernel_spmd)


def _run(inputs, cfg, run_fn):
    f = lambda k: np.ascontiguousarray(np.asarray(inputs[k], np.float32))
    x = f("L0_attn_x").reshape(cfg.n_cores * cfg.t_core, D)
    wq_t = np.ascontiguousarray(f("L0_attn_Wq").T)
    wk = f("L0_attn_Wk")
    wv = f("L0_attn_Wv")
    wo_t = np.ascontiguousarray(f("L0_attn_Wo").T)
    bk = f("L0_attn_bk")
    bv = f("L0_attn_bv")
    z128 = np.zeros((128, D), np.float32)
    repsA = np.ascontiguousarray(np.stack(
        [_rep(bk[0]), _rep(bk[1]), _rep(bv[0]), _rep(bv[1]), z128, z128]))
    repsB = np.ascontiguousarray(np.stack(
        [_rep(f("L0_attn_bq")), _rep(f("L0_attn_bo")), _rep(f("L0_attn_ln_g")),
         _rep(f("L0_attn_ln_b")), _rep(f("L0_ffn_sb_down"))]))
    c_col = np.ascontiguousarray(f("L0_c").reshape(NCHUNK, 128).T)
    wmod = f("L0_wmod")
    ident = np.eye(128, dtype=np.float32)
    ones1 = np.ones((1, 128), np.float32)
    gbias_rep = _rep(f("L0_ffn_bias"))
    upb_fm = np.ascontiguousarray(
        f("L0_ffn_up_b").reshape(EXP, II // 128, 128).transpose(0, 2, 1))
    sbup_fm = np.ascontiguousarray(f("L0_ffn_sb_up").reshape(II // 128, 128).T)
    import ml_dtypes
    bf = ml_dtypes.bfloat16
    lts = np.triu(np.ones((128, 128), np.float32), 1).astype(bf)
    onec = np.ones((128, 1), np.float32).astype(bf)
    nw = EXP * cfg.sct
    l8 = np.zeros((nw, nw), np.float32)
    for e in range(EXP):
        l8[e * cfg.sct:(e + 1) * cfg.sct, e * cfg.sct:(e + 1) * cfg.sct] = \
            np.triu(np.ones((cfg.sct, cfg.sct)), 1)
    lts8 = l8.astype(bf)

    ncF = build_fused(cfg)
    in_maps = []
    for ci in range(cfg.n_cores):
        in_maps.append({
            "xs": x[ci * cfg.t_core:(ci + 1) * cfg.t_core],
            "wk0t": np.ascontiguousarray(wk[0].T),
            "wk1t": np.ascontiguousarray(wk[1].T),
            "wv0t": np.ascontiguousarray(wv[0].T),
            "wv1t": np.ascontiguousarray(wv[1].T),
            "repsA": repsA,
            "wqt": wq_t, "wot": wo_t, "repsB": repsB,
            "gatew": f("L0_ffn_gate_w"),
            "gbias_rep": gbias_rep, "upw": f("L0_ffn_up_w"),
            "downw": f("L0_ffn_down_w"), "swup": f("L0_ffn_sw_up"),
            "swdown": f("L0_ffn_sw_down"), "upb_fm": upb_fm,
            "sbup_fm": sbup_fm, "downb": f("L0_ffn_down_b"),
            "c_col": c_col, "wmod": wmod, "ident": ident, "ones1": ones1,
            "lts": lts, "onec": onec, "lts8": lts8,
        })
    res = run_fn(ncF, in_maps, core_ids=list(range(cfg.n_cores)))
    y = np.concatenate([np.asarray(r["y"]) for r in res.results], axis=0)
    return y.reshape(1, cfg.n_cores * cfg.t_core, D).astype(np.float32)
